# version 28
# baseline (speedup 1.0000x reference)
import numpy as np
from contextlib import ExitStack

import concourse.bass as bass
import concourse.mybir as mybir
from concourse.bass_utils import run_bass_kernel_spmd

F32 = mybir.dt.float32
BF16 = mybir.dt.bfloat16
I32 = mybir.dt.int32
AF = mybir.ActivationFunctionType
OP = mybir.AluOpType
AX = mybir.AxisListType.X

P = 128
CK = 8
D = 1024
DF = 4096
FCK = 32
EPS = 1e-5
NCORE = 8
HB = 8
HF = 8

ENGS = ("sync", "scalar", "vector", "tensor", "gpsimd")
_ESEM = {"scalar": "s", "vector": "v", "tensor": "t", "gpsimd": "g"}
_SEMS = ("ld", "st", "lw", "l2", "lw2", "l3", "v", "s", "t", "g")


class Buf:
    __slots__ = ("t", "w", "rs")

    def __init__(self, t=None):
        self.t = t
        self.w = None
        self.rs = {}

    def __getitem__(self, sl):
        return self.t[sl]


class PB:
    def __init__(self):
        self.recs = {e: [] for e in ENGS}
        self.cnt = {n: 0 for n in _SEMS}
        self.marks = {e: {} for e in ENGS}
        self.since_drain = {e: 0 for e in ENGS}
        self.lastrec = {e: None for e in ENGS}
        self.bardeps = None
        self.barseen = set()

    def dep(self, name):
        return (name, self.cnt[name])

    def _want(self, eng, d):
        if d is None:
            return
        n, v = d
        if v <= 0:
            return
        if self.marks[eng].get(n, 0) >= v:
            return
        self.marks[eng][n] = v
        self.recs[eng].append(("wait", n, v))

    def op(self, eng, fns, r=(), w=(), deps=(), kind="c"):
        if not isinstance(fns, (list, tuple)):
            fns = [fns]
        dd = list(deps)
        if self.bardeps is not None and eng not in self.barseen:
            self.barseen.add(eng)
            dd.extend(self.bardeps)
        dma = kind != "c"
        sem = kind if dma else _ESEM[eng]
        for b in r:
            dd.append(b.w)
        for b in w:
            if b.rs:
                dd.extend((k, val) for k, val in b.rs.items())
            else:
                dd.append(b.w)
        for d in dd:
            self._want(eng, d)
        last = None
        for fn in fns:
            if dma:
                self.cnt[sem] += 16
                last = ["inst", fn, sem, self.cnt[sem], True, True]
            else:
                last = ["inst", fn, sem, None, False, False]
            self.recs[eng].append(last)
            self.since_drain[eng] += 1
        if not dma:
            self.cnt[sem] += 1
            last[3] = self.cnt[sem]
            last[4] = True
        self.lastrec[eng] = last
        t = (sem, self.cnt[sem])
        for b in r:
            b.rs[sem] = max(b.rs.get(sem, 0), t[1])
        for b in w:
            b.w = t
            b.rs = {}
        return t

    def tagnow(self, eng):
        sem = _ESEM[eng]
        rec = self.lastrec[eng]
        if rec is not None and not rec[4]:
            self.cnt[sem] += 1
            rec[3] = self.cnt[sem]
            rec[4] = True
        return (sem, self.cnt[sem])

    def barrier(self):
        for e in ("scalar", "vector", "tensor", "gpsimd"):
            if self.lastrec[e] is not None:
                self.tagnow(e)
        self.bardeps = [(n, self.cnt[n]) for n in _SEMS if self.cnt[n] > 0]
        self.barseen = set()

    def fence(self, eng, sem):
        v = self.cnt[sem]
        if v <= 0:
            return
        self.recs[eng].append(("wait", sem, v))
        self.marks[eng][sem] = max(self.marks[eng].get(sem, 0), v)

    def run(self, blk, engines, sem_h):
        maxw = {}
        waited = {n: set() for n in ("ld", "st", "lw", "l2", "lw2", "l3")}
        for e in ENGS:
            for rec in self.recs[e]:
                if rec[0] == "wait":
                    maxw[rec[1]] = max(maxw.get(rec[1], 0), rec[2])
                    if rec[1] in waited:
                        waited[rec[1]].add(rec[2])
        for e in ENGS:
            recs = self.recs[e]
            if not any(rec[0] == "inst" and rec[5] for rec in recs):
                continue
            out = []
            fenced = {}
            for rec in recs:
                if rec[0] == "inst" and rec[5]:
                    s_, v_ = rec[2], rec[3]
                    prev = v_ - 16
                    if (prev > fenced.get(s_, 0) and prev in waited[s_]):
                        out.append(("wait", s_, prev))
                        fenced[s_] = prev
                elif rec[0] == "wait" and rec[1] in waited:
                    fenced[rec[1]] = max(fenced.get(rec[1], 0), rec[2])
                out.append(rec)
            self.recs[e] = out
        for ename in ENGS:
            recs = self.recs[ename]
            if not recs:
                continue

            def body(E, recs=recs):
                for rec in recs:
                    if rec[0] == "wait":
                        E.wait_ge(sem_h[rec[1]], rec[2])
                    elif rec[0] == "drain":
                        E.drain()
                    else:
                        _, fn, sem, val, has, dma = rec
                        inst = fn(E)
                        if has and (dma or val <= maxw.get(sem, 0)):
                            inst.then_inc(sem_h[sem], 16 if dma else 1)
            getattr(blk, ename)(body)


def build(nc, n_own):
    RNG = HB + n_own + HF
    WA = (RNG - 16) // 8
    assert 8 * WA + 16 == RNG and WA % 8 == 0 and WA >= 16
    subs = [(s * WA, WA) for s in range(8)] + [(8 * WA, 16)]
    NX = RNG + 1
    OWN0, OWN1 = HB, HB + n_own
    NW = n_own
    STA = WA + 1
    STW = WA
    STS = WA + 1
    STF = 256
    assert NW % STF == 0 or NW < STF

    def tsubs(width, lo, hi):
        out, c = [], lo
        while c < hi:
            w = min(width, hi - c)
            out.append((c, w))
            c += w
        return out

    def ein(name, shape, dt=F32):
        return nc.dram_tensor(name, list(shape), dt, kind="ExternalInput").ap()

    def idr(name, shape, dt):
        return Buf(nc.dram_tensor(name, list(shape), dt, kind="Internal").ap())

    xT = ein("xT", (D, NX))
    prm_in = ein("params_pm", (P, 184))
    meta_in = ein("meta", (P, 8))
    ident_in = ein("ident", (P, P))
    identb_in = ein("identb", (P, P), BF16)
    onescol_in = ein("onescol", (P, 1))
    onesrow_in = ein("onesrow", (1, P))
    ones2d_in = ein("ones2d", (P, P))
    td8_in = ein("td8", (8, D))
    wfull = {w: ein("w_" + w, (DF if w in ("fk", "w1") else D,
                               DF if w in ("fv", "w2") else D))
             for w in ("wk", "wv", "wr", "wo", "fr", "fk", "fv", "w1", "w2")}
    outT = Buf(nc.dram_tensor("outT", [D, n_own], F32, kind="ExternalOutput").ap())

    gsc = {}
    for w in ("wk", "wv", "wr", "wo", "fr", "fk", "fv"):
        r = DF if w == "fk" else D
        c = DF if w == "fv" else D
        for hl in ("h", "l"):
            gsc[w + hl] = idr(f"g_{w}{hl}", (r, c), BF16)
    g_w1q = idr("g_w1q", (DF, D), BF16)
    g_w2q = idr("g_w2q", (D, DF), BF16)
    ci_r1 = idr("ci_r1", (4, D), F32)
    g_r1 = idr("g_r1", (4 * NCORE, D), F32)
    ci_r2 = idr("ci_r2", (2, D), F32)
    g_r2 = idr("g_r2", (2 * NCORE, D), F32)
    sp = {n: idr("sp_" + n, (D, RNG), F32) for n in ("n1", "wk", "z", "y1", "kv", "x1")}
    spb = {n: idr("sp_" + n, (D, RNG), BF16)
           for n in ("xrh", "xrl", "gkh", "gkl", "grh", "grl")}
    sp_kfh = idr("sp_kfh", (DF, RNG), BF16)
    sp_kfl = idr("sp_kfl", (DF, RNG), BF16)
    sp_hs = idr("sp_hs", (DF, NW), F32)

    def dchv(ap_, c0, w, nck=CK):
        return ap_.rearrange("(ck p) t -> p ck t", p=P)[:, :, c0:c0 + w]

    def schv(buf, w, nck=CK):
        return buf.t[:].rearrange("p (ck t) -> p ck t", ck=nck)[:, :, 0:w]

    def c3(buf, st_, n=CK, o=0):
        v = buf.t[:].rearrange("p (ck t) -> p ck t", t=st_)
        return v if (n is None and o == 0) else v[:, o:o + n, :]

    def bc3(row_ap, n, w_):
        return row_ap.unsqueeze(1).broadcast_to([P, n, w_])

    def col3(buf2d, n=CK):
        return buf2d.t[:].unsqueeze(2)

    pb = PB()
    op, dep = pb.op, pb.dep
    RG = list(range(NCORE))

    with ExitStack() as ctx:
        sem_h = {n: ctx.enter_context(nc.semaphore("sem_" + n)) for n in _SEMS}

        def sbt(st, name, shape, dt=F32):
            return Buf(st.enter_context(nc.sbuf_tensor(name, list(shape), dt)))

        def pst(st, name, cols):
            return Buf(st.enter_context(nc.psum_tensor(name, [P, cols], F32)))

        prm = sbt(ctx, "prm", (P, 184))
        met = sbt(ctx, "met", (P, 8))
        idn = sbt(ctx, "idn", (P, P))
        idb = sbt(ctx, "idb", (P, P), BF16)
        onc = sbt(ctx, "onc", (P, 1))
        onr = sbt(ctx, "onr", (1, P))
        on2 = sbt(ctx, "on2", (P, P))
        pm8 = sbt(ctx, "pm8", (P, 8))
        d_pm = sbt(ctx, "d_pm", (P, 8))
        w_pm = sbt(ctx, "w_pm", (P, 8))
        eu_pm = sbt(ctx, "eu_pm", (P, 8))
        gbuf = sbt(ctx, "gbuf", (P, 32))
        iA = sbt(ctx, "iA", (P, 8))
        iB = sbt(ctx, "iB", (P, 8))
        stA = sbt(ctx, "stA", (P, 8))
        stB = sbt(ctx, "stB", (P, 8))
        swc = sbt(ctx, "swc", (P, 4))
        fqc = sbt(ctx, "fqc", (P, 16))

        nc.const_aps.register(F32, EPS, met[:, 6:7])
        nc.const_aps.register(F32, float(n_own - 8), met[:, 7:8])
        op("sync", [(lambda s, d=dst, sr=src: s.dma_start(d.t[:], sr))
                    for dst, src in ((prm, prm_in), (met, meta_in), (idn, ident_in),
                                     (idb, identb_in), (onc, onescol_in),
                                     (onr, onesrow_in), (on2, ones2d_in))],
           w=[prm, met, idn, idb, onc, onr, on2], kind="ld")
        op("vector", [(lambda v, b=b: v.memset(b.t[:], 0.0))
                      for b in (stA, stB, gbuf, iA, iB)],
           w=[stA, stB, gbuf, iA, iB])
        op("scalar", lambda s: s.activation(pm8.t[:], prm[:, 40:48], AF.Exp),
           r=[prm], w=[pm8])
        op("vector", lambda v: v.tensor_scalar_mul(w_pm.t[:], pm8.t[:], -1.0),
           r=[pm8], w=[w_pm])
        op("scalar", lambda s: s.activation(d_pm.t[:], w_pm.t[:], AF.Exp),
           r=[w_pm], w=[d_pm])
        op("scalar", lambda s: s.activation(eu_pm.t[:], prm[:, 48:56], AF.Exp),
           r=[prm], w=[eu_pm])

        def build_dtile(dt_buf):
            op("vector", lambda v: v.memset(dt_buf.t[:], 0.0), w=[dt_buf])
            op("scalar", [(lambda s, ck=ck: s.activation(
                dt_buf[:, ck * WA:(ck + 1) * WA], dt_buf[:, ck * WA:(ck + 1) * WA],
                AF.Identity, bias=d_pm[:, ck:ck + 1], scale=0.0))
                for ck in range(CK)], r=[d_pm], w=[dt_buf])

        def ln_na(xbuf, outbuf, W, ST, sqb, trb, stb, pstat, pe="vector",
                  mm_stats=False):
            x3, s3 = c3(xbuf, ST), c3(sqb, ST)
            o3 = c3(outbuf, ST)
            op("scalar", lambda s: s.activation(
                s3[:, :, 0:W], x3[:, :, 0:W], AF.Square), r=[xbuf], w=[sqb])

            if mm_stats:
                fns = []
                for src, off in ((xbuf, 0), (sqb, 512)):
                    for ck in range(CK):
                        fns.append(lambda t, src=src, off=off, ck=ck: t.matmul(
                            pstat[:, off:off + W], on2.t[:],
                            src[:, ck * ST:ck * ST + W],
                            start=(ck == 0), stop=(ck == CK - 1)))
                op("tensor", fns, r=[on2, xbuf, sqb], w=[pstat])
            else:
                t3 = c3(trb, ST, n=4)

                def tree(srcb, src3, doff):
                    op(pe, lambda v: v.tensor_tensor(
                        t3[:, 0:4, 0:W], src3[:, 0:8:2, 0:W], src3[:, 1:8:2, 0:W],
                        OP.add), r=[srcb], w=[trb])
                    op(pe, lambda v: v.tensor_tensor(
                        t3[:, 0:4:2, 0:W], t3[:, 0:4:2, 0:W], t3[:, 1:4:2, 0:W],
                        OP.add), w=[trb])
                    op(pe, lambda v, doff=doff: v.tensor_tensor(
                        stb[:, doff * ST:doff * ST + W], trb[:, 0:W],
                        trb[:, 2 * ST:2 * ST + W], OP.add), r=[trb], w=[stb])
                tree(xbuf, x3, 0)
                tree(sqb, s3, 1)
                op("tensor", [
                    lambda t: t.matmul(pstat[:, 0:W], on2.t[:], stb[:, 0:W],
                                       start=True, stop=True),
                    lambda t: t.matmul(pstat[:, 512:512 + W], on2.t[:],
                                       stb[:, ST:ST + W], start=True, stop=True)],
                    r=[on2, stb], w=[pstat])
            op("vector", lambda v: v.tensor_scalar_mul(
                stb[:, 2 * ST:2 * ST + W], pstat[:, 0:W], 1.0 / D),
                r=[pstat], w=[stb])
            op("vector", lambda v: v.tensor_tensor(
                stb[:, 3 * ST:3 * ST + W], stb[:, 2 * ST:2 * ST + W],
                stb[:, 2 * ST:2 * ST + W], OP.mult), w=[stb])
            op("vector", lambda v: v.scalar_tensor_tensor(
                stb[:, 4 * ST:4 * ST + W], pstat[:, 512:512 + W], 1.0 / D,
                stb[:, 3 * ST:3 * ST + W], OP.mult, OP.subtract),
                r=[pstat], w=[stb])
            op("scalar", lambda s: s.activation(
                stb[:, 3 * ST:3 * ST + W], stb[:, 4 * ST:4 * ST + W], AF.Sqrt,
                bias=EPS), r=[met], w=[stb])
            op("vector", lambda v: v.reciprocal(
                stb[:, 4 * ST:4 * ST + W], stb[:, 3 * ST:3 * ST + W]), w=[stb])
            op(pe, lambda v: v.tensor_tensor(
                o3[:, :, 0:W], x3[:, :, 0:W],
                bc3(stb[:, 2 * ST:2 * ST + W], CK, W), OP.subtract),
                r=[xbuf, stb], w=[outbuf])
            op("vector", lambda v: v.tensor_tensor(
                o3[:, :, 0:W], o3[:, :, 0:W],
                bc3(stb[:, 4 * ST:4 * ST + W], CK, W), OP.mult),
                r=[stb], w=[outbuf])

        def proj3(psms, whb, wlb, rhb, rlb, W, RST, evac):
            for dch in range(CK):
                ps = psms[dch % 2]
                fns = []
                for kch in range(CK):
                    lh = whb[:, kch * D + dch * P:kch * D + (dch + 1) * P]
                    ll = wlb[:, kch * D + dch * P:kch * D + (dch + 1) * P]
                    rh = rhb[:, kch * RST:kch * RST + W]
                    rl = rlb[:, kch * RST:kch * RST + W]
                    st_ = (kch == 0)
                    sp_ = (kch == CK - 1)
                    fns.append(lambda t, lh=lh, rh=rh, st_=st_, ps=ps, W=W:
                               t.matmul(ps[:, 0:W], lh, rh, start=st_, stop=False))
                    fns.append(lambda t, lh=lh, rl=rl, ps=ps, W=W:
                               t.matmul(ps[:, 0:W], lh, rl, start=False, stop=False))
                    fns.append(lambda t, ll=ll, rh=rh, sp_=sp_, ps=ps, W=W:
                               t.matmul(ps[:, 0:W], ll, rh, start=False, stop=sp_))
                op("tensor", fns, r=[whb, wlb, rhb, rlb], w=[ps])
                evac(dch, ps)

        def split_hi_lo(srcb, dh, dl, W, ST):
            op("scalar", lambda s: s.copy(
                c3(dh, ST)[:, :, 0:W], c3(srcb, ST)[:, :, 0:W]),
                r=[srcb], w=[dh])
            op("vector", lambda v: v.tensor_tensor(
                c3(dl, ST)[:, :, 0:W], c3(srcb, ST)[:, :, 0:W],
                c3(dh, ST)[:, :, 0:W], OP.subtract), r=[srcb, dh], w=[dl])

        def tloads(items, srcs, bufs, split=True):
            fns = [(lambda s, b=b, co=co, w=w, src=src, r0=r0, r1=r1, ca=ca, cb=cb:
                    s.dma_start(b[:, co:co + w], src.t[r0:r1, ca:cb],
                                transpose=True))
                   for (b, co, w, src, r0, r1, ca, cb) in items]
            op("sync", fns, r=srcs, w=bufs, kind="lw")

        def _ph_p0():
         with ExitStack() as p0:
            wt = [sbt(p0, "p0_w%d" % i, (P, DF)) for i in range(8)]
            wh = [sbt(p0, "p0_h%d" % i, (P, DF), BF16) for i in range(4)]
            wl = [sbt(p0, "p0_l%d" % i, (P, DF), BF16) for i in range(4)]
            acc = sbt(p0, "p0_a", (P, 34))
            prow = sbt(p0, "p0_row", (1, 16))
            ps0 = pst(p0, "ps0", 512)

            lq = [0]

            def ldgroup(w, c, it0, n):
                wts = [wt[(it0 + j) % 8] for j in range(n)]
                eng, kq = (("sync", "ld"), ("scalar", "l2"))[lq[0] % 2]
                lq[0] += 1
                op(eng, [(lambda s, it=it0 + j, c=c, w=w, b_=wts[j]:
                          s.dma_start(b_[:, 0:c],
                                      wfull[w][it * P:(it + 1) * P, :]))
                         for j in range(n)], w=wts, kind=kq)
                return wts

            for w in ("wk", "wv", "wr", "wo", "fr", "fk", "fv"):
                rr = DF if w == "fk" else D
                c = DF if w == "fv" else D
                for it0 in range(0, rr // P, 4):
                    wts = ldgroup(w, c, it0, 4)
                    for j in range(4):
                        it = it0 + j
                        wtx = wts[j]
                        whb, wlb = wh[it % 4], wl[it % 4]
                        op("scalar", lambda s, c=c, wtx=wtx, whb=whb: s.copy(
                            whb[:, 0:c], wtx[:, 0:c]), r=[wtx], w=[whb])
                        op("vector", lambda v, c=c, wtx=wtx, whb=whb, wlb=wlb:
                           v.tensor_tensor(wlb[:, 0:c], wtx[:, 0:c],
                                           whb[:, 0:c], OP.subtract),
                           r=[wtx, whb], w=[wlb])
                        op("gpsimd",
                           [(lambda g, w=w, it=it, c=c, whb=whb: g.dma_start(
                               gsc[w + "h"].t[it * P:(it + 1) * P, :],
                               whb[:, 0:c])),
                            (lambda g, w=w, it=it, c=c, wlb=wlb: g.dma_start(
                                gsc[w + "l"].t[it * P:(it + 1) * P, :],
                                wlb[:, 0:c]))],
                           r=[whb, wlb], w=[gsc[w + "h"], gsc[w + "l"]],
                           kind="st")

            for it0 in range(0, DF // P, 4):
                wts = ldgroup("w1", D, it0, 4)
                op("vector", [(lambda v, it=it0 + j, wtx=wts[j]: v.reduce_sum(
                    acc[:, it:it + 1], wtx[:, 0:D], axis=AX,
                    apply_absolute_value=True)) for j in range(4)],
                   r=wts, w=[acc])
            op("vector", lambda v: v.reduce_sum(acc[:, 32:33], acc[:, 0:32],
                                                axis=AX), r=[acc], w=[acc])
            for it0 in range(0, D // P, 4):
                wts = ldgroup("w2", DF, it0, 4)
                op("vector", [(lambda v, it=it0 + j, wtx=wts[j]: v.reduce_sum(
                    acc[:, it:it + 1], wtx[:, 0:DF], axis=AX,
                    apply_absolute_value=True)) for j in range(4)],
                   r=wts, w=[acc])
            op("vector", lambda v: v.reduce_sum(acc[:, 33:34], acc[:, 0:8],
                                                axis=AX), r=[acc], w=[acc])
            op("tensor", [
                lambda t: t.matmul(ps0[0:1, 0:1], acc[:, 32:33], onc.t[:],
                                   start=True, stop=True),
                lambda t: t.matmul(ps0[0:1, 1:2], acc[:, 33:34], onc.t[:],
                                   start=True, stop=True)],
                r=[acc, onc], w=[ps0])
            op("vector", lambda v: v.tensor_copy(prow[0:1, 8:10], ps0[0:1, 0:2]),
               r=[ps0], w=[prow])
            op("vector", lambda v: v.tensor_scalar(
                prow[0:1, 10:12], prow[0:1, 8:10], 1.0 / (DF * D), 1e-5,
                OP.mult, OP.max), w=[prow])
            op("vector", lambda v: v.reciprocal(prow[0:1, 12:14], prow[0:1, 10:12]),
               w=[prow])
            op("tensor", [(lambda t, i=i, off=off: t.matmul(
                ps0[:, 8 + i:9 + i], onr.t[:, :], prow[0:1, off:off + 1],
                start=True, stop=True)) for i, off in enumerate((12, 13, 10, 11))],
                r=[onr, prow], w=[ps0])
            op("vector", lambda v: v.tensor_copy(swc.t[:], ps0[:, 8:12]),
               r=[ps0], w=[swc])

            def quant_full(w, gout, col):
                rr = DF if w == "w1" else D
                c = D if w == "w1" else DF
                for it0 in range(0, rr // P, 4):
                    wts = ldgroup(w, c, it0, 4)
                    for j in range(4):
                        it = it0 + j
                        wtx = wts[j]
                        whb, wlb = wh[it % 4], wl[it % 4]
                        op("vector", lambda v, c=c, wtx=wtx, wlb=wlb:
                           v.tensor_scalar(wlb[:, 0:c], wtx[:, 0:c],
                                           swc[:, col:col + 1], 0.5,
                                           OP.mult, OP.is_gt),
                           r=[wtx, swc], w=[wlb])
                        op("vector", lambda v, c=c, wtx=wtx, whb=whb:
                           v.tensor_scalar(whb[:, 0:c], wtx[:, 0:c],
                                           swc[:, col:col + 1], -0.5,
                                           OP.mult, OP.is_lt),
                           r=[wtx, swc], w=[whb])
                        op("vector", lambda v, c=c, whb=whb, wlb=wlb:
                           v.tensor_tensor(wlb[:, 0:c], wlb[:, 0:c],
                                           whb[:, 0:c], OP.subtract),
                           r=[whb], w=[wlb])
                        op("gpsimd", lambda g, it=it, c=c, gout=gout, wlb=wlb:
                           g.dma_start(gout.t[it * P:(it + 1) * P, :],
                                       wlb[:, 0:c]),
                           r=[wlb], w=[gout], kind="st")
            quant_full("w1", g_w1q, 0)
            quant_full("w2", g_w2q, 1)
        _ph_p0()
        pb.barrier()

        def _ph_a():
         with ExitStack() as pa:
            wkh = sbt(pa, "wkh", (P, CK * D), BF16)
            wkl = sbt(pa, "wkl", (P, CK * D), BF16)
            wvh = sbt(pa, "wvh", (P, CK * D), BF16)
            wvl = sbt(pa, "wvl", (P, CK * D), BF16)
            tloads([(buf, cc * D, D, g_, 0, D, cc * P, (cc + 1) * P)
                    for buf, g_ in ((wkh, gsc["wkh"]), (wkl, gsc["wkl"]),
                                    (wvh, gsc["wvh"]), (wvl, gsc["wvl"]))
                    for cc in range(CK)],
                   [gsc["wkh"], gsc["wkl"], gsc["wvh"], gsc["wvl"]],
                   [wkh, wkl, wvh, wvl])
            pstat = pst(pa, "A_pst", 1024)
            dtile = sbt(pa, "A_dt", (P, CK * WA))
            build_dtile(dtile)
            xa = sbt(pa, "A_x", (P, CK * STA))
            sq = sbt(pa, "A_sq", (P, CK * STA))
            n1 = sbt(pa, "A_n1", (P, CK * STA))
            hh = sbt(pa, "A_h", (P, CK * STA))
            stb = sbt(pa, "A_stb", (P, 5 * STA))
            wkt = sbt(pa, "A_wk", (P, CK * STW))
            zt = sbt(pa, "A_z", (P, CK * STW))
            scA = sbt(pa, "A_scA", (P, CK * STW))
            scB = sbt(pa, "A_scB", (P, CK * STW))
            mxf = sbt(pa, "A_mx", (P, CK * STW))
            bhk = [sbt(pa, "A_bhk0", (P, CK * STW), BF16),
                   sbt(pa, "A_bhk1", (P, CK * STW), BF16)]
            blk_ = [sbt(pa, "A_blk0", (P, CK * STW), BF16),
                    sbt(pa, "A_blk1", (P, CK * STW), BF16)]
            bhv = [sbt(pa, "A_bhv0", (P, CK * STW), BF16),
                   sbt(pa, "A_bhv1", (P, CK * STW), BF16)]
            blv = [sbt(pa, "A_blv0", (P, CK * STW), BF16),
                   sbt(pa, "A_blv1", (P, CK * STW), BF16)]
            bhr = sbt(pa, "A_bhr", (P, CK * STW), BF16)
            blr = sbt(pa, "A_blr", (P, CK * STW), BF16)
            pmA = [pst(pa, "pmA0", 512), pst(pa, "pmA1", 512)]

            def stage1a(si):
                c0, W = subs[si]
                W1 = W + 1
                op("sync", lambda s, c0=c0, W1=W1: s.dma_start(
                    schv(xa, W1), dchv(xT, c0, W1)), w=[xa], kind="ld")
                op("scalar", lambda s, W1=W1: s.activation(
                    c3(sq, STA)[:, :, 0:W1], c3(xa, STA)[:, :, 0:W1],
                    AF.Square), r=[xa], w=[sq])

            def stage1b(si):
                c0, W = subs[si]
                W1 = W + 1
                p = si % 2
                fns = []
                for srcb, off in ((xa, 0), (sq, 512)):
                    for ck in range(CK):
                        fns.append(lambda t, srcb=srcb, off=off, ck=ck, W1=W1:
                                   t.matmul(pstat[:, off:off + W1], on2.t[:],
                                            srcb[:, ck * STA:ck * STA + W1],
                                            start=(ck == 0), stop=(ck == CK - 1)))
                op("tensor", fns, r=[on2, xa, sq], w=[pstat])
                op("vector", lambda v, W1=W1: v.tensor_scalar_mul(
                    stb[:, 2 * STA:2 * STA + W1], pstat[:, 0:W1], 1.0 / D),
                    r=[pstat], w=[stb])
                op("vector", lambda v, W1=W1: v.tensor_tensor(
                    stb[:, 3 * STA:3 * STA + W1], stb[:, 2 * STA:2 * STA + W1],
                    stb[:, 2 * STA:2 * STA + W1], OP.mult), w=[stb])
                op("vector", lambda v, W1=W1: v.scalar_tensor_tensor(
                    stb[:, 4 * STA:4 * STA + W1], pstat[:, 512:512 + W1],
                    1.0 / D, stb[:, 3 * STA:3 * STA + W1], OP.mult,
                    OP.subtract), r=[pstat], w=[stb])
                op("scalar", lambda s, W1=W1: s.activation(
                    stb[:, 3 * STA:3 * STA + W1], stb[:, 4 * STA:4 * STA + W1],
                    AF.Sqrt, bias=EPS), r=[met], w=[stb])
                op("vector", lambda v, W1=W1: v.reciprocal(
                    stb[:, 4 * STA:4 * STA + W1], stb[:, 3 * STA:3 * STA + W1]),
                    w=[stb])
                op("vector", lambda v, W1=W1: v.tensor_tensor(
                    c3(n1, STA)[:, :, 0:W1], c3(xa, STA)[:, :, 0:W1],
                    bc3(stb[:, 2 * STA:2 * STA + W1], CK, W1), OP.subtract),
                    r=[xa, stb], w=[n1])
                op("vector", lambda v, W1=W1: v.tensor_tensor(
                    c3(n1, STA)[:, :, 0:W1], c3(n1, STA)[:, :, 0:W1],
                    bc3(stb[:, 4 * STA:4 * STA + W1], CK, W1), OP.mult),
                    r=[stb], w=[n1])
                op("gpsimd", lambda s, c0=c0, W=W: s.dma_start(
                    dchv(sp["n1"].t, c0, W),
                    n1.t[:].rearrange("p (ck t) -> p ck t", ck=CK)[:, :, 1:W + 1]),
                    r=[n1], w=[sp["n1"]], kind="st")
                op("vector", lambda v, W1=W1: v.tensor_tensor(
                    stb[:, 3 * STA:3 * STA + W1], stb[:, 4 * STA:4 * STA + W1],
                    stb[:, 4 * STA:4 * STA + W1], OP.mult), w=[stb])
                op("vector", lambda v, W1=W1: v.tensor_scalar(
                    stb[:, 3 * STA:3 * STA + W1], stb[:, 3 * STA:3 * STA + W1],
                    -EPS, 1.0 + EPS, OP.mult, OP.add), w=[stb])
                op("scalar", lambda s, W1=W1: s.activation(
                    stb[:, 3 * STA:3 * STA + W1], stb[:, 3 * STA:3 * STA + W1],
                    AF.Sqrt), w=[stb])
                op("vector", lambda v, W1=W1: v.reciprocal(
                    stb[:, 4 * STA:4 * STA + W1], stb[:, 3 * STA:3 * STA + W1]),
                    w=[stb])
                op("vector", lambda v, W1=W1: v.tensor_tensor(
                    c3(hh, STA)[:, :, 0:W1], c3(n1, STA)[:, :, 0:W1],
                    bc3(stb[:, 4 * STA:4 * STA + W1], CK, W1), OP.mult),
                    r=[n1, stb], w=[hh])
                if si == 0:
                    op("vector", lambda v: v.tensor_scalar_mul(
                        c3(hh, STA)[:, :, 1:1 + HB], c3(hh, STA)[:, :, 1:1 + HB],
                        met[:, 0:1]), r=[met], w=[hh])
                op("vector", lambda v, W1=W1: v.tensor_tensor(
                    c3(sq, STA)[:, :, 1:W1], c3(hh, STA)[:, :, 1:W1],
                    c3(hh, STA)[:, :, 0:W1 - 1], OP.subtract), r=[hh], w=[sq])

                def mix(tmoff, oh, ol):
                    tmb = prm[:, tmoff:tmoff + 8].unsqueeze(2).broadcast_to(
                        [P, CK, W])
                    op("vector", lambda v, tmb=tmb, W=W: v.tensor_tensor(
                        c3(mxf, STW)[:, :, 0:W], c3(sq, STA)[:, :, 1:1 + W],
                        tmb, OP.mult), r=[sq, prm], w=[mxf])
                    op("vector", lambda v, W=W: v.tensor_tensor(
                        c3(mxf, STW)[:, :, 0:W], c3(mxf, STW)[:, :, 0:W],
                        c3(hh, STA)[:, :, 0:W], OP.add), r=[hh], w=[mxf])
                    split_hi_lo(mxf, oh, ol, W, STW)

                mix(0, bhk[p], blk_[p])
                mix(8, bhv[p], blv[p])
                mix(16, bhr, blr)
                op("gpsimd", [
                    lambda s, c0=c0, W=W: s.dma_start(
                        dchv(spb["xrh"].t, c0, W), schv(bhr, W)),
                    lambda s, c0=c0, W=W: s.dma_start(
                        dchv(spb["xrl"].t, c0, W), schv(blr, W))],
                    r=[bhr, blr], w=[spb["xrh"], spb["xrl"]], kind="st")

            def stage2a(si):
                c0, W = subs[si]
                p = si % 2
                proj3(pmA, wkh, wkl, bhk[p], blk_[p], W, STW,
                      lambda dch, ps: op("scalar", lambda s, dch=dch, ps=ps, W=W:
                                         s.activation(
                                             wkt[:, dch * STW:dch * STW + W],
                                             ps[:, 0:W], AF.Exp),
                                         r=[ps], w=[wkt]))

            def stage2b(si):
                c0, W = subs[si]
                p = si % 2
                proj3(pmA, wvh, wvl, bhv[p], blv[p], W, STW,
                      lambda dch, ps: op("vector", lambda v, dch=dch, ps=ps, W=W:
                                         v.tensor_tensor(
                                             zt[:, dch * STW:dch * STW + W],
                                             ps[:, 0:W],
                                             wkt[:, dch * STW:dch * STW + W],
                                             OP.mult),
                                         r=[ps, wkt], w=[zt]))
                op("gpsimd", [
                    lambda s, c0=c0, W=W: s.dma_start(
                        dchv(sp["wk"].t, c0, W), schv(wkt, W)),
                    lambda s, c0=c0, W=W: s.dma_start(
                        dchv(sp["z"].t, c0, W), schv(zt, W))],
                    r=[wkt, zt], w=[sp["wk"], sp["z"]], kind="st")
                lo, hi = max(c0, OWN0), min(c0 + W, OWN1)
                if lo < hi:
                    o0, w2 = lo - c0, hi - lo
                    op("vector",
                       [(lambda v, ck=ck, o0=o0, w2=w2: v.tensor_tensor_scan(
                           scA[:, ck * STW:ck * STW + w2],
                           dtile[:, ck * WA:ck * WA + w2],
                           zt[:, ck * STW + o0:ck * STW + o0 + w2],
                           stA[:, ck:ck + 1], OP.mult, OP.add))
                        for ck in range(CK)] +
                       [(lambda v, ck=ck, o0=o0, w2=w2: v.tensor_tensor_scan(
                           scB[:, ck * STW:ck * STW + w2],
                           dtile[:, ck * WA:ck * WA + w2],
                           wkt[:, ck * STW + o0:ck * STW + o0 + w2],
                           stB[:, ck:ck + 1], OP.mult, OP.add))
                        for ck in range(CK)],
                       r=[dtile, zt, wkt, stA, stB], w=[scA, scB])
                    op("vector",
                       [(lambda v, w2=w2: v.tensor_copy(
                           col3(stA), c3(scA, STW)[:, :, w2 - 1:w2])),
                        (lambda v, w2=w2: v.tensor_copy(
                            col3(stB), c3(scB, STW)[:, :, w2 - 1:w2]))],
                       r=[scA, scB], w=[stA, stB])
                    caps = []
                    for owncol, gcol in ((OWN1 - 1, 0), (OWN1 - 9, 16)):
                        if lo <= owncol < hi:
                            oc = owncol - lo
                            caps.append(lambda v, oc=oc, gcol=gcol: v.tensor_copy(
                                gbuf.t[:, gcol:gcol + 8].unsqueeze(2),
                                c3(scA, STW)[:, :, oc:oc + 1]))
                            caps.append(lambda v, oc=oc, gcol=gcol: v.tensor_copy(
                                gbuf.t[:, gcol + 8:gcol + 16].unsqueeze(2),
                                c3(scB, STW)[:, :, oc:oc + 1]))
                    if caps:
                        op("vector", caps, r=[scA, scB], w=[gbuf])

            NS = len(subs)
            stage1a(0)
            stage1b(0)
            for si in range(NS):
                stage2a(si)
                if si + 1 < NS:
                    stage1a(si + 1)
                stage2b(si)
                if si + 1 < NS:
                    stage1b(si + 1)
        _ph_a()
        pb.barrier()

        def _ph_exch():
         with ExitStack() as pe:
            psx = pst(pe, "psx", 512)
            rows = sbt(pe, "rows", (1, 6 * D))
            t8 = sbt(pe, "t8", (8, D))
            t8b = sbt(pe, "t8b", (8, D))
            e8 = sbt(pe, "e8", (8, D))
            gtA = sbt(pe, "gtA", (8, D))
            gtB = sbt(pe, "gtB", (8, D))
            op("sync", lambda s: s.dma_start(t8.t[:], td8_in), w=[t8], kind="ld")
            op("scalar", lambda s: s.activation(t8b.t[:], t8.t[:], AF.Exp),
               r=[t8], w=[t8b])
            op("vector", lambda v: v.tensor_scalar_mul(t8.t[:], t8b.t[:], -1.0),
               r=[t8b], w=[t8])
            for gcol, roff in ((0, 0), (8, 1), (16, 2), (24, 3)):
                for ck in range(CK):
                    op("tensor", lambda t, gcol=gcol, ck=ck: t.matmul(
                        psx[0:1, 0:P], gbuf[:, gcol + ck:gcol + ck + 1], idn.t[:],
                        start=True, stop=True), r=[gbuf, idn], w=[psx])
                    op("vector", lambda v, roff=roff, ck=ck: v.tensor_copy(
                        rows[0:1, roff * D + ck * P:roff * D + (ck + 1) * P],
                        psx[0:1, 0:P]), r=[psx], w=[rows])
            op("gpsimd", lambda s: s.dma_start(ci_r1.t, rows[0:1, 0:4 * D]),
               r=[rows], w=[ci_r1], kind="st")
            op("gpsimd", lambda g: g.collective_compute(
                "AllGather", OP.bypass, [RG], [ci_r1.t], [g_r1.t]),
                r=[ci_r1], w=[g_r1])
            op("sync", [lambda s: s.dma_start(gtA.t[:], g_r1.t[0::4, :]),
                        lambda s: s.dma_start(gtB.t[:], g_r1.t[1::4, :])],
               r=[g_r1], w=[gtA, gtB], kind="ld")
            op("scalar", lambda s: s.activation(e8.t[:], t8.t[:], AF.Exp,
                                                scale=met[0:8, 3:4]),
               r=[t8, met], w=[e8])
            op("vector", lambda v: v.tensor_scalar_mul(e8.t[:], e8.t[:],
                                                       met[0:8, 4:5]),
               r=[met], w=[e8])
            for tile, off in ((gtA, 4 * D), (gtB, 5 * D)):
                op("vector", lambda v, tile=tile: v.tensor_tensor(
                    tile.t[:], tile.t[:], e8.t[:], OP.mult), r=[e8], w=[tile])
                for half in range(2):
                    op("tensor", lambda t, tile=tile, half=half: t.matmul(
                        psx[0:1, 0:512], onc[0:8, :],
                        tile[:, half * 512:(half + 1) * 512],
                        start=True, stop=True), r=[onc, tile], w=[psx])
                    op("vector", lambda v, off=off, half=half: v.tensor_copy(
                        rows[0:1, off + half * 512:off + (half + 1) * 512],
                        psx[0:1, 0:512]), r=[psx], w=[rows])
            op("scalar", lambda s: s.activation(rows[0:1, 0:D], t8[0:1, :], AF.Exp,
                                                scale=float(n_own - 8)),
               r=[t8, met], w=[rows])
            op("vector", [
                lambda v: v.tensor_tensor(rows[0:1, 4 * D:5 * D],
                                          rows[0:1, 4 * D:5 * D],
                                          rows[0:1, 0:D], OP.mult),
                lambda v: v.tensor_tensor(rows[0:1, 5 * D:6 * D],
                                          rows[0:1, 5 * D:6 * D],
                                          rows[0:1, 0:D], OP.mult)],
                w=[rows])
            op("vector", [
                lambda v: v.tensor_tensor(rows[0:1, 4 * D:5 * D],
                                          rows[0:1, 4 * D:5 * D],
                                          rows[0:1, 2 * D:3 * D], OP.add),
                lambda v: v.tensor_tensor(rows[0:1, 5 * D:6 * D],
                                          rows[0:1, 5 * D:6 * D],
                                          rows[0:1, 3 * D:4 * D], OP.add)],
                w=[rows])
            op("gpsimd", lambda s: s.dma_start(ci_r2.t, rows[0:1, 4 * D:6 * D]),
               r=[rows], w=[ci_r2], kind="st")
            op("gpsimd", lambda g: g.collective_compute(
                "AllGather", OP.bypass, [RG], [ci_r2.t], [g_r2.t]),
                r=[ci_r2], w=[g_r2])
            op("sync", [lambda s: s.dma_start(gtA.t[:], g_r2.t[0::2, :]),
                        lambda s: s.dma_start(gtB.t[:], g_r2.t[1::2, :])],
               r=[g_r2], w=[gtA, gtB], kind="ld")
            for tile, off in ((gtA, 4 * D), (gtB, 5 * D)):
                op("vector", lambda v, tile=tile: v.tensor_scalar_mul(
                    tile.t[:], tile.t[:], met[0:8, 5:6]), r=[met], w=[tile])
                for half in range(2):
                    op("tensor", lambda t, tile=tile, half=half: t.matmul(
                        psx[0:1, 0:512], onc[0:8, :],
                        tile[:, half * 512:(half + 1) * 512],
                        start=True, stop=True), r=[onc, tile], w=[psx])
                    op("vector", lambda v, off=off, half=half: v.tensor_copy(
                        rows[0:1, off + half * 512:off + (half + 1) * 512],
                        psx[0:1, 0:512]), r=[psx], w=[rows])
            for off, colbuf in ((4 * D, iA), (5 * D, iB)):
                for ck in range(CK):
                    op("tensor", lambda t, off=off, ck=ck: t.matmul(
                        psx[:, 0:1], rows[0:1, off + ck * P:off + (ck + 1) * P],
                        onr[0:1, 0:1], start=True, stop=True),
                        r=[rows, onr], w=[psx])
                    op("vector", lambda v, colbuf=colbuf, ck=ck: v.tensor_copy(
                        colbuf.t[:, ck:ck + 1], psx[:, 0:1]), r=[psx], w=[colbuf])
        _ph_exch()
        pb.barrier()

        def _ph_s1():
         with ExitStack() as s1:
            woh = sbt(s1, "woh", (P, CK * D), BF16)
            wol = sbt(s1, "wol", (P, CK * D), BF16)
            wrh = sbt(s1, "wrh", (P, CK * D), BF16)
            wrl = sbt(s1, "wrl", (P, CK * D), BF16)
            tloads([(buf, cc * D, D, g_, 0, D, cc * P, (cc + 1) * P)
                    for buf, g_ in ((wrh, gsc["wrh"]), (wrl, gsc["wrl"]),
                                    (woh, gsc["woh"]), (wol, gsc["wol"]))
                    for cc in range(CK)],
                   [gsc["wrh"], gsc["wrl"], gsc["woh"], gsc["wol"]],
                   [wrh, wrl, woh, wol])
            pstat = pst(s1, "S_pst", 1024)
            dtile = sbt(s1, "S_dt", (P, CK * WA))
            build_dtile(dtile)
            wkt = sbt(s1, "B_wk", (P, CK * STW))
            zt = sbt(s1, "B_z", (P, CK * STW))
            n1t = sbt(s1, "B_n1", (P, CK * STW))
            xrh = sbt(s1, "B_xrh", (P, CK * STW), BF16)
            xrl = sbt(s1, "B_xrl", (P, CK * STW), BF16)
            scA = sbt(s1, "B_scA", (P, CK * STS))
            scB = sbt(s1, "B_scB", (P, CK * STS))
            numt = sbt(s1, "B_num", (P, CK * STW))
            dent = sbt(s1, "B_den", (P, CK * STW))
            srt = sbt(s1, "B_sr", (P, CK * STW))
            y1t = sbt(s1, "B_y1", (P, CK * STW))
            gt = sbt(s1, "B_g", (P, CK * STW))
            sq = sbt(s1, "B_sq", (P, CK * STW))
            stb = sbt(s1, "B_stb", (P, 5 * STW))
            gprev = sbt(s1, "B_gp", (P, CK))
            bh = sbt(s1, "B_bh", (P, CK * STW), BF16)
            bl = sbt(s1, "B_bl", (P, CK * STW), BF16)
            pmB = [pst(s1, "pmB0", 512), pst(s1, "pmB1", 512)]
            pmC = [pst(s1, "pmC0", 512), pst(s1, "pmC1", 512)]
            op("vector", lambda v: v.memset(gprev.t[:], 0.0), w=[gprev])

            def loads_a(si):
                c0, W = subs[si]
                op("sync", [(lambda s, b=b, spn=spn, c0=c0, W=W: s.dma_start(
                    schv(b, W), dchv(spn.t, c0, W)))
                    for b, spn in ((xrh, spb["xrh"]), (xrl, spb["xrl"]))],
                    r=[spb["xrh"], spb["xrl"]], w=[xrh, xrl], kind="ld")

            def loads_b(si):
                c0, W = subs[si]
                op("sync", [(lambda s, b=b, spn=spn, c0=c0, W=W: s.dma_start(
                    schv(b, W), dchv(spn.t, c0, W)))
                    for b, spn in ((wkt, sp["wk"]), (zt, sp["z"]),
                                   (n1t, sp["n1"]))],
                    r=[sp["wk"], sp["z"], sp["n1"]], w=[wkt, zt, n1t], kind="ld")

            def scanchain(si):
                c0, W = subs[si]
                if si == 0:
                    op("vector", [
                        lambda v: v.tensor_scalar_mul(
                            c3(zt, STW)[:, :, 0:HB], c3(zt, STW)[:, :, 0:HB],
                            met[:, 0:1]),
                        lambda v: v.tensor_scalar_mul(
                            c3(wkt, STW)[:, :, 0:HB], c3(wkt, STW)[:, :, 0:HB],
                            met[:, 0:1])], r=[met], w=[zt, wkt])
                    op("vector", [
                        lambda v: v.tensor_copy(c3(scA, STS)[:, :, 0:1], col3(iA)),
                        lambda v: v.tensor_copy(c3(scB, STS)[:, :, 0:1], col3(iB))],
                       r=[iA, iB], w=[scA, scB])
                op("vector",
                   [(lambda v, ck=ck, W=W: v.tensor_tensor_scan(
                       scA[:, ck * STS + 1:ck * STS + 1 + W],
                       dtile[:, ck * WA:ck * WA + W],
                       zt[:, ck * STW:ck * STW + W],
                       scA[:, ck * STS:ck * STS + 1], OP.mult, OP.add))
                    for ck in range(CK)] +
                   [(lambda v, ck=ck, W=W: v.tensor_tensor_scan(
                       scB[:, ck * STS + 1:ck * STS + 1 + W],
                       dtile[:, ck * WA:ck * WA + W],
                       wkt[:, ck * STW:ck * STW + W],
                       scB[:, ck * STS:ck * STS + 1], OP.mult, OP.add))
                    for ck in range(CK)],
                   r=[dtile, zt, wkt], w=[scA, scB])
                eub = eu_pm.t[:].unsqueeze(2).broadcast_to([P, CK, W])
                op("vector", lambda v, W=W, eub=eub: v.tensor_tensor(
                    c3(numt, STW)[:, :, 0:W], c3(zt, STW)[:, :, 0:W], eub,
                    OP.mult), r=[zt, eu_pm], w=[numt])
                op("vector", lambda v, W=W: v.tensor_tensor(
                    c3(numt, STW)[:, :, 0:W], c3(numt, STW)[:, :, 0:W],
                    c3(scA, STS)[:, :, 0:W], OP.add), r=[scA], w=[numt])
                op("vector", lambda v, W=W, eub=eub: v.tensor_tensor(
                    c3(dent, STW)[:, :, 0:W], c3(wkt, STW)[:, :, 0:W], eub,
                    OP.mult), r=[wkt, eu_pm], w=[dent])
                op("vector", lambda v, W=W: v.tensor_tensor(
                    c3(dent, STW)[:, :, 0:W], c3(dent, STW)[:, :, 0:W],
                    c3(scB, STS)[:, :, 0:W], OP.add), r=[scB], w=[dent])
                if si == 0:
                    op("vector", lambda v: v.tensor_scalar_add(
                        c3(dent, STW)[:, :, 0:HB], c3(dent, STW)[:, :, 0:HB],
                        met[:, 2:3]), r=[met], w=[dent])
                op("vector", lambda v, W=W: v.reciprocal(
                    c3(sq, STW)[:, :, 0:W], c3(dent, STW)[:, :, 0:W]),
                    r=[dent], w=[sq])
                op("vector", lambda v, W=W: v.tensor_tensor(
                    c3(numt, STW)[:, :, 0:W], c3(numt, STW)[:, :, 0:W],
                    c3(sq, STW)[:, :, 0:W], OP.mult), r=[sq], w=[numt])
                op("vector", [
                    (lambda v, W=W: v.tensor_copy(
                        c3(scA, STS)[:, :, 0:1], c3(scA, STS)[:, :, W:W + 1])),
                    (lambda v, W=W: v.tensor_copy(
                        c3(scB, STS)[:, :, 0:1], c3(scB, STS)[:, :, W:W + 1]))],
                   w=[scA, scB])

            def wrproj(si):
                c0, W = subs[si]
                proj3(pmB, wrh, wrl, xrh, xrl, W, STW,
                      lambda dch, ps: op("scalar", lambda s, dch=dch, ps=ps, W=W:
                                         s.activation(
                                             srt[:, dch * STW:dch * STW + W],
                                             ps[:, 0:W], AF.Sigmoid),
                                         r=[ps], w=[srt]))

            def rwkv_split(si):
                c0, W = subs[si]
                op("vector", lambda v, W=W: v.tensor_tensor(
                    c3(srt, STW)[:, :, 0:W], c3(srt, STW)[:, :, 0:W],
                    c3(numt, STW)[:, :, 0:W], OP.mult), r=[numt], w=[srt])
                split_hi_lo(srt, bh, bl, W, STW)

            def woproj(si):
                c0, W = subs[si]
                proj3(pmC, woh, wol, bh, bl, W, STW,
                      lambda dch, ps: op("vector", lambda v, dch=dch, ps=ps, W=W:
                                         v.tensor_tensor(
                                             y1t[:, dch * STW:dch * STW + W],
                                             ps[:, 0:W],
                                             n1t[:, dch * STW:dch * STW + W],
                                             OP.add),
                                         r=[ps, n1t], w=[y1t]))
                op("gpsimd", lambda s, c0=c0, W=W: s.dma_start(
                    dchv(sp["y1"].t, c0, W), schv(y1t, W)),
                    r=[y1t], w=[sp["y1"]], kind="st")

            def rest(si):
                c0, W = subs[si]
                ln_na(y1t, gt, W, STW, sq, None, stb, pstat, mm_stats=True)
                if si == 0:
                    op("vector", lambda v: v.tensor_scalar_mul(
                        c3(gt, STW)[:, :, HB - 1:HB], c3(gt, STW)[:, :, HB - 1:HB],
                        met[:, 0:1]), r=[met], w=[gt])
                op("vector", [
                    (lambda v, W=W: v.tensor_tensor(
                        c3(sq, STW)[:, :, 1:W], c3(gt, STW)[:, :, 1:W],
                        c3(gt, STW)[:, :, 0:W - 1], OP.subtract)),
                    (lambda v: v.tensor_tensor(
                        c3(sq, STW)[:, :, 0:1], c3(gt, STW)[:, :, 0:1],
                        col3(gprev), OP.subtract))],
                   r=[gt, gprev], w=[sq])

                def mix2(tmoff, sph, spl):
                    tmb = prm[:, tmoff:tmoff + 8].unsqueeze(2).broadcast_to(
                        [P, CK, W])
                    op("vector", lambda v, tmb=tmb, W=W: v.tensor_tensor(
                        c3(numt, STW)[:, :, 0:W], c3(sq, STW)[:, :, 0:W], tmb,
                        OP.mult), r=[sq, prm], w=[numt])
                    op("vector", [
                        (lambda v, W=W: v.tensor_tensor(
                            c3(numt, STW)[:, :, 1:W], c3(numt, STW)[:, :, 1:W],
                            c3(gt, STW)[:, :, 0:W - 1], OP.add)),
                        (lambda v: v.tensor_tensor(
                            c3(numt, STW)[:, :, 0:1], c3(numt, STW)[:, :, 0:1],
                            col3(gprev), OP.add))],
                       r=[gt, gprev], w=[numt])
                    split_hi_lo(numt, bh, bl, W, STW)
                    op("gpsimd", [
                        (lambda s, c0=c0, W=W, sph=sph: s.dma_start(
                            dchv(sph.t, c0, W), schv(bh, W))),
                        (lambda s, c0=c0, W=W, spl=spl: s.dma_start(
                            dchv(spl.t, c0, W), schv(bl, W)))],
                        r=[bh, bl], w=[sph, spl], kind="st")

                mix2(24, spb["gkh"], spb["gkl"])
                mix2(32, spb["grh"], spb["grl"])
                op("vector", lambda v, W=W: v.tensor_copy(
                    col3(gprev), c3(gt, STW)[:, :, W - 1:W]), r=[gt], w=[gprev])

            NS = len(subs)
            loads_a(0)
            loads_b(0)
            wrproj(0)
            for si in range(NS):
                scanchain(si)
                if si + 1 < NS:
                    loads_a(si + 1)
                rwkv_split(si)
                if si + 1 < NS:
                    wrproj(si + 1)
                woproj(si)
                if si + 1 < NS:
                    loads_b(si + 1)
                rest(si)
        _ph_s1()
        pb.barrier()

        def _ph_k():
         with ExitStack() as sk:
            gkh_r = sbt(sk, "K_gkh", (P, CK * RNG), BF16)
            gkl_r = sbt(sk, "K_gkl", (P, CK * RNG), BF16)
            op("sync", [lambda s: s.dma_start(schv(gkh_r, RNG),
                                              dchv(spb["gkh"].t, 0, RNG)),
                        lambda s: s.dma_start(schv(gkl_r, RNG),
                                              dchv(spb["gkl"].t, 0, RNG))],
               r=[spb["gkh"], spb["gkl"]], w=[gkh_r, gkl_r], kind="ld")
            HD2 = DF // 2
            fkh = sbt(sk, "K_fkh", (P, CK * HD2), BF16)
            fkl = sbt(sk, "K_fkl", (P, CK * HD2), BF16)
            tmp1 = [sbt(sk, "K_t1a", (P, 512)), sbt(sk, "K_t1b", (P, 512))]
            tmp2 = [sbt(sk, "K_t2a", (P, 512)), sbt(sk, "K_t2b", (P, 512))]
            kbh = [sbt(sk, "K_bha", (P, 512), BF16), sbt(sk, "K_bhb", (P, 512), BF16)]
            kbl = [sbt(sk, "K_bla", (P, 512), BF16), sbt(sk, "K_blb", (P, 512), BF16)]
            pmK = [pst(sk, "pmK0", 512), pst(sk, "pmK1", 512)]
    
            NQ = 4
            FLQ = (HD2 // P) // NQ
            fkhF = [Buf(fkh.t) for _ in range(NQ)]
            fklF = [Buf(fkl.t) for _ in range(NQ)]
            pi = 0
            for half in range(2):
                for q_ in range(NQ):
                    tloads([(buf, cc * HD2 + q_ * FLQ * P, FLQ * P, g_,
                             half * HD2 + q_ * FLQ * P,
                             half * HD2 + (q_ + 1) * FLQ * P,
                             cc * P, (cc + 1) * P)
                            for buf, g_ in ((fkh, gsc["fkh"]), (fkl, gsc["fkl"]))
                            for cc in range(CK)],
                           [gsc["fkh"], gsc["fkl"]], [fkhF[q_], fklF[q_]])
                for fl in range(HD2 // P):
                    f = half * (HD2 // P) + fl
                    for ti, (c0, W) in enumerate(tsubs(512, 0, RNG)):
                        bi = pi % 2
                        pi += 1
                        ps = pmK[bi]
                        fns = []
                        for kch in range(CK):
                            lh = fkh[:, kch * HD2 + fl * P:kch * HD2 + (fl + 1) * P]
                            ll = fkl[:, kch * HD2 + fl * P:kch * HD2 + (fl + 1) * P]
                            rh = gkh_r[:, kch * RNG + c0:kch * RNG + c0 + W]
                            rl = gkl_r[:, kch * RNG + c0:kch * RNG + c0 + W]
                            st_, sp_ = kch == 0, kch == CK - 1
                            fns.append(lambda t, lh=lh, rh=rh, st_=st_, ps=ps, W=W:
                                       t.matmul(ps[:, 0:W], lh, rh, start=st_,
                                                stop=False))
                            fns.append(lambda t, lh=lh, rl=rl, ps=ps, W=W:
                                       t.matmul(ps[:, 0:W], lh, rl, start=False,
                                                stop=False))
                            fns.append(lambda t, ll=ll, rh=rh, sp_=sp_, ps=ps, W=W:
                                       t.matmul(ps[:, 0:W], ll, rh, start=False,
                                                stop=sp_))
                        op("tensor", fns,
                           r=[fkhF[fl // FLQ], fklF[fl // FLQ], gkh_r, gkl_r],
                           w=[ps])
                        t1, t2 = tmp1[bi], tmp2[bi]
                        bh_, bl_ = kbh[bi], kbl[bi]
                        op("scalar", lambda s, ps=ps, t1=t1, W=W: s.activation(
                            t1[:, 0:W], ps[:, 0:W], AF.Relu), r=[ps], w=[t1])
                        op("scalar", lambda s, t1=t1, t2=t2, W=W: s.activation(
                            t2[:, 0:W], t1[:, 0:W], AF.Square), r=[t1], w=[t2])
                        op("scalar", lambda s, t2=t2, bh_=bh_, W=W: s.copy(
                            bh_[:, 0:W], t2[:, 0:W]), r=[t2], w=[bh_])
                        op("vector", lambda v, t2=t2, bh_=bh_, bl_=bl_, W=W:
                           v.tensor_tensor(bl_[:, 0:W], t2[:, 0:W], bh_[:, 0:W],
                                           OP.subtract), r=[t2, bh_], w=[bl_])
                        op("gpsimd", lambda s, f=f, c0=c0, bh_=bh_, W=W: s.dma_start(
                            sp_kfh.t[f * P:(f + 1) * P, c0:c0 + W], bh_[:, 0:W]),
                            r=[bh_], w=[sp_kfh], kind="st")
                        op("gpsimd", lambda s, f=f, c0=c0, bl_=bl_, W=W: s.dma_start(
                            sp_kfl.t[f * P:(f + 1) * P, c0:c0 + W], bl_[:, 0:W]),
                            r=[bl_], w=[sp_kfl], kind="st")
        _ph_k()
        pb.barrier()

        def _ph_v():
         with ExitStack() as sv:
            fvh = sbt(sv, "V_fvh", (P, FCK * D), BF16)
            fvl = sbt(sv, "V_fvl", (P, FCK * D), BF16)
            tloads([(buf, f * D, D, g_, 0, D, f * P, (f + 1) * P)
                    for buf, g_ in ((fvh, gsc["fvh"]), (fvl, gsc["fvl"]))
                    for f in range(FCK)],
                   [gsc["fvh"], gsc["fvl"]], [fvh, fvl])
            kfh_t = sbt(sv, "V_kfh", (P, FCK * 512), BF16)
            kfl_t = sbt(sv, "V_kfl", (P, FCK * 512), BF16)
            kfhR = [kfh_t, Buf(kfh_t.t)]
            kflR = [kfl_t, Buf(kfl_t.t)]
            ev = [sbt(sv, "V_ev0", (P, 512)), sbt(sv, "V_ev1", (P, 512))]
            pk = [pst(sv, "pkV0", 2048), pst(sv, "pkV1", 2048)]
            for si, (c0, W) in enumerate(tsubs(256, 0, RNG)):
                sl = si % 2
                o0 = sl * 256
                op("sync", [lambda s, c0=c0, W=W, o0=o0: s.dma_start(
                    kfh_t.t[:].rearrange("p (f t) -> p f t", f=FCK)[:, :, o0:o0 + W],
                    dchv(sp_kfh.t, c0, W, nck=FCK)),
                    lambda s, c0=c0, W=W, o0=o0: s.dma_start(
                    kfl_t.t[:].rearrange("p (f t) -> p f t", f=FCK)[:, :, o0:o0 + W],
                    dchv(sp_kfl.t, c0, W, nck=FCK))],
                   r=[sp_kfh, sp_kfl], w=[kfhR[sl], kflR[sl]], kind="ld")
                for grp in range(2):
                    pkg = pk[grp]
                    fns = []
                    for f in range(FCK):
                        for ci in range(4):
                            c = grp * 4 + ci
                            lh = fvh[:, f * D + c * P:f * D + (c + 1) * P]
                            ll = fvl[:, f * D + c * P:f * D + (c + 1) * P]
                            rh = kfh_t[:, f * 512 + o0:f * 512 + o0 + W]
                            rl = kfl_t[:, f * 512 + o0:f * 512 + o0 + W]
                            st_, sp_ = f == 0, f == FCK - 1
                            o = ci * 512
                            fns.append(lambda t, lh=lh, rh=rh, st_=st_, o=o,
                                       pkg=pkg, W=W:
                                       t.matmul(pkg[:, o:o + W], lh, rh,
                                                start=st_, stop=False))
                            fns.append(lambda t, lh=lh, rl=rl, o=o, pkg=pkg, W=W:
                                       t.matmul(pkg[:, o:o + W], lh, rl,
                                                start=False, stop=False))
                            fns.append(lambda t, ll=ll, rh=rh, sp_=sp_, o=o,
                                       pkg=pkg, W=W:
                                       t.matmul(pkg[:, o:o + W], ll, rh,
                                                start=False, stop=sp_))
                    op("tensor", fns, r=[fvh, fvl, kfhR[sl], kflR[sl]], w=[pkg])
                    for ci in range(4):
                        c = grp * 4 + ci
                        eb = ev[c % 2]
                        op("vector", lambda v, eb=eb, ci=ci, W=W, pkg=pkg:
                           v.tensor_copy(eb.t[:, 0:W], pkg[:, ci * 512:ci * 512 + W]),
                           r=[pkg], w=[eb])
                        op("gpsimd", lambda s, eb=eb, c=c, c0=c0, W=W: s.dma_start(
                            sp["kv"].t[c * P:(c + 1) * P, c0:c0 + W], eb.t[:, 0:W]),
                            r=[eb], w=[sp["kv"]], kind="st")
        _ph_v()
        pb.barrier()

        def _ph_frh():
         with ExitStack() as sf:
            n2q = sbt(sf, "n2q", (P, CK * RNG), BF16)
            r1row = sbt(sf, "r1row", (1, RNG))
            rrow = sbt(sf, "rrow", (1, 512))
            qcb = sbt(sf, "qcb", (P, 16))
            with ExitStack() as sfr:
                frh = sbt(sfr, "frh", (P, CK * D), BF16)
                frl = sbt(sfr, "frl", (P, CK * D), BF16)
                tloads([(buf, cc * D, D, g_, 0, D, cc * P, (cc + 1) * P)
                        for buf, g_ in ((frh, gsc["frh"]), (frl, gsc["frl"]))
                        for cc in range(CK)],
                       [gsc["frh"], gsc["frl"]], [frh, frl])
                pstat = pst(sfr, "R_pst", 1024)
                grh = sbt(sfr, "R_grh", (P, CK * STW), BF16)
                grl = sbt(sfr, "R_grl", (P, CK * STW), BF16)
                y1t = sbt(sfr, "R_y1", (P, CK * STW))
                kvt = sbt(sfr, "R_kv", (P, CK * STW))
                xt_ = sbt(sfr, "R_x", (P, CK * STW))
                sigt = sbt(sfr, "R_sig", (P, CK * STW))
                n2t = sbt(sfr, "R_n2", (P, CK * STW))
                sq = sbt(sfr, "R_sq", (P, CK * STW))
                stb = sbt(sfr, "R_stb", (P, 5 * STW))
                qi = sbt(sfr, "R_qi", (P, CK * STW), I32)
                q0 = sbt(sfr, "R_q0", (P, CK * STW))
                q1 = sbt(sfr, "R_q1", (P, CK * STW))
                pmF = [pst(sfr, "pmR0", 512), pst(sfr, "pmR1", 512)]
                for si, (c0, W) in enumerate(subs):
                    op("sync", [(lambda s, b=b, spn=spn, c0=c0, W=W: s.dma_start(
                        schv(b, W), dchv(spn.t, c0, W)))
                        for b, spn in ((y1t, sp["y1"]), (kvt, sp["kv"]),
                                       (grh, spb["grh"]), (grl, spb["grl"]))] +
                        [lambda s, c0=c0, W=W: s.dma_start(
                            schv(xt_, W), dchv(xT, c0 + 1, W))],
                        r=[sp["y1"], sp["kv"], spb["grh"], spb["grl"]],
                        w=[y1t, kvt, grh, grl, xt_], kind="ld")
                    proj3(pmF, frh, frl, grh, grl, W, STW,
                          lambda dch, ps: op("scalar",
                                             lambda s, dch=dch, ps=ps, W=W:
                                             s.activation(
                                                 sigt[:, dch * STW:dch * STW + W],
                                                 ps[:, 0:W], AF.Sigmoid),
                                             r=[ps], w=[sigt]))
                    op("vector", lambda v, W=W: v.tensor_tensor(
                        c3(kvt, STW)[:, :, 0:W], c3(sigt, STW)[:, :, 0:W],
                        c3(kvt, STW)[:, :, 0:W], OP.mult), r=[sigt], w=[kvt])
                    op("vector", lambda v, W=W: v.tensor_tensor(
                        c3(kvt, STW)[:, :, 0:W], c3(kvt, STW)[:, :, 0:W],
                        c3(y1t, STW)[:, :, 0:W], OP.add), r=[y1t], w=[kvt])
                    ln_na(kvt, n2t, W, STW, sq, None, stb, pstat,
                          mm_stats=True)
                    op("vector", lambda v, W=W: v.tensor_tensor(
                        c3(xt_, STW)[:, :, 0:W], c3(xt_, STW)[:, :, 0:W],
                        c3(n2t, STW)[:, :, 0:W], OP.add), r=[n2t], w=[xt_])
                    op("gpsimd", lambda s, c0=c0, W=W: s.dma_start(
                        dchv(sp["x1"].t, c0, W), schv(xt_, W)),
                        r=[xt_], w=[sp["x1"]], kind="st")
                    ln_na(xt_, n2t, W, STW, sq, None, stb, pstat,
                          mm_stats=True)
                    op("scalar", lambda s, W=W: s.activation(
                        c3(q1, STW)[:, :, 0:W], c3(n2t, STW)[:, :, 0:W], AF.Abs),
                        r=[n2t], w=[q1])
                    op("vector", lambda v, W=W: v.tensor_tensor(
                        c3(sq, STW)[:, 0:4, 0:W], c3(q1, STW)[:, 0:8:2, 0:W],
                        c3(q1, STW)[:, 1:8:2, 0:W], OP.max), r=[q1], w=[sq])
                    op("vector", lambda v, W=W: v.tensor_tensor(
                        c3(sq, STW)[:, 0:4:2, 0:W], c3(sq, STW)[:, 0:4:2, 0:W],
                        c3(sq, STW)[:, 1:4:2, 0:W], OP.max), w=[sq])
                    op("vector", lambda v, W=W: v.tensor_tensor(
                        sq[:, 0:W], sq[:, 0:W], sq[:, 2 * STW:2 * STW + W], OP.max),
                        w=[sq])
                    nK = (W + P - 1) // P
                    for k in range(nK):
                        wk_ = min(P, W - k * P)
                        op("tensor", lambda t, k=k, wk_=wk_: t.transpose(
                            pstat[0:wk_, 0:P], sq[:, k * P:k * P + wk_], idn.t[:]),
                            r=[sq, idn], w=[pstat])
                        op("vector", lambda v, k=k, wk_=wk_: v.reduce_max(
                            qcb.t[0:wk_, k:k + 1], pstat[0:wk_, 0:P], axis=AX),
                            r=[pstat], w=[qcb])
                    op("tensor", [(lambda t, k=k, wk_=min(P, W - k * P): t.matmul(
                        pstat[0:1, 512 + k * P:512 + k * P + wk_],
                        qcb[0:wk_, k:k + 1], idn[0:wk_, 0:wk_],
                        start=True, stop=True)) for k in range(nK)],
                        r=[qcb, idn], w=[pstat])
                    op("vector", lambda v, W=W: v.tensor_scalar_max(
                        rrow[0:1, 0:W], pstat[0:1, 512:512 + W], 1e-5),
                        r=[pstat], w=[rrow])
                    op("vector", lambda v, W=W: v.reciprocal(
                        rrow[0:1, 256:256 + W], rrow[0:1, 0:W]), w=[rrow])
                    op("vector", lambda v, c0=c0, W=W: v.tensor_scalar_mul(
                        r1row.t[0:1, c0:c0 + W], rrow[0:1, 256:256 + W], 127.0),
                        r=[rrow], w=[r1row])
                    op("tensor", lambda t, c0=c0, W=W: t.matmul(
                        pstat[:, 0:W], onr[0:1, 0:P], r1row[0:1, c0:c0 + W],
                        start=True, stop=True), r=[onr, r1row], w=[pstat])
                    op("vector", lambda v, W=W: v.tensor_tensor(
                        c3(q0, STW)[:, :, 0:W], c3(n2t, STW)[:, :, 0:W],
                        bc3(pstat[:, 0:W], CK, W), OP.mult),
                        r=[n2t, pstat], w=[q0])
                    op("vector", lambda v, W=W: v.tensor_copy(
                        c3(qi, STW)[:, :, 0:W], c3(q0, STW)[:, :, 0:W]),
                        r=[q0], w=[qi])
                    op("vector", lambda v, W=W: v.tensor_copy(
                        c3(q1, STW)[:, :, 0:W], c3(qi, STW)[:, :, 0:W]),
                        r=[qi], w=[q1])
                    op("vector", lambda v, W=W: v.tensor_tensor(
                        c3(sq, STW)[:, :, 0:W], c3(q0, STW)[:, :, 0:W],
                        c3(q1, STW)[:, :, 0:W], OP.subtract), r=[q0, q1], w=[sq])
                    op("vector", lambda v, W=W: v.scalar_tensor_tensor(
                        c3(q0, STW)[:, :, 0:W], c3(sq, STW)[:, :, 0:W], 0.5,
                        c3(q1, STW)[:, :, 0:W], OP.is_gt, OP.add),
                        r=[sq, q1], w=[q0])
                    op("vector", lambda v, W=W: v.tensor_scalar(
                        c3(q1, STW)[:, :, 0:W], c3(sq, STW)[:, :, 0:W],
                        -0.5, -1.0, OP.is_lt, OP.mult), r=[sq], w=[q1])
                    op("vector", lambda v, c0=c0, W=W: v.tensor_tensor(
                        c3(n2q, RNG)[:, :, c0:c0 + W], c3(q0, STW)[:, :, 0:W],
                        c3(q1, STW)[:, :, 0:W], OP.add), r=[q0, q1], w=[n2q])
            pb.barrier()

            with ExitStack() as sh:
                w1t = sbt(sh, "H_w1t", (P, CK * DF), BF16)
                tloads([(w1t, cc * DF, DF, g_w1q, 0, DF, cc * P, (cc + 1) * P)
                        for cc in range(CK)], [g_w1q], [w1t])
                pmH = [pst(sh, "pmH0", 512), pst(sh, "pmH1", 512)]
                rht = sbt(sh, "H_rh", (P, RNG))
                for c0_, W_ in tsubs(512, 0, RNG):
                    op("tensor", lambda t, c0_=c0_, W_=W_: t.matmul(
                        pmH[0][:, 0:W_], onr[0:1, 0:P], r1row[0:1, c0_:c0_ + W_],
                        start=True, stop=True), r=[onr, r1row], w=[pmH[0]])
                    op("vector", lambda v, c0_=c0_, W_=W_: v.reciprocal(
                        rht.t[:, c0_:c0_ + W_], pmH[0][:, 0:W_]),
                        r=[pmH[0]], w=[rht])
                op("vector", lambda v: v.tensor_scalar_mul(rht.t[:], rht.t[:],
                                                           swc[:, 2:3]),
                   r=[swc], w=[rht])
                hbuf = [sbt(sh, "H_h0", (P, RNG)), sbt(sh, "H_h1", (P, RNG))]
                cbuf = [sbt(sh, "H_c0", (P, NW)), sbt(sh, "H_c1", (P, NW))]
                hsb = [sbt(sh, "H_s0", (P, NW)), sbt(sh, "H_s1", (P, NW))]
                mrun = sbt(sh, "H_mr", (P, NW))
                op("vector", lambda v: v.memset(mrun.t[:], 0.0), w=[mrun])
                for f in range(FCK):
                    hb_, cb_, sb_ = hbuf[f % 2], cbuf[f % 2], hsb[f % 2]
                    for ti, (c0, W) in enumerate(tsubs(512, 0, RNG)):
                        ps = pmH[ti % 2]
                        op("tensor", [(lambda t, f=f, kch=kch, c0=c0, W=W, ps=ps:
                                       t.matmul(ps[:, 0:W],
                                                w1t[:, kch * DF + f * P:
                                                    kch * DF + (f + 1) * P],
                                                n2q[:, kch * RNG + c0:
                                                    kch * RNG + c0 + W],
                                                start=(kch == 0),
                                                stop=(kch == CK - 1)))
                                      for kch in range(CK)],
                           r=[w1t, n2q], w=[ps])
                        op("vector", lambda v, c0=c0, W=W, ps=ps, hb_=hb_:
                           v.tensor_tensor(hb_.t[:, c0:c0 + W], ps[:, 0:W],
                                           rht[:, c0:c0 + W], OP.mult),
                           r=[ps, rht], w=[hb_])
                    op("vector", [
                        lambda v, hb_=hb_: v.tensor_scalar_mul(
                            hb_.t[:, HB - 1:HB], hb_.t[:, HB - 1:HB], met[:, 0:1]),
                        lambda v, hb_=hb_: v.tensor_scalar_mul(
                            hb_.t[:, OWN1:OWN1 + 1], hb_.t[:, OWN1:OWN1 + 1],
                            met[:, 1:2])], r=[met], w=[hb_])
                    op("vector", lambda v, f=f, hb_=hb_, cb_=cb_: v.tensor_scalar(
                        cb_.t[:], hb_.t[:, HB:HB + NW],
                        prm[:, 56 + 3 * f + 1:56 + 3 * f + 2],
                        prm[:, 152 + f:152 + f + 1], OP.mult, OP.add),
                        r=[hb_, prm], w=[cb_])
                    op("vector", lambda v, f=f, hb_=hb_, cb_=cb_:
                       v.scalar_tensor_tensor(
                           cb_.t[:], hb_.t[:, HB - 1:HB - 1 + NW],
                           prm[:, 56 + 3 * f:56 + 3 * f + 1], cb_.t[:],
                           OP.mult, OP.add), r=[hb_, prm], w=[cb_])
                    op("vector", lambda v, f=f, hb_=hb_, cb_=cb_:
                       v.scalar_tensor_tensor(
                           cb_.t[:], hb_.t[:, HB + 1:HB + 1 + NW],
                           prm[:, 56 + 3 * f + 2:56 + 3 * f + 3], cb_.t[:],
                           OP.mult, OP.add), r=[hb_, prm], w=[cb_])
                    op("scalar", lambda s, cb_=cb_, sb_=sb_: s.activation(
                        sb_.t[:], cb_.t[:], AF.Silu), r=[cb_], w=[sb_])
                    op("scalar", lambda s, cb_=cb_, sb_=sb_: s.activation(
                        cb_.t[:], sb_.t[:], AF.Abs), r=[sb_], w=[cb_])
                    op("vector", lambda v, cb_=cb_: v.tensor_tensor(
                        mrun.t[:], mrun.t[:], cb_.t[:], OP.max), r=[cb_], w=[mrun])
                    op("gpsimd", lambda s, f=f, sb_=sb_: s.dma_start(
                        sp_hs.t[f * P:(f + 1) * P, :], sb_.t[:]),
                        r=[sb_], w=[sp_hs], kind="st")
                for k in range(NW // P):
                    pp = pmH[k % 2]
                    op("tensor", lambda t, k=k, pp=pp: t.transpose(
                        pp[0:P, 0:P], mrun[:, k * P:k * P + P], idn.t[:]),
                        r=[mrun, idn], w=[pp])
                    op("vector", lambda v, k=k, pp=pp: v.reduce_max(
                        fqc.t[:, k:k + 1], pp[0:P, 0:P], axis=AX), r=[pp], w=[fqc])
        _ph_frh()
        pb.barrier()

        def _ph_f():
         with ExitStack() as sF:
            w2t = sbt(sF, "F_w2t", (P, FCK * D), BF16)
            tloads([(w2t, f * D, D, g_w2q, 0, D, f * P, (f + 1) * P)
                    for f in range(FCK)], [g_w2q], [w2t])
            psL = pst(sF, "F_psL", 1024)
            psR = pst(sF, "F_psR", 512)
            pkF = pst(sF, "F_pkF", 2048)
            rrow = sbt(sF, "F_rr", (1, 512))
            rbc = [sbt(sF, "F_rb0", (P, 512)), sbt(sF, "F_rb1", (P, 512))]
            hst = sbt(sF, "F_hs", (P, FCK * STF))
            hq2 = [sbt(sF, "F_hq0", (P, FCK * STF), BF16),
                   sbt(sF, "F_hq1", (P, FCK * STF), BF16)]
            x1t = sbt(sF, "F_x1", (P, CK * STF))
            qa = sbt(sF, "F_qa", (P, CK * STF))
            qb = sbt(sF, "F_qb", (P, CK * STF))
            sq = sbt(sF, "F_sq", (P, CK * STF))
            qa2 = sbt(sF, "F_qa2", (P, CK * STF))
            qb2 = sbt(sF, "F_qb2", (P, CK * STF))
            stb = sbt(sF, "F_stb", (P, 5 * STF))
            tiles = tsubs(STF, OWN0, OWN1)

            def emit_loads(ti):
                c0, W = tiles[ti]
                ow = c0 - OWN0
                op("sync", lambda s, ow=ow, W=W: s.dma_start(
                    hst.t[:].rearrange("p (f t) -> p f t", f=FCK)[:, :, 0:W],
                    dchv(sp_hs.t, ow, W, nck=FCK)),
                    r=[sp_hs], w=[hst], kind="ld")

            def emit_x1load(ti):
                c0, W = tiles[ti]
                op("sync", lambda s, c0=c0, W=W: s.dma_start(
                    schv(x1t, W), dchv(sp["x1"].t, c0, W)),
                    r=[sp["x1"]], w=[x1t], kind="ld")

            def emit_rows(ti):
                pi = ti % 2
                W = tiles[ti][1]
                for j in range((W + P - 1) // P):
                    op("tensor", lambda t, j=j, ti=ti: t.matmul(
                        psR[0:1, 0:P], fqc[:, 2 * ti + j:2 * ti + j + 1],
                        idn.t[:], start=True, stop=True), r=[fqc, idn], w=[psR])
                    op("vector", lambda v, j=j: v.tensor_copy(
                        rrow[0:1, j * P:(j + 1) * P], psR[0:1, 0:P]),
                        r=[psR], w=[rrow])
                op("vector", lambda v, W=W: v.tensor_scalar_max(
                    rrow[0:1, 0:W], rrow[0:1, 0:W], 1e-5), w=[rrow])
                op("vector", lambda v, W=W: v.reciprocal(
                    rrow[0:1, 256:256 + W], rrow[0:1, 0:W]), w=[rrow])
                op("vector", lambda v, W=W: v.tensor_scalar_mul(
                    rrow[0:1, 0:W], rrow[0:1, 256:256 + W], 127.0), w=[rrow])
                op("vector", lambda v, W=W: v.reciprocal(
                    rrow[0:1, 256:256 + W], rrow[0:1, 0:W]), w=[rrow])
                op("vector", lambda v, W=W: v.tensor_scalar_mul(
                    rrow[0:1, 256:256 + W], rrow[0:1, 256:256 + W],
                    swc[0:1, 3:4]), r=[swc], w=[rrow])
                op("tensor", lambda t, W=W: t.matmul(
                    psR[:, 0:W], onr[0:1, 0:P], rrow[0:1, 0:W],
                    start=True, stop=True), r=[onr, rrow], w=[psR])
                op("vector", lambda v, pi=pi, W=W: v.tensor_copy(
                    rbc[pi].t[:, 0:W], psR[:, 0:W]), r=[psR], w=[rbc[pi]])
                op("tensor", lambda t, W=W: t.matmul(
                    psR[:, 256:256 + W], onr[0:1, 0:P], rrow[0:1, 256:256 + W],
                    start=True, stop=True), r=[onr, rrow], w=[psR])
                op("vector", lambda v, pi=pi, W=W: v.tensor_copy(
                    rbc[pi].t[:, 256:256 + W], psR[:, 256:256 + W]),
                    r=[psR], w=[rbc[pi]])

            def emit_quant(ti):
                pi = ti % 2
                W = tiles[ti][1]
                hq = hq2[pi]
                sqi3 = sq.t[:].bitcast(I32).rearrange("p (ck t) -> p ck t", t=STF)
                for grp in range(4):
                    go = grp * 2
                    op("vector", lambda v, grp=grp, W=W: v.tensor_tensor(
                        c3(qa, STF)[:, :, 0:W],
                        c3(hst, STF, n=8, o=grp * 8)[:, :, 0:W],
                        bc3(rbc[pi][:, 0:W], CK, W), OP.mult),
                        r=[hst, rbc[pi]], w=[qa])
                    op("vector", lambda v, W=W, sqi3=sqi3: v.tensor_copy(
                        sqi3[:, :, 0:W], c3(qa, STF)[:, :, 0:W]), r=[qa], w=[sq])
                    op("vector", lambda v, W=W, sqi3=sqi3: v.tensor_copy(
                        c3(qb, STF)[:, :, 0:W], sqi3[:, :, 0:W]), r=[sq], w=[qb])
                    op("vector", lambda v, W=W: v.tensor_tensor(
                        c3(sq, STF)[:, :, 0:W], c3(qa, STF)[:, :, 0:W],
                        c3(qb, STF)[:, :, 0:W], OP.subtract), r=[qa, qb], w=[sq])
                    op("vector", lambda v, W=W: v.scalar_tensor_tensor(
                        c3(qa, STF)[:, :, 0:W], c3(sq, STF)[:, :, 0:W], 0.5,
                        c3(qb, STF)[:, :, 0:W], OP.is_gt, OP.add),
                        r=[sq, qb], w=[qa])
                    op("vector", lambda v, W=W: v.tensor_scalar(
                        c3(qb, STF)[:, :, 0:W], c3(sq, STF)[:, :, 0:W],
                        -0.5, -1.0, OP.is_lt, OP.mult), r=[sq], w=[qb])
                    op("vector", lambda v, grp=grp, W=W, hq=hq: v.tensor_tensor(
                        c3(hq, STF, n=8, o=grp * 8)[:, :, 0:W],
                        c3(qa, STF)[:, :, 0:W], c3(qb, STF)[:, :, 0:W],
                        OP.add), r=[qa, qb], w=[hq])

            def emit_mm(ti):
                pi = ti % 2
                W = tiles[ti][1]
                hq = hq2[pi]
                for grp in range(2):
                    fns = []
                    for f in range(FCK):
                        for ci in range(4):
                            c = grp * 4 + ci
                            lh = w2t[:, f * D + c * P:f * D + (c + 1) * P]
                            rh = hq[:, f * STF:f * STF + W]
                            fns.append(lambda t, lh=lh, rh=rh, st_=(f == 0),
                                       sp_=(f == FCK - 1), o=ci * 512, W=W:
                                       t.matmul(pkF[:, o:o + W], lh, rh,
                                                start=st_, stop=sp_))
                    op("tensor", fns, r=[w2t, hq], w=[pkF])
                    op("vector", lambda v, grp=grp, pi=pi, W=W: v.tensor_tensor(
                        c3(qa2, STF, n=4, o=grp * 4)[:, :, 0:W],
                        pkF.t[:].rearrange("p (ck t) -> p ck t", t=512)[:, :, 0:W],
                        bc3(rbc[pi][:, 256:256 + W], 4, W), OP.mult),
                       r=[pkF, rbc[pi]], w=[qa2])

            def emit_tail(ti):
                c0, W = tiles[ti]
                ow = c0 - OWN0
                ln_na(qa2, qb2, W, STF, qb2, None, stb, psL, mm_stats=True)
                emit_x1load(ti)
                op("vector", lambda v, W=W: v.tensor_tensor(
                    c3(qb2, STF)[:, :, 0:W], c3(qb2, STF)[:, :, 0:W],
                    c3(x1t, STF)[:, :, 0:W], OP.add), r=[x1t], w=[qb2])
                op("gpsimd", lambda s, ow=ow, W=W: s.dma_start(
                    dchv(outT.t, ow, W), schv(qb2, W)),
                    r=[qb2], w=[outT], kind="st")

            NT = len(tiles)
            emit_loads(0)
            emit_rows(0)
            emit_quant(0)
            for ti in range(NT):
                if ti + 1 < NT:
                    emit_loads(ti + 1)
                    emit_rows(ti + 1)
                    emit_quant(ti + 1)
                emit_mm(ti)
                emit_tail(ti)
        _ph_f()

        pb.fence("sync", "st")
        pb.fence("sync", "ld")
        pb.recs["sync"].append(["inst", lambda s: s.nop(), None, None, False, False])

        E = dict(sync=nc.sync, scalar=nc.scalar, vector=nc.vector,
                 tensor=nc.tensor, gpsimd=nc.gpsimd)
        blk = ctx.enter_context(nc.Block())
        pb.run(blk, E, sem_h)
    return nc


def host_inputs(inputs, n_own):
    x = np.asarray(inputs["L0_attn_x"], dtype=np.float32)
    B, T, _ = x.shape
    RNG = HB + n_own + HF
    NX = RNG + 1

    def col(vec):
        return np.ascontiguousarray(np.asarray(vec, np.float32).reshape(CK, P).T)

    pcols = np.zeros((P, 184), np.float32)
    pcols[:, 0:8] = col(inputs["L0_attn_tm_k"])
    pcols[:, 8:16] = col(inputs["L0_attn_tm_v"])
    pcols[:, 16:24] = col(inputs["L0_attn_tm_r"])
    pcols[:, 24:32] = col(inputs["L0_attn_fm_k"])
    pcols[:, 32:40] = col(inputs["L0_attn_fm_r"])
    pcols[:, 40:48] = col(inputs["L0_attn_time_decay"])
    pcols[:, 48:56] = col(inputs["L0_attn_time_first"])
    cw = np.asarray(inputs["L0_ffn_conv_w"], np.float32)[:, 0, :]
    pcols[:, 56:152] = cw.reshape(FCK, P, 3).transpose(1, 0, 2).reshape(P, 96)
    pcols[:, 152:184] = np.asarray(inputs["L0_ffn_conv_b"], np.float32).reshape(FCK, P).T

    import ml_dtypes
    ident = np.eye(P, dtype=np.float32)
    identb = np.eye(P, dtype=ml_dtypes.bfloat16)
    td = np.asarray(inputs["L0_attn_time_decay"], np.float32)
    td8 = np.ascontiguousarray(np.tile(td[None, :], (8, 1)))
    Wm = {nm: np.ascontiguousarray(np.asarray(inputs[key], np.float32))
          for nm, key in (("wk", "L0_attn_Wk"), ("wv", "L0_attn_Wv"),
                          ("wr", "L0_attn_Wr"), ("wo", "L0_attn_Wo"),
                          ("fr", "L0_attn_Fr"), ("fk", "L0_attn_Fk"),
                          ("fv", "L0_attn_Fv"), ("w1", "L0_ffn_w1"),
                          ("w2", "L0_ffn_w2"))}

    in_maps = []
    for c in range(NCORE):
        b, q = divmod(c, 4)
        t0 = q * n_own
        lo = t0 - 9
        xs = np.zeros((NX, D), np.float32)
        s0, s1_ = max(lo, 0), min(lo + NX, T)
        xs[s0 - lo:s1_ - lo] = x[b, s0:s1_]
        meta = np.zeros((P, 8), np.float32)
        meta[:, 0] = 0.0 if q == 0 else 1.0
        meta[:, 1] = 0.0 if q == 3 else 1.0
        meta[:, 2] = 1.0 - meta[0, 0]
        meta[:, 6] = EPS
        meta[:, 7] = float(n_own - 8)
        for j in range(8):
            jb = j // 4
            if jb == b and j < c and (c - j) <= 3:
                meta[j, 3] = float(n_own * (c - 1 - j))
                meta[j, 4] = 1.0
            if jb == b and j == c - 1:
                meta[j, 5] = 1.0
        m = dict(xT=np.ascontiguousarray(xs.T), params_pm=pcols, meta=meta,
                 ident=ident, identb=identb,
                 onescol=np.ones((P, 1), np.float32),
                 onesrow=np.ones((1, P), np.float32),
                 ones2d=np.ones((P, P), np.float32), td8=td8)
        for nm, arr in Wm.items():
            m["w_" + nm] = arr
        in_maps.append(m)
    return in_maps


def kernel(**inputs):
    x = np.asarray(inputs["L0_attn_x"])
    B, T, _ = x.shape
    n_own = B * T // NCORE
    nc = bass.Bass("TRN2", target_bir_lowering=False)
    build(nc, n_own)
    in_maps = host_inputs(inputs, n_own)
    res = run_bass_kernel_spmd(nc, in_maps, core_ids=list(range(NCORE)))
    out = np.empty((B, T, D), np.float32)
    for c in range(NCORE):
        b, q = divmod(c, 4)
        out[b, q * n_own:(q + 1) * n_own, :] = np.asarray(res.results[c]["outT"]).T
    return out


# version 29
# speedup vs baseline: 1.0137x; 1.0137x over previous
import numpy as np
from contextlib import ExitStack

import concourse.bass as bass
import concourse.mybir as mybir
from concourse.bass_utils import run_bass_kernel_spmd

F32 = mybir.dt.float32
BF16 = mybir.dt.bfloat16
I32 = mybir.dt.int32
AF = mybir.ActivationFunctionType
OP = mybir.AluOpType
AX = mybir.AxisListType.X

P = 128
CK = 8
D = 1024
DF = 4096
FCK = 32
EPS = 1e-5
NCORE = 8
HB = 8
HF = 8

ENGS = ("sync", "scalar", "vector", "tensor", "gpsimd")
_ESEM = {"scalar": "s", "vector": "v", "tensor": "t", "gpsimd": "g"}
_SEMS = ("ld", "st", "lw", "l2", "lw2", "l3", "v", "s", "t", "g")


class Buf:
    __slots__ = ("t", "w", "rs")

    def __init__(self, t=None):
        self.t = t
        self.w = None
        self.rs = {}

    def __getitem__(self, sl):
        return self.t[sl]


class PB:
    def __init__(self):
        self.recs = {e: [] for e in ENGS}
        self.cnt = {n: 0 for n in _SEMS}
        self.marks = {e: {} for e in ENGS}
        self.since_drain = {e: 0 for e in ENGS}
        self.lastrec = {e: None for e in ENGS}
        self.bardeps = None
        self.barseen = set()

    def dep(self, name):
        return (name, self.cnt[name])

    def _want(self, eng, d):
        if d is None:
            return
        n, v = d
        if v <= 0:
            return
        if self.marks[eng].get(n, 0) >= v:
            return
        self.marks[eng][n] = v
        self.recs[eng].append(("wait", n, v))

    def op(self, eng, fns, r=(), w=(), deps=(), kind="c"):
        if not isinstance(fns, (list, tuple)):
            fns = [fns]
        dd = list(deps)
        if self.bardeps is not None and eng not in self.barseen:
            self.barseen.add(eng)
            dd.extend(self.bardeps)
        dma = kind != "c"
        sem = kind if dma else _ESEM[eng]
        for b in r:
            dd.append(b.w)
        for b in w:
            if b.rs:
                dd.extend((k, val) for k, val in b.rs.items())
            else:
                dd.append(b.w)
        for d in dd:
            self._want(eng, d)
        last = None
        for fn in fns:
            if dma:
                self.cnt[sem] += 16
                last = ["inst", fn, sem, self.cnt[sem], True, True]
            else:
                last = ["inst", fn, sem, None, False, False]
            self.recs[eng].append(last)
            self.since_drain[eng] += 1
        if not dma:
            self.cnt[sem] += 1
            last[3] = self.cnt[sem]
            last[4] = True
        self.lastrec[eng] = last
        t = (sem, self.cnt[sem])
        for b in r:
            b.rs[sem] = max(b.rs.get(sem, 0), t[1])
        for b in w:
            b.w = t
            b.rs = {}
        return t

    def tagnow(self, eng):
        sem = _ESEM[eng]
        rec = self.lastrec[eng]
        if rec is not None and not rec[4]:
            self.cnt[sem] += 1
            rec[3] = self.cnt[sem]
            rec[4] = True
        return (sem, self.cnt[sem])

    def barrier(self):
        for e in ("scalar", "vector", "tensor", "gpsimd"):
            if self.lastrec[e] is not None:
                self.tagnow(e)
        self.bardeps = [(n, self.cnt[n]) for n in _SEMS if self.cnt[n] > 0]
        self.barseen = set()

    def fence(self, eng, sem):
        v = self.cnt[sem]
        if v <= 0:
            return
        self.recs[eng].append(("wait", sem, v))
        self.marks[eng][sem] = max(self.marks[eng].get(sem, 0), v)

    def run(self, blk, engines, sem_h):
        maxw = {}
        waited = {n: set() for n in ("ld", "st", "lw", "l2", "lw2", "l3")}
        for e in ENGS:
            for rec in self.recs[e]:
                if rec[0] == "wait":
                    maxw[rec[1]] = max(maxw.get(rec[1], 0), rec[2])
                    if rec[1] in waited:
                        waited[rec[1]].add(rec[2])
        for e in ENGS:
            recs = self.recs[e]
            if not any(rec[0] == "inst" and rec[5] for rec in recs):
                continue
            out = []
            fenced = {}
            for rec in recs:
                if rec[0] == "inst" and rec[5]:
                    s_, v_ = rec[2], rec[3]
                    prev = v_ - 16
                    if (prev > fenced.get(s_, 0) and prev in waited[s_]):
                        out.append(("wait", s_, prev))
                        fenced[s_] = prev
                elif rec[0] == "wait" and rec[1] in waited:
                    fenced[rec[1]] = max(fenced.get(rec[1], 0), rec[2])
                out.append(rec)
            self.recs[e] = out
        for ename in ENGS:
            recs = self.recs[ename]
            if not recs:
                continue

            def body(E, recs=recs):
                for rec in recs:
                    if rec[0] == "wait":
                        E.wait_ge(sem_h[rec[1]], rec[2])
                    elif rec[0] == "drain":
                        E.drain()
                    else:
                        _, fn, sem, val, has, dma = rec
                        inst = fn(E)
                        if has and (dma or val <= maxw.get(sem, 0)):
                            inst.then_inc(sem_h[sem], 16 if dma else 1)
            getattr(blk, ename)(body)


def build(nc, n_own):
    RNG = HB + n_own + HF
    WA = (RNG - 16) // 8
    assert 8 * WA + 16 == RNG and WA % 8 == 0 and WA >= 16
    subs = [(s * WA, WA) for s in range(8)] + [(8 * WA, 16)]
    NX = RNG + 1
    OWN0, OWN1 = HB, HB + n_own
    NW = n_own
    STA = WA + 1
    STW = WA
    STS = WA + 1
    STF = 256
    assert NW % STF == 0 or NW < STF

    def tsubs(width, lo, hi):
        out, c = [], lo
        while c < hi:
            w = min(width, hi - c)
            out.append((c, w))
            c += w
        return out

    def ein(name, shape, dt=F32):
        return nc.dram_tensor(name, list(shape), dt, kind="ExternalInput").ap()

    def idr(name, shape, dt):
        return Buf(nc.dram_tensor(name, list(shape), dt, kind="Internal").ap())

    xT = ein("xT", (D, NX))
    prm_in = ein("params_pm", (P, 184))
    meta_in = ein("meta", (P, 8))
    ident_in = ein("ident", (P, P))
    identb_in = ein("identb", (P, P), BF16)
    onescol_in = ein("onescol", (P, 1))
    onesrow_in = ein("onesrow", (1, P))
    ones2d_in = ein("ones2d", (P, P))
    td8_in = ein("td8", (8, D))
    wfull = {w: ein("w_" + w, (DF if w in ("fk", "w1") else D,
                               DF if w in ("fv", "w2") else D))
             for w in ("wk", "wv", "wr", "wo", "fr", "fk", "fv", "w1", "w2")}
    outT = Buf(nc.dram_tensor("outT", [D, n_own], F32, kind="ExternalOutput").ap())

    gsc = {}
    for w in ("wk", "wv", "wr", "wo", "fr", "fk", "fv"):
        r = DF if w == "fk" else D
        c = DF if w == "fv" else D
        for hl in ("h", "l"):
            gsc[w + hl] = idr(f"g_{w}{hl}", (r, c), BF16)
    g_w1q = idr("g_w1q", (DF, D), BF16)
    g_w2q = idr("g_w2q", (D, DF), BF16)
    ci_r1 = idr("ci_r1", (4, D), F32)
    g_r1 = idr("g_r1", (4 * NCORE, D), F32)
    ci_r2 = idr("ci_r2", (2, D), F32)
    g_r2 = idr("g_r2", (2 * NCORE, D), F32)
    sp = {n: idr("sp_" + n, (D, RNG), F32) for n in ("n1", "wk", "z", "y1", "kv", "x1")}
    spb = {n: idr("sp_" + n, (D, RNG), BF16)
           for n in ("xrh", "xrl", "gkh", "gkl", "grh", "grl")}
    sp_kfh = idr("sp_kfh", (DF, RNG), BF16)
    sp_kfl = idr("sp_kfl", (DF, RNG), BF16)
    sp_hs = idr("sp_hs", (DF, NW), F32)

    def dchv(ap_, c0, w, nck=CK):
        return ap_.rearrange("(ck p) t -> p ck t", p=P)[:, :, c0:c0 + w]

    def schv(buf, w, nck=CK):
        return buf.t[:].rearrange("p (ck t) -> p ck t", ck=nck)[:, :, 0:w]

    def c3(buf, st_, n=CK, o=0):
        v = buf.t[:].rearrange("p (ck t) -> p ck t", t=st_)
        return v if (n is None and o == 0) else v[:, o:o + n, :]

    def bc3(row_ap, n, w_):
        return row_ap.unsqueeze(1).broadcast_to([P, n, w_])

    def col3(buf2d, n=CK):
        return buf2d.t[:].unsqueeze(2)

    pb = PB()
    op, dep = pb.op, pb.dep
    RG = list(range(NCORE))

    with ExitStack() as ctx:
        sem_h = {n: ctx.enter_context(nc.semaphore("sem_" + n)) for n in _SEMS}

        def sbt(st, name, shape, dt=F32):
            return Buf(st.enter_context(nc.sbuf_tensor(name, list(shape), dt)))

        def pst(st, name, cols):
            return Buf(st.enter_context(nc.psum_tensor(name, [P, cols], F32)))

        prm = sbt(ctx, "prm", (P, 184))
        met = sbt(ctx, "met", (P, 8))
        idn = sbt(ctx, "idn", (P, P))
        idb = sbt(ctx, "idb", (P, P), BF16)
        onc = sbt(ctx, "onc", (P, 1))
        onr = sbt(ctx, "onr", (1, P))
        on2 = sbt(ctx, "on2", (P, P))
        pm8 = sbt(ctx, "pm8", (P, 8))
        d_pm = sbt(ctx, "d_pm", (P, 8))
        w_pm = sbt(ctx, "w_pm", (P, 8))
        eu_pm = sbt(ctx, "eu_pm", (P, 8))
        gbuf = sbt(ctx, "gbuf", (P, 32))
        iA = sbt(ctx, "iA", (P, 8))
        iB = sbt(ctx, "iB", (P, 8))
        stA = sbt(ctx, "stA", (P, 8))
        stB = sbt(ctx, "stB", (P, 8))
        swc = sbt(ctx, "swc", (P, 4))
        fqc = sbt(ctx, "fqc", (P, 16))

        nc.const_aps.register(F32, EPS, met[:, 6:7])
        nc.const_aps.register(F32, float(n_own - 8), met[:, 7:8])
        op("sync", [(lambda s, d=dst, sr=src: s.dma_start(d.t[:], sr))
                    for dst, src in ((prm, prm_in), (met, meta_in), (idn, ident_in),
                                     (idb, identb_in), (onc, onescol_in),
                                     (onr, onesrow_in), (on2, ones2d_in))],
           w=[prm, met, idn, idb, onc, onr, on2], kind="ld")
        op("vector", [(lambda v, b=b: v.memset(b.t[:], 0.0))
                      for b in (stA, stB, gbuf, iA, iB)],
           w=[stA, stB, gbuf, iA, iB])
        op("scalar", lambda s: s.activation(pm8.t[:], prm[:, 40:48], AF.Exp),
           r=[prm], w=[pm8])
        op("vector", lambda v: v.tensor_scalar_mul(w_pm.t[:], pm8.t[:], -1.0),
           r=[pm8], w=[w_pm])
        op("scalar", lambda s: s.activation(d_pm.t[:], w_pm.t[:], AF.Exp),
           r=[w_pm], w=[d_pm])
        op("scalar", lambda s: s.activation(eu_pm.t[:], prm[:, 48:56], AF.Exp),
           r=[prm], w=[eu_pm])

        def build_dtile(dt_buf):
            op("vector", lambda v: v.memset(dt_buf.t[:], 0.0), w=[dt_buf])
            op("scalar", [(lambda s, ck=ck: s.activation(
                dt_buf[:, ck * WA:(ck + 1) * WA], dt_buf[:, ck * WA:(ck + 1) * WA],
                AF.Identity, bias=d_pm[:, ck:ck + 1], scale=0.0))
                for ck in range(CK)], r=[d_pm], w=[dt_buf])

        def ln_na(xbuf, outbuf, W, ST, sqb, trb, stb, pstat, pe="vector",
                  mm_stats=False):
            x3, s3 = c3(xbuf, ST), c3(sqb, ST)
            o3 = c3(outbuf, ST)
            op("scalar", lambda s: s.activation(
                s3[:, :, 0:W], x3[:, :, 0:W], AF.Square), r=[xbuf], w=[sqb])

            if mm_stats:
                fns = []
                for src, off in ((xbuf, 0), (sqb, 512)):
                    for ck in range(CK):
                        fns.append(lambda t, src=src, off=off, ck=ck: t.matmul(
                            pstat[:, off:off + W], on2.t[:],
                            src[:, ck * ST:ck * ST + W],
                            start=(ck == 0), stop=(ck == CK - 1)))
                op("tensor", fns, r=[on2, xbuf, sqb], w=[pstat])
            else:
                t3 = c3(trb, ST, n=4)

                def tree(srcb, src3, doff):
                    op(pe, lambda v: v.tensor_tensor(
                        t3[:, 0:4, 0:W], src3[:, 0:8:2, 0:W], src3[:, 1:8:2, 0:W],
                        OP.add), r=[srcb], w=[trb])
                    op(pe, lambda v: v.tensor_tensor(
                        t3[:, 0:4:2, 0:W], t3[:, 0:4:2, 0:W], t3[:, 1:4:2, 0:W],
                        OP.add), w=[trb])
                    op(pe, lambda v, doff=doff: v.tensor_tensor(
                        stb[:, doff * ST:doff * ST + W], trb[:, 0:W],
                        trb[:, 2 * ST:2 * ST + W], OP.add), r=[trb], w=[stb])
                tree(xbuf, x3, 0)
                tree(sqb, s3, 1)
                op("tensor", [
                    lambda t: t.matmul(pstat[:, 0:W], on2.t[:], stb[:, 0:W],
                                       start=True, stop=True),
                    lambda t: t.matmul(pstat[:, 512:512 + W], on2.t[:],
                                       stb[:, ST:ST + W], start=True, stop=True)],
                    r=[on2, stb], w=[pstat])
            op("vector", lambda v: v.tensor_scalar_mul(
                stb[:, 2 * ST:2 * ST + W], pstat[:, 0:W], 1.0 / D),
                r=[pstat], w=[stb])
            op("vector", lambda v: v.tensor_tensor(
                stb[:, 3 * ST:3 * ST + W], stb[:, 2 * ST:2 * ST + W],
                stb[:, 2 * ST:2 * ST + W], OP.mult), w=[stb])
            op("vector", lambda v: v.scalar_tensor_tensor(
                stb[:, 4 * ST:4 * ST + W], pstat[:, 512:512 + W], 1.0 / D,
                stb[:, 3 * ST:3 * ST + W], OP.mult, OP.subtract),
                r=[pstat], w=[stb])
            op("scalar", lambda s: s.activation(
                stb[:, 3 * ST:3 * ST + W], stb[:, 4 * ST:4 * ST + W], AF.Sqrt,
                bias=EPS), r=[met], w=[stb])
            op("vector", lambda v: v.reciprocal(
                stb[:, 4 * ST:4 * ST + W], stb[:, 3 * ST:3 * ST + W]), w=[stb])
            op(pe, lambda v: v.tensor_tensor(
                o3[:, :, 0:W], x3[:, :, 0:W],
                bc3(stb[:, 2 * ST:2 * ST + W], CK, W), OP.subtract),
                r=[xbuf, stb], w=[outbuf])
            op("vector", lambda v: v.tensor_tensor(
                o3[:, :, 0:W], o3[:, :, 0:W],
                bc3(stb[:, 4 * ST:4 * ST + W], CK, W), OP.mult),
                r=[stb], w=[outbuf])

        def proj3(psms, whb, wlb, rhb, rlb, W, RST, evac):
            for dch in range(CK):
                ps = psms[dch % 2]
                fns = []
                for kch in range(CK):
                    lh = whb[:, kch * D + dch * P:kch * D + (dch + 1) * P]
                    ll = wlb[:, kch * D + dch * P:kch * D + (dch + 1) * P]
                    rh = rhb[:, kch * RST:kch * RST + W]
                    rl = rlb[:, kch * RST:kch * RST + W]
                    st_ = (kch == 0)
                    sp_ = (kch == CK - 1)
                    fns.append(lambda t, lh=lh, rh=rh, st_=st_, ps=ps, W=W:
                               t.matmul(ps[:, 0:W], lh, rh, start=st_, stop=False))
                    fns.append(lambda t, lh=lh, rl=rl, ps=ps, W=W:
                               t.matmul(ps[:, 0:W], lh, rl, start=False, stop=False))
                    fns.append(lambda t, ll=ll, rh=rh, sp_=sp_, ps=ps, W=W:
                               t.matmul(ps[:, 0:W], ll, rh, start=False, stop=sp_))
                op("tensor", fns, r=[whb, wlb, rhb, rlb], w=[ps])
                evac(dch, ps)

        def split_hi_lo(srcb, dh, dl, W, ST):
            op("scalar", lambda s: s.copy(
                c3(dh, ST)[:, :, 0:W], c3(srcb, ST)[:, :, 0:W]),
                r=[srcb], w=[dh])
            op("vector", lambda v: v.tensor_tensor(
                c3(dl, ST)[:, :, 0:W], c3(srcb, ST)[:, :, 0:W],
                c3(dh, ST)[:, :, 0:W], OP.subtract), r=[srcb, dh], w=[dl])

        def tloads(items, srcs, bufs, split=True):
            fns = [(lambda s, b=b, co=co, w=w, src=src, r0=r0, r1=r1, ca=ca, cb=cb:
                    s.dma_start(b[:, co:co + w], src.t[r0:r1, ca:cb],
                                transpose=True))
                   for (b, co, w, src, r0, r1, ca, cb) in items]
            op("sync", fns, r=srcs, w=bufs, kind="lw")

        def _ph_p0():
         with ExitStack() as p0:
            wt = [sbt(p0, "p0_w%d" % i, (P, DF)) for i in range(8)]
            wh = [sbt(p0, "p0_h%d" % i, (P, DF), BF16) for i in range(4)]
            wl = [sbt(p0, "p0_l%d" % i, (P, DF), BF16) for i in range(4)]
            acc = sbt(p0, "p0_a", (P, 34))
            prow = sbt(p0, "p0_row", (1, 16))
            ps0 = pst(p0, "ps0", 512)

            lq = [0]

            def ldgroup(w, c, it0, n):
                wts = [wt[(it0 + j) % 8] for j in range(n)]
                eng, kq = (("sync", "ld"), ("scalar", "l2"))[lq[0] % 2]
                lq[0] += 1
                op(eng, [(lambda s, it=it0 + j, c=c, w=w, b_=wts[j]:
                          s.dma_start(b_[:, 0:c],
                                      wfull[w][it * P:(it + 1) * P, :]))
                         for j in range(n)], w=wts, kind=kq)
                return wts

            for w in ("wk", "wv", "wr", "wo", "fr", "fk", "fv"):
                rr = DF if w == "fk" else D
                c = DF if w == "fv" else D
                for it0 in range(0, rr // P, 4):
                    wts = ldgroup(w, c, it0, 4)
                    for j in range(4):
                        it = it0 + j
                        wtx = wts[j]
                        whb, wlb = wh[it % 4], wl[it % 4]
                        op("scalar", lambda s, c=c, wtx=wtx, whb=whb: s.copy(
                            whb[:, 0:c], wtx[:, 0:c]), r=[wtx], w=[whb])
                        op("vector", lambda v, c=c, wtx=wtx, whb=whb, wlb=wlb:
                           v.tensor_tensor(wlb[:, 0:c], wtx[:, 0:c],
                                           whb[:, 0:c], OP.subtract),
                           r=[wtx, whb], w=[wlb])
                        op("gpsimd",
                           [(lambda g, w=w, it=it, c=c, whb=whb: g.dma_start(
                               gsc[w + "h"].t[it * P:(it + 1) * P, :],
                               whb[:, 0:c])),
                            (lambda g, w=w, it=it, c=c, wlb=wlb: g.dma_start(
                                gsc[w + "l"].t[it * P:(it + 1) * P, :],
                                wlb[:, 0:c]))],
                           r=[whb, wlb], w=[gsc[w + "h"], gsc[w + "l"]],
                           kind="st")

            for it0 in range(0, DF // P, 4):
                wts = ldgroup("w1", D, it0, 4)
                op("vector", [(lambda v, it=it0 + j, wtx=wts[j]: v.reduce_sum(
                    acc[:, it:it + 1], wtx[:, 0:D], axis=AX,
                    apply_absolute_value=True)) for j in range(4)],
                   r=wts, w=[acc])
            op("vector", lambda v: v.reduce_sum(acc[:, 32:33], acc[:, 0:32],
                                                axis=AX), r=[acc], w=[acc])
            for it0 in range(0, D // P, 4):
                wts = ldgroup("w2", DF, it0, 4)
                op("vector", [(lambda v, it=it0 + j, wtx=wts[j]: v.reduce_sum(
                    acc[:, it:it + 1], wtx[:, 0:DF], axis=AX,
                    apply_absolute_value=True)) for j in range(4)],
                   r=wts, w=[acc])
            op("vector", lambda v: v.reduce_sum(acc[:, 33:34], acc[:, 0:8],
                                                axis=AX), r=[acc], w=[acc])
            op("tensor", [
                lambda t: t.matmul(ps0[0:1, 0:1], acc[:, 32:33], onc.t[:],
                                   start=True, stop=True),
                lambda t: t.matmul(ps0[0:1, 1:2], acc[:, 33:34], onc.t[:],
                                   start=True, stop=True)],
                r=[acc, onc], w=[ps0])
            op("vector", lambda v: v.tensor_copy(prow[0:1, 8:10], ps0[0:1, 0:2]),
               r=[ps0], w=[prow])
            op("vector", lambda v: v.tensor_scalar(
                prow[0:1, 10:12], prow[0:1, 8:10], 1.0 / (DF * D), 1e-5,
                OP.mult, OP.max), w=[prow])
            op("vector", lambda v: v.reciprocal(prow[0:1, 12:14], prow[0:1, 10:12]),
               w=[prow])
            op("tensor", [(lambda t, i=i, off=off: t.matmul(
                ps0[:, 8 + i:9 + i], onr.t[:, :], prow[0:1, off:off + 1],
                start=True, stop=True)) for i, off in enumerate((12, 13, 10, 11))],
                r=[onr, prow], w=[ps0])
            op("vector", lambda v: v.tensor_copy(swc.t[:], ps0[:, 8:12]),
               r=[ps0], w=[swc])

            def quant_full(w, gout, col):
                rr = DF if w == "w1" else D
                c = D if w == "w1" else DF
                for it0 in range(0, rr // P, 4):
                    wts = ldgroup(w, c, it0, 4)
                    for j in range(4):
                        it = it0 + j
                        wtx = wts[j]
                        whb, wlb = wh[it % 4], wl[it % 4]
                        op("vector", lambda v, c=c, wtx=wtx, wlb=wlb:
                           v.tensor_scalar(wlb[:, 0:c], wtx[:, 0:c],
                                           swc[:, col:col + 1], 0.5,
                                           OP.mult, OP.is_gt),
                           r=[wtx, swc], w=[wlb])
                        op("vector", lambda v, c=c, wtx=wtx, whb=whb:
                           v.tensor_scalar(whb[:, 0:c], wtx[:, 0:c],
                                           swc[:, col:col + 1], -0.5,
                                           OP.mult, OP.is_lt),
                           r=[wtx, swc], w=[whb])
                        op("vector", lambda v, c=c, whb=whb, wlb=wlb:
                           v.tensor_tensor(wlb[:, 0:c], wlb[:, 0:c],
                                           whb[:, 0:c], OP.subtract),
                           r=[whb], w=[wlb])
                        op("gpsimd", lambda g, it=it, c=c, gout=gout, wlb=wlb:
                           g.dma_start(gout.t[it * P:(it + 1) * P, :],
                                       wlb[:, 0:c]),
                           r=[wlb], w=[gout], kind="st")
            quant_full("w1", g_w1q, 0)
            quant_full("w2", g_w2q, 1)
        _ph_p0()
        pb.barrier()

        def _ph_a():
         with ExitStack() as pa:
            wkh = sbt(pa, "wkh", (P, CK * D), BF16)
            wkl = sbt(pa, "wkl", (P, CK * D), BF16)
            wvh = sbt(pa, "wvh", (P, CK * D), BF16)
            wvl = sbt(pa, "wvl", (P, CK * D), BF16)
            tloads([(buf, cc * D, D, g_, 0, D, cc * P, (cc + 1) * P)
                    for buf, g_ in ((wkh, gsc["wkh"]), (wkl, gsc["wkl"]),
                                    (wvh, gsc["wvh"]), (wvl, gsc["wvl"]))
                    for cc in range(CK)],
                   [gsc["wkh"], gsc["wkl"], gsc["wvh"], gsc["wvl"]],
                   [wkh, wkl, wvh, wvl])
            pstat = pst(pa, "A_pst", 1024)
            dtile = sbt(pa, "A_dt", (P, CK * WA))
            build_dtile(dtile)
            xa = sbt(pa, "A_x", (P, CK * STA))
            sq = sbt(pa, "A_sq", (P, CK * STA))
            n1 = sbt(pa, "A_n1", (P, CK * STA))
            hh = sbt(pa, "A_h", (P, CK * STA))
            stb = sbt(pa, "A_stb", (P, 5 * STA))
            wkt = sbt(pa, "A_wk", (P, CK * STW))
            zt = sbt(pa, "A_z", (P, CK * STW))
            scA = sbt(pa, "A_scA", (P, CK * STW))
            scB = sbt(pa, "A_scB", (P, CK * STW))
            mxf = sbt(pa, "A_mx", (P, CK * STW))
            bhk = [sbt(pa, "A_bhk0", (P, CK * STW), BF16),
                   sbt(pa, "A_bhk1", (P, CK * STW), BF16)]
            blk_ = [sbt(pa, "A_blk0", (P, CK * STW), BF16),
                    sbt(pa, "A_blk1", (P, CK * STW), BF16)]
            bhv = [sbt(pa, "A_bhv0", (P, CK * STW), BF16),
                   sbt(pa, "A_bhv1", (P, CK * STW), BF16)]
            blv = [sbt(pa, "A_blv0", (P, CK * STW), BF16),
                   sbt(pa, "A_blv1", (P, CK * STW), BF16)]
            bhr = sbt(pa, "A_bhr", (P, CK * STW), BF16)
            blr = sbt(pa, "A_blr", (P, CK * STW), BF16)
            pmA = [pst(pa, "pmA0", 512), pst(pa, "pmA1", 512)]

            def stage1a(si):
                c0, W = subs[si]
                W1 = W + 1
                op("sync", lambda s, c0=c0, W1=W1: s.dma_start(
                    schv(xa, W1), dchv(xT, c0, W1)), w=[xa], kind="ld")
                op("scalar", lambda s, W1=W1: s.activation(
                    c3(sq, STA)[:, :, 0:W1], c3(xa, STA)[:, :, 0:W1],
                    AF.Square), r=[xa], w=[sq])

            def stage1b(si):
                c0, W = subs[si]
                W1 = W + 1
                p = si % 2
                fns = []
                for srcb, off in ((xa, 0), (sq, 512)):
                    for ck in range(CK):
                        fns.append(lambda t, srcb=srcb, off=off, ck=ck, W1=W1:
                                   t.matmul(pstat[:, off:off + W1], on2.t[:],
                                            srcb[:, ck * STA:ck * STA + W1],
                                            start=(ck == 0), stop=(ck == CK - 1)))
                op("tensor", fns, r=[on2, xa, sq], w=[pstat])
                op("vector", lambda v, W1=W1: v.tensor_scalar_mul(
                    stb[:, 2 * STA:2 * STA + W1], pstat[:, 0:W1], 1.0 / D),
                    r=[pstat], w=[stb])
                op("vector", lambda v, W1=W1: v.tensor_tensor(
                    stb[:, 3 * STA:3 * STA + W1], stb[:, 2 * STA:2 * STA + W1],
                    stb[:, 2 * STA:2 * STA + W1], OP.mult), w=[stb])
                op("vector", lambda v, W1=W1: v.scalar_tensor_tensor(
                    stb[:, 4 * STA:4 * STA + W1], pstat[:, 512:512 + W1],
                    1.0 / D, stb[:, 3 * STA:3 * STA + W1], OP.mult,
                    OP.subtract), r=[pstat], w=[stb])
                op("scalar", lambda s, W1=W1: s.activation(
                    stb[:, 3 * STA:3 * STA + W1], stb[:, 4 * STA:4 * STA + W1],
                    AF.Sqrt, bias=EPS), r=[met], w=[stb])
                op("vector", lambda v, W1=W1: v.reciprocal(
                    stb[:, 4 * STA:4 * STA + W1], stb[:, 3 * STA:3 * STA + W1]),
                    w=[stb])
                op("vector", lambda v, W1=W1: v.tensor_tensor(
                    c3(n1, STA)[:, :, 0:W1], c3(xa, STA)[:, :, 0:W1],
                    bc3(stb[:, 2 * STA:2 * STA + W1], CK, W1), OP.subtract),
                    r=[xa, stb], w=[n1])
                op("vector", lambda v, W1=W1: v.tensor_tensor(
                    c3(n1, STA)[:, :, 0:W1], c3(n1, STA)[:, :, 0:W1],
                    bc3(stb[:, 4 * STA:4 * STA + W1], CK, W1), OP.mult),
                    r=[stb], w=[n1])
                op("gpsimd", lambda s, c0=c0, W=W: s.dma_start(
                    dchv(sp["n1"].t, c0, W),
                    n1.t[:].rearrange("p (ck t) -> p ck t", ck=CK)[:, :, 1:W + 1]),
                    r=[n1], w=[sp["n1"]], kind="st")
                op("vector", lambda v, W1=W1: v.tensor_tensor(
                    stb[:, 3 * STA:3 * STA + W1], stb[:, 4 * STA:4 * STA + W1],
                    stb[:, 4 * STA:4 * STA + W1], OP.mult), w=[stb])
                op("vector", lambda v, W1=W1: v.tensor_scalar(
                    stb[:, 3 * STA:3 * STA + W1], stb[:, 3 * STA:3 * STA + W1],
                    -EPS, 1.0 + EPS, OP.mult, OP.add), w=[stb])
                op("scalar", lambda s, W1=W1: s.activation(
                    stb[:, 3 * STA:3 * STA + W1], stb[:, 3 * STA:3 * STA + W1],
                    AF.Sqrt), w=[stb])
                op("vector", lambda v, W1=W1: v.reciprocal(
                    stb[:, 4 * STA:4 * STA + W1], stb[:, 3 * STA:3 * STA + W1]),
                    w=[stb])
                op("vector", lambda v, W1=W1: v.tensor_tensor(
                    c3(hh, STA)[:, :, 0:W1], c3(n1, STA)[:, :, 0:W1],
                    bc3(stb[:, 4 * STA:4 * STA + W1], CK, W1), OP.mult),
                    r=[n1, stb], w=[hh])
                if si == 0:
                    op("vector", lambda v: v.tensor_scalar_mul(
                        c3(hh, STA)[:, :, 1:1 + HB], c3(hh, STA)[:, :, 1:1 + HB],
                        met[:, 0:1]), r=[met], w=[hh])
                op("vector", lambda v, W1=W1: v.tensor_tensor(
                    c3(sq, STA)[:, :, 1:W1], c3(hh, STA)[:, :, 1:W1],
                    c3(hh, STA)[:, :, 0:W1 - 1], OP.subtract), r=[hh], w=[sq])

                def mix(tmoff, oh, ol):
                    tmb = prm[:, tmoff:tmoff + 8].unsqueeze(2).broadcast_to(
                        [P, CK, W])
                    op("vector", lambda v, tmb=tmb, W=W: v.tensor_tensor(
                        c3(mxf, STW)[:, :, 0:W], c3(sq, STA)[:, :, 1:1 + W],
                        tmb, OP.mult), r=[sq, prm], w=[mxf])
                    op("vector", lambda v, W=W: v.tensor_tensor(
                        c3(mxf, STW)[:, :, 0:W], c3(mxf, STW)[:, :, 0:W],
                        c3(hh, STA)[:, :, 0:W], OP.add), r=[hh], w=[mxf])
                    split_hi_lo(mxf, oh, ol, W, STW)

                mix(0, bhk[p], blk_[p])
                mix(8, bhv[p], blv[p])
                mix(16, bhr, blr)
                op("gpsimd", [
                    lambda s, c0=c0, W=W: s.dma_start(
                        dchv(spb["xrh"].t, c0, W), schv(bhr, W)),
                    lambda s, c0=c0, W=W: s.dma_start(
                        dchv(spb["xrl"].t, c0, W), schv(blr, W))],
                    r=[bhr, blr], w=[spb["xrh"], spb["xrl"]], kind="st")

            def stage2a(si):
                c0, W = subs[si]
                p = si % 2
                proj3(pmA, wkh, wkl, bhk[p], blk_[p], W, STW,
                      lambda dch, ps: op("scalar", lambda s, dch=dch, ps=ps, W=W:
                                         s.activation(
                                             wkt[:, dch * STW:dch * STW + W],
                                             ps[:, 0:W], AF.Exp),
                                         r=[ps], w=[wkt]))

            def stage2b(si):
                c0, W = subs[si]
                p = si % 2
                proj3(pmA, wvh, wvl, bhv[p], blv[p], W, STW,
                      lambda dch, ps: op("vector", lambda v, dch=dch, ps=ps, W=W:
                                         v.tensor_tensor(
                                             zt[:, dch * STW:dch * STW + W],
                                             ps[:, 0:W],
                                             wkt[:, dch * STW:dch * STW + W],
                                             OP.mult),
                                         r=[ps, wkt], w=[zt]))
                op("gpsimd", [
                    lambda s, c0=c0, W=W: s.dma_start(
                        dchv(sp["wk"].t, c0, W), schv(wkt, W)),
                    lambda s, c0=c0, W=W: s.dma_start(
                        dchv(sp["z"].t, c0, W), schv(zt, W))],
                    r=[wkt, zt], w=[sp["wk"], sp["z"]], kind="st")
                lo, hi = max(c0, OWN0), min(c0 + W, OWN1)
                if lo < hi:
                    o0, w2 = lo - c0, hi - lo
                    op("vector",
                       [(lambda v, ck=ck, o0=o0, w2=w2: v.tensor_tensor_scan(
                           scA[:, ck * STW:ck * STW + w2],
                           dtile[:, ck * WA:ck * WA + w2],
                           zt[:, ck * STW + o0:ck * STW + o0 + w2],
                           stA[:, ck:ck + 1], OP.mult, OP.add))
                        for ck in range(CK)] +
                       [(lambda v, ck=ck, o0=o0, w2=w2: v.tensor_tensor_scan(
                           scB[:, ck * STW:ck * STW + w2],
                           dtile[:, ck * WA:ck * WA + w2],
                           wkt[:, ck * STW + o0:ck * STW + o0 + w2],
                           stB[:, ck:ck + 1], OP.mult, OP.add))
                        for ck in range(CK)],
                       r=[dtile, zt, wkt, stA, stB], w=[scA, scB])
                    op("vector",
                       [(lambda v, w2=w2: v.tensor_copy(
                           col3(stA), c3(scA, STW)[:, :, w2 - 1:w2])),
                        (lambda v, w2=w2: v.tensor_copy(
                            col3(stB), c3(scB, STW)[:, :, w2 - 1:w2]))],
                       r=[scA, scB], w=[stA, stB])
                    caps = []
                    for owncol, gcol in ((OWN1 - 1, 0), (OWN1 - 9, 16)):
                        if lo <= owncol < hi:
                            oc = owncol - lo
                            caps.append(lambda v, oc=oc, gcol=gcol: v.tensor_copy(
                                gbuf.t[:, gcol:gcol + 8].unsqueeze(2),
                                c3(scA, STW)[:, :, oc:oc + 1]))
                            caps.append(lambda v, oc=oc, gcol=gcol: v.tensor_copy(
                                gbuf.t[:, gcol + 8:gcol + 16].unsqueeze(2),
                                c3(scB, STW)[:, :, oc:oc + 1]))
                    if caps:
                        op("vector", caps, r=[scA, scB], w=[gbuf])

            NS = len(subs)
            stage1a(0)
            stage1b(0)
            for si in range(NS):
                stage2a(si)
                if si + 1 < NS:
                    stage1a(si + 1)
                stage2b(si)
                if si + 1 < NS:
                    stage1b(si + 1)
        _ph_a()
        pb.barrier()

        def _ph_exch():
         with ExitStack() as pe:
            psx = pst(pe, "psx", 512)
            rows = sbt(pe, "rows", (1, 6 * D))
            t8 = sbt(pe, "t8", (8, D))
            t8b = sbt(pe, "t8b", (8, D))
            e8 = sbt(pe, "e8", (8, D))
            gtA = sbt(pe, "gtA", (8, D))
            gtB = sbt(pe, "gtB", (8, D))
            op("sync", lambda s: s.dma_start(t8.t[:], td8_in), w=[t8], kind="ld")
            op("scalar", lambda s: s.activation(t8b.t[:], t8.t[:], AF.Exp),
               r=[t8], w=[t8b])
            op("vector", lambda v: v.tensor_scalar_mul(t8.t[:], t8b.t[:], -1.0),
               r=[t8b], w=[t8])
            for gcol, roff in ((0, 0), (8, 1), (16, 2), (24, 3)):
                for ck in range(CK):
                    op("tensor", lambda t, gcol=gcol, ck=ck: t.matmul(
                        psx[0:1, 0:P], gbuf[:, gcol + ck:gcol + ck + 1], idn.t[:],
                        start=True, stop=True), r=[gbuf, idn], w=[psx])
                    op("vector", lambda v, roff=roff, ck=ck: v.tensor_copy(
                        rows[0:1, roff * D + ck * P:roff * D + (ck + 1) * P],
                        psx[0:1, 0:P]), r=[psx], w=[rows])
            op("gpsimd", lambda s: s.dma_start(ci_r1.t, rows[0:1, 0:4 * D]),
               r=[rows], w=[ci_r1], kind="st")
            op("gpsimd", lambda g: g.collective_compute(
                "AllGather", OP.bypass, [RG], [ci_r1.t], [g_r1.t]),
                r=[ci_r1], w=[g_r1])
            op("sync", [lambda s: s.dma_start(gtA.t[:], g_r1.t[0::4, :]),
                        lambda s: s.dma_start(gtB.t[:], g_r1.t[1::4, :])],
               r=[g_r1], w=[gtA, gtB], kind="ld")
            op("scalar", lambda s: s.activation(e8.t[:], t8.t[:], AF.Exp,
                                                scale=met[0:8, 3:4]),
               r=[t8, met], w=[e8])
            op("vector", lambda v: v.tensor_scalar_mul(e8.t[:], e8.t[:],
                                                       met[0:8, 4:5]),
               r=[met], w=[e8])
            for tile, off in ((gtA, 4 * D), (gtB, 5 * D)):
                op("vector", lambda v, tile=tile: v.tensor_tensor(
                    tile.t[:], tile.t[:], e8.t[:], OP.mult), r=[e8], w=[tile])
                for half in range(2):
                    op("tensor", lambda t, tile=tile, half=half: t.matmul(
                        psx[0:1, 0:512], onc[0:8, :],
                        tile[:, half * 512:(half + 1) * 512],
                        start=True, stop=True), r=[onc, tile], w=[psx])
                    op("vector", lambda v, off=off, half=half: v.tensor_copy(
                        rows[0:1, off + half * 512:off + (half + 1) * 512],
                        psx[0:1, 0:512]), r=[psx], w=[rows])
            op("scalar", lambda s: s.activation(rows[0:1, 0:D], t8[0:1, :], AF.Exp,
                                                scale=float(n_own - 8)),
               r=[t8, met], w=[rows])
            op("vector", [
                lambda v: v.tensor_tensor(rows[0:1, 4 * D:5 * D],
                                          rows[0:1, 4 * D:5 * D],
                                          rows[0:1, 0:D], OP.mult),
                lambda v: v.tensor_tensor(rows[0:1, 5 * D:6 * D],
                                          rows[0:1, 5 * D:6 * D],
                                          rows[0:1, 0:D], OP.mult)],
                w=[rows])
            op("vector", [
                lambda v: v.tensor_tensor(rows[0:1, 4 * D:5 * D],
                                          rows[0:1, 4 * D:5 * D],
                                          rows[0:1, 2 * D:3 * D], OP.add),
                lambda v: v.tensor_tensor(rows[0:1, 5 * D:6 * D],
                                          rows[0:1, 5 * D:6 * D],
                                          rows[0:1, 3 * D:4 * D], OP.add)],
                w=[rows])
            op("gpsimd", lambda s: s.dma_start(ci_r2.t, rows[0:1, 4 * D:6 * D]),
               r=[rows], w=[ci_r2], kind="st")
            op("gpsimd", lambda g: g.collective_compute(
                "AllGather", OP.bypass, [RG], [ci_r2.t], [g_r2.t]),
                r=[ci_r2], w=[g_r2])
            op("sync", [lambda s: s.dma_start(gtA.t[:], g_r2.t[0::2, :]),
                        lambda s: s.dma_start(gtB.t[:], g_r2.t[1::2, :])],
               r=[g_r2], w=[gtA, gtB], kind="ld")
            for tile, off in ((gtA, 4 * D), (gtB, 5 * D)):
                op("vector", lambda v, tile=tile: v.tensor_scalar_mul(
                    tile.t[:], tile.t[:], met[0:8, 5:6]), r=[met], w=[tile])
                for half in range(2):
                    op("tensor", lambda t, tile=tile, half=half: t.matmul(
                        psx[0:1, 0:512], onc[0:8, :],
                        tile[:, half * 512:(half + 1) * 512],
                        start=True, stop=True), r=[onc, tile], w=[psx])
                    op("vector", lambda v, off=off, half=half: v.tensor_copy(
                        rows[0:1, off + half * 512:off + (half + 1) * 512],
                        psx[0:1, 0:512]), r=[psx], w=[rows])
            for off, colbuf in ((4 * D, iA), (5 * D, iB)):
                for ck in range(CK):
                    op("tensor", lambda t, off=off, ck=ck: t.matmul(
                        psx[:, 0:1], rows[0:1, off + ck * P:off + (ck + 1) * P],
                        onr[0:1, 0:1], start=True, stop=True),
                        r=[rows, onr], w=[psx])
                    op("vector", lambda v, colbuf=colbuf, ck=ck: v.tensor_copy(
                        colbuf.t[:, ck:ck + 1], psx[:, 0:1]), r=[psx], w=[colbuf])
        _ph_exch()
        pb.barrier()

        def _ph_s1():
         with ExitStack() as s1:
            woh = sbt(s1, "woh", (P, CK * D), BF16)
            wol = sbt(s1, "wol", (P, CK * D), BF16)
            wrh = sbt(s1, "wrh", (P, CK * D), BF16)
            wrl = sbt(s1, "wrl", (P, CK * D), BF16)
            tloads([(buf, cc * D, D, g_, 0, D, cc * P, (cc + 1) * P)
                    for buf, g_ in ((wrh, gsc["wrh"]), (wrl, gsc["wrl"]),
                                    (woh, gsc["woh"]), (wol, gsc["wol"]))
                    for cc in range(CK)],
                   [gsc["wrh"], gsc["wrl"], gsc["woh"], gsc["wol"]],
                   [wrh, wrl, woh, wol])
            pstat = pst(s1, "S_pst", 1024)
            dtile = sbt(s1, "S_dt", (P, CK * WA))
            build_dtile(dtile)
            wkt = sbt(s1, "B_wk", (P, CK * STW))
            zt = sbt(s1, "B_z", (P, CK * STW))
            n1t = sbt(s1, "B_n1", (P, CK * STW))
            xrh = sbt(s1, "B_xrh", (P, CK * STW), BF16)
            xrl = sbt(s1, "B_xrl", (P, CK * STW), BF16)
            scA = sbt(s1, "B_scA", (P, CK * STS))
            scB = sbt(s1, "B_scB", (P, CK * STS))
            numt = sbt(s1, "B_num", (P, CK * STW))
            dent = sbt(s1, "B_den", (P, CK * STW))
            srt = sbt(s1, "B_sr", (P, CK * STW))
            y1t = sbt(s1, "B_y1", (P, CK * STW))
            gt = sbt(s1, "B_g", (P, CK * STW))
            sq = sbt(s1, "B_sq", (P, CK * STW))
            stb = sbt(s1, "B_stb", (P, 5 * STW))
            gprev = sbt(s1, "B_gp", (P, CK))
            bh = sbt(s1, "B_bh", (P, CK * STW), BF16)
            bl = sbt(s1, "B_bl", (P, CK * STW), BF16)
            pmB = [pst(s1, "pmB0", 512), pst(s1, "pmB1", 512)]
            pmC = [pst(s1, "pmC0", 512), pst(s1, "pmC1", 512)]
            op("vector", lambda v: v.memset(gprev.t[:], 0.0), w=[gprev])

            def loads_a(si):
                c0, W = subs[si]
                op("sync", [(lambda s, b=b, spn=spn, c0=c0, W=W: s.dma_start(
                    schv(b, W), dchv(spn.t, c0, W)))
                    for b, spn in ((xrh, spb["xrh"]), (xrl, spb["xrl"]))],
                    r=[spb["xrh"], spb["xrl"]], w=[xrh, xrl], kind="ld")

            def loads_b(si):
                c0, W = subs[si]
                op("sync", [(lambda s, b=b, spn=spn, c0=c0, W=W: s.dma_start(
                    schv(b, W), dchv(spn.t, c0, W)))
                    for b, spn in ((wkt, sp["wk"]), (zt, sp["z"]),
                                   (n1t, sp["n1"]))],
                    r=[sp["wk"], sp["z"], sp["n1"]], w=[wkt, zt, n1t], kind="ld")

            def scanchain(si):
                c0, W = subs[si]
                if si == 0:
                    op("vector", [
                        lambda v: v.tensor_scalar_mul(
                            c3(zt, STW)[:, :, 0:HB], c3(zt, STW)[:, :, 0:HB],
                            met[:, 0:1]),
                        lambda v: v.tensor_scalar_mul(
                            c3(wkt, STW)[:, :, 0:HB], c3(wkt, STW)[:, :, 0:HB],
                            met[:, 0:1])], r=[met], w=[zt, wkt])
                    op("vector", [
                        lambda v: v.tensor_copy(c3(scA, STS)[:, :, 0:1], col3(iA)),
                        lambda v: v.tensor_copy(c3(scB, STS)[:, :, 0:1], col3(iB))],
                       r=[iA, iB], w=[scA, scB])
                op("vector",
                   [(lambda v, ck=ck, W=W: v.tensor_tensor_scan(
                       scA[:, ck * STS + 1:ck * STS + 1 + W],
                       dtile[:, ck * WA:ck * WA + W],
                       zt[:, ck * STW:ck * STW + W],
                       scA[:, ck * STS:ck * STS + 1], OP.mult, OP.add))
                    for ck in range(CK)] +
                   [(lambda v, ck=ck, W=W: v.tensor_tensor_scan(
                       scB[:, ck * STS + 1:ck * STS + 1 + W],
                       dtile[:, ck * WA:ck * WA + W],
                       wkt[:, ck * STW:ck * STW + W],
                       scB[:, ck * STS:ck * STS + 1], OP.mult, OP.add))
                    for ck in range(CK)],
                   r=[dtile, zt, wkt], w=[scA, scB])
                eub = eu_pm.t[:].unsqueeze(2).broadcast_to([P, CK, W])
                op("vector", lambda v, W=W, eub=eub: v.tensor_tensor(
                    c3(numt, STW)[:, :, 0:W], c3(zt, STW)[:, :, 0:W], eub,
                    OP.mult), r=[zt, eu_pm], w=[numt])
                op("vector", lambda v, W=W: v.tensor_tensor(
                    c3(numt, STW)[:, :, 0:W], c3(numt, STW)[:, :, 0:W],
                    c3(scA, STS)[:, :, 0:W], OP.add), r=[scA], w=[numt])
                op("vector", lambda v, W=W, eub=eub: v.tensor_tensor(
                    c3(dent, STW)[:, :, 0:W], c3(wkt, STW)[:, :, 0:W], eub,
                    OP.mult), r=[wkt, eu_pm], w=[dent])
                op("vector", lambda v, W=W: v.tensor_tensor(
                    c3(dent, STW)[:, :, 0:W], c3(dent, STW)[:, :, 0:W],
                    c3(scB, STS)[:, :, 0:W], OP.add), r=[scB], w=[dent])
                if si == 0:
                    op("vector", lambda v: v.tensor_scalar_add(
                        c3(dent, STW)[:, :, 0:HB], c3(dent, STW)[:, :, 0:HB],
                        met[:, 2:3]), r=[met], w=[dent])
                op("vector", lambda v, W=W: v.reciprocal(
                    c3(sq, STW)[:, :, 0:W], c3(dent, STW)[:, :, 0:W]),
                    r=[dent], w=[sq])
                op("vector", lambda v, W=W: v.tensor_tensor(
                    c3(numt, STW)[:, :, 0:W], c3(numt, STW)[:, :, 0:W],
                    c3(sq, STW)[:, :, 0:W], OP.mult), r=[sq], w=[numt])
                op("vector", [
                    (lambda v, W=W: v.tensor_copy(
                        c3(scA, STS)[:, :, 0:1], c3(scA, STS)[:, :, W:W + 1])),
                    (lambda v, W=W: v.tensor_copy(
                        c3(scB, STS)[:, :, 0:1], c3(scB, STS)[:, :, W:W + 1]))],
                   w=[scA, scB])

            def wrproj(si):
                c0, W = subs[si]
                proj3(pmB, wrh, wrl, xrh, xrl, W, STW,
                      lambda dch, ps: op("scalar", lambda s, dch=dch, ps=ps, W=W:
                                         s.activation(
                                             srt[:, dch * STW:dch * STW + W],
                                             ps[:, 0:W], AF.Sigmoid),
                                         r=[ps], w=[srt]))

            def rwkv_split(si):
                c0, W = subs[si]
                op("vector", lambda v, W=W: v.tensor_tensor(
                    c3(srt, STW)[:, :, 0:W], c3(srt, STW)[:, :, 0:W],
                    c3(numt, STW)[:, :, 0:W], OP.mult), r=[numt], w=[srt])
                split_hi_lo(srt, bh, bl, W, STW)

            def woproj(si):
                c0, W = subs[si]
                proj3(pmC, woh, wol, bh, bl, W, STW,
                      lambda dch, ps: op("vector", lambda v, dch=dch, ps=ps, W=W:
                                         v.tensor_tensor(
                                             y1t[:, dch * STW:dch * STW + W],
                                             ps[:, 0:W],
                                             n1t[:, dch * STW:dch * STW + W],
                                             OP.add),
                                         r=[ps, n1t], w=[y1t]))
                op("gpsimd", lambda s, c0=c0, W=W: s.dma_start(
                    dchv(sp["y1"].t, c0, W), schv(y1t, W)),
                    r=[y1t], w=[sp["y1"]], kind="st")

            def rest(si):
                c0, W = subs[si]
                ln_na(y1t, gt, W, STW, sq, None, stb, pstat, mm_stats=True)
                if si == 0:
                    op("vector", lambda v: v.tensor_scalar_mul(
                        c3(gt, STW)[:, :, HB - 1:HB], c3(gt, STW)[:, :, HB - 1:HB],
                        met[:, 0:1]), r=[met], w=[gt])
                op("vector", [
                    (lambda v, W=W: v.tensor_tensor(
                        c3(sq, STW)[:, :, 1:W], c3(gt, STW)[:, :, 1:W],
                        c3(gt, STW)[:, :, 0:W - 1], OP.subtract)),
                    (lambda v: v.tensor_tensor(
                        c3(sq, STW)[:, :, 0:1], c3(gt, STW)[:, :, 0:1],
                        col3(gprev), OP.subtract))],
                   r=[gt, gprev], w=[sq])

                def mix2(tmoff, sph, spl):
                    tmb = prm[:, tmoff:tmoff + 8].unsqueeze(2).broadcast_to(
                        [P, CK, W])
                    op("vector", lambda v, tmb=tmb, W=W: v.tensor_tensor(
                        c3(numt, STW)[:, :, 0:W], c3(sq, STW)[:, :, 0:W], tmb,
                        OP.mult), r=[sq, prm], w=[numt])
                    op("vector", [
                        (lambda v, W=W: v.tensor_tensor(
                            c3(numt, STW)[:, :, 1:W], c3(numt, STW)[:, :, 1:W],
                            c3(gt, STW)[:, :, 0:W - 1], OP.add)),
                        (lambda v: v.tensor_tensor(
                            c3(numt, STW)[:, :, 0:1], c3(numt, STW)[:, :, 0:1],
                            col3(gprev), OP.add))],
                       r=[gt, gprev], w=[numt])
                    split_hi_lo(numt, bh, bl, W, STW)
                    op("gpsimd", [
                        (lambda s, c0=c0, W=W, sph=sph: s.dma_start(
                            dchv(sph.t, c0, W), schv(bh, W))),
                        (lambda s, c0=c0, W=W, spl=spl: s.dma_start(
                            dchv(spl.t, c0, W), schv(bl, W)))],
                        r=[bh, bl], w=[sph, spl], kind="st")

                mix2(24, spb["gkh"], spb["gkl"])
                mix2(32, spb["grh"], spb["grl"])
                op("vector", lambda v, W=W: v.tensor_copy(
                    col3(gprev), c3(gt, STW)[:, :, W - 1:W]), r=[gt], w=[gprev])

            NS = len(subs)
            loads_a(0)
            loads_b(0)
            wrproj(0)
            for si in range(NS):
                scanchain(si)
                if si + 1 < NS:
                    loads_a(si + 1)
                rwkv_split(si)
                if si + 1 < NS:
                    wrproj(si + 1)
                woproj(si)
                if si + 1 < NS:
                    loads_b(si + 1)
                rest(si)
        _ph_s1()
        pb.barrier()

        def _ph_k():
         with ExitStack() as sk:
            gkh_r = sbt(sk, "K_gkh", (P, CK * RNG), BF16)
            gkl_r = sbt(sk, "K_gkl", (P, CK * RNG), BF16)
            op("sync", [lambda s: s.dma_start(schv(gkh_r, RNG),
                                              dchv(spb["gkh"].t, 0, RNG)),
                        lambda s: s.dma_start(schv(gkl_r, RNG),
                                              dchv(spb["gkl"].t, 0, RNG))],
               r=[spb["gkh"], spb["gkl"]], w=[gkh_r, gkl_r], kind="ld")
            HD2 = DF // 2
            fkh = sbt(sk, "K_fkh", (P, CK * HD2), BF16)
            fkl = sbt(sk, "K_fkl", (P, CK * HD2), BF16)
            tmp1 = [sbt(sk, "K_t1a", (P, 512)), sbt(sk, "K_t1b", (P, 512))]
            tmp2 = [sbt(sk, "K_t2a", (P, 512)), sbt(sk, "K_t2b", (P, 512))]
            kbh = [sbt(sk, "K_bha", (P, 512), BF16), sbt(sk, "K_bhb", (P, 512), BF16)]
            kbl = [sbt(sk, "K_bla", (P, 512), BF16), sbt(sk, "K_blb", (P, 512), BF16)]
            pmK = [pst(sk, "pmK0", 512), pst(sk, "pmK1", 512)]
    
            NQ = 4
            FLQ = (HD2 // P) // NQ
            fkhF = [Buf(fkh.t) for _ in range(NQ)]
            fklF = [Buf(fkl.t) for _ in range(NQ)]
            pi = 0
            for half in range(2):
                for q_ in range(NQ):
                    tloads([(buf, cc * HD2 + q_ * FLQ * P, FLQ * P, g_,
                             half * HD2 + q_ * FLQ * P,
                             half * HD2 + (q_ + 1) * FLQ * P,
                             cc * P, (cc + 1) * P)
                            for buf, g_ in ((fkh, gsc["fkh"]), (fkl, gsc["fkl"]))
                            for cc in range(CK)],
                           [gsc["fkh"], gsc["fkl"]], [fkhF[q_], fklF[q_]])
                for fl in range(HD2 // P):
                    f = half * (HD2 // P) + fl
                    for ti, (c0, W) in enumerate(tsubs(512, 0, RNG)):
                        bi = pi % 2
                        pi += 1
                        ps = pmK[bi]
                        fns = []
                        for kch in range(CK):
                            lh = fkh[:, kch * HD2 + fl * P:kch * HD2 + (fl + 1) * P]
                            ll = fkl[:, kch * HD2 + fl * P:kch * HD2 + (fl + 1) * P]
                            rh = gkh_r[:, kch * RNG + c0:kch * RNG + c0 + W]
                            rl = gkl_r[:, kch * RNG + c0:kch * RNG + c0 + W]
                            st_, sp_ = kch == 0, kch == CK - 1
                            fns.append(lambda t, lh=lh, rh=rh, st_=st_, ps=ps, W=W:
                                       t.matmul(ps[:, 0:W], lh, rh, start=st_,
                                                stop=False))
                            fns.append(lambda t, lh=lh, rl=rl, ps=ps, W=W:
                                       t.matmul(ps[:, 0:W], lh, rl, start=False,
                                                stop=False))
                            fns.append(lambda t, ll=ll, rh=rh, sp_=sp_, ps=ps, W=W:
                                       t.matmul(ps[:, 0:W], ll, rh, start=False,
                                                stop=sp_))
                        op("tensor", fns,
                           r=[fkhF[fl // FLQ], fklF[fl // FLQ], gkh_r, gkl_r],
                           w=[ps])
                        t1, t2 = tmp1[bi], tmp2[bi]
                        bh_, bl_ = kbh[bi], kbl[bi]
                        op("scalar", lambda s, ps=ps, t1=t1, W=W: s.activation(
                            t1[:, 0:W], ps[:, 0:W], AF.Relu), r=[ps], w=[t1])
                        op("scalar", lambda s, t1=t1, t2=t2, W=W: s.activation(
                            t2[:, 0:W], t1[:, 0:W], AF.Square), r=[t1], w=[t2])
                        op("scalar", lambda s, t2=t2, bh_=bh_, W=W: s.copy(
                            bh_[:, 0:W], t2[:, 0:W]), r=[t2], w=[bh_])
                        op("vector", lambda v, t2=t2, bh_=bh_, bl_=bl_, W=W:
                           v.tensor_tensor(bl_[:, 0:W], t2[:, 0:W], bh_[:, 0:W],
                                           OP.subtract), r=[t2, bh_], w=[bl_])
                        op("gpsimd", lambda s, f=f, c0=c0, bh_=bh_, W=W: s.dma_start(
                            sp_kfh.t[f * P:(f + 1) * P, c0:c0 + W], bh_[:, 0:W]),
                            r=[bh_], w=[sp_kfh], kind="st")
                        op("gpsimd", lambda s, f=f, c0=c0, bl_=bl_, W=W: s.dma_start(
                            sp_kfl.t[f * P:(f + 1) * P, c0:c0 + W], bl_[:, 0:W]),
                            r=[bl_], w=[sp_kfl], kind="st")
        _ph_k()
        pb.barrier()

        def _ph_v():
         with ExitStack() as sv:
            fvh = sbt(sv, "V_fvh", (P, FCK * D), BF16)
            fvl = sbt(sv, "V_fvl", (P, FCK * D), BF16)
            tloads([(buf, f * D, D, g_, 0, D, f * P, (f + 1) * P)
                    for buf, g_ in ((fvh, gsc["fvh"]), (fvl, gsc["fvl"]))
                    for f in range(FCK)],
                   [gsc["fvh"], gsc["fvl"]], [fvh, fvl])
            kfh_t = sbt(sv, "V_kfh", (P, FCK * 512), BF16)
            kfl_t = sbt(sv, "V_kfl", (P, FCK * 512), BF16)
            kfhR = [kfh_t, Buf(kfh_t.t)]
            kflR = [kfl_t, Buf(kfl_t.t)]
            ev = [sbt(sv, "V_ev0", (P, 512)), sbt(sv, "V_ev1", (P, 512))]
            pk = [pst(sv, "pkV0", 2048), pst(sv, "pkV1", 2048)]
            for si, (c0, W) in enumerate(tsubs(256, 0, RNG)):
                sl = si % 2
                o0 = sl * 256
                op("sync", [lambda s, c0=c0, W=W, o0=o0: s.dma_start(
                    kfh_t.t[:].rearrange("p (f t) -> p f t", f=FCK)[:, :, o0:o0 + W],
                    dchv(sp_kfh.t, c0, W, nck=FCK)),
                    lambda s, c0=c0, W=W, o0=o0: s.dma_start(
                    kfl_t.t[:].rearrange("p (f t) -> p f t", f=FCK)[:, :, o0:o0 + W],
                    dchv(sp_kfl.t, c0, W, nck=FCK))],
                   r=[sp_kfh, sp_kfl], w=[kfhR[sl], kflR[sl]], kind="ld")
                for grp in range(2):
                    pkg = pk[grp]
                    fns = []
                    for f in range(FCK):
                        for ci in range(4):
                            c = grp * 4 + ci
                            lh = fvh[:, f * D + c * P:f * D + (c + 1) * P]
                            ll = fvl[:, f * D + c * P:f * D + (c + 1) * P]
                            rh = kfh_t[:, f * 512 + o0:f * 512 + o0 + W]
                            rl = kfl_t[:, f * 512 + o0:f * 512 + o0 + W]
                            st_, sp_ = f == 0, f == FCK - 1
                            o = ci * 512
                            fns.append(lambda t, lh=lh, rh=rh, st_=st_, o=o,
                                       pkg=pkg, W=W:
                                       t.matmul(pkg[:, o:o + W], lh, rh,
                                                start=st_, stop=False))
                            fns.append(lambda t, lh=lh, rl=rl, o=o, pkg=pkg, W=W:
                                       t.matmul(pkg[:, o:o + W], lh, rl,
                                                start=False, stop=False))
                            fns.append(lambda t, ll=ll, rh=rh, sp_=sp_, o=o,
                                       pkg=pkg, W=W:
                                       t.matmul(pkg[:, o:o + W], ll, rh,
                                                start=False, stop=sp_))
                    op("tensor", fns, r=[fvh, fvl, kfhR[sl], kflR[sl]], w=[pkg])
                    for ci in range(4):
                        c = grp * 4 + ci
                        eb = ev[c % 2]
                        op("vector", lambda v, eb=eb, ci=ci, W=W, pkg=pkg:
                           v.tensor_copy(eb.t[:, 0:W], pkg[:, ci * 512:ci * 512 + W]),
                           r=[pkg], w=[eb])
                        op("gpsimd", lambda s, eb=eb, c=c, c0=c0, W=W: s.dma_start(
                            sp["kv"].t[c * P:(c + 1) * P, c0:c0 + W], eb.t[:, 0:W]),
                            r=[eb], w=[sp["kv"]], kind="st")
        _ph_v()
        pb.barrier()

        def _ph_frh():
         with ExitStack() as sf:
            n2q = sbt(sf, "n2q", (P, CK * RNG), BF16)
            r1row = sbt(sf, "r1row", (1, RNG))
            rrow = sbt(sf, "rrow", (1, 512))
            qcb = sbt(sf, "qcb", (P, 16))
            with ExitStack() as sfr:
                frh = sbt(sfr, "frh", (P, CK * D), BF16)
                frl = sbt(sfr, "frl", (P, CK * D), BF16)
                tloads([(buf, cc * D, D, g_, 0, D, cc * P, (cc + 1) * P)
                        for buf, g_ in ((frh, gsc["frh"]), (frl, gsc["frl"]))
                        for cc in range(CK)],
                       [gsc["frh"], gsc["frl"]], [frh, frl])
                pstat = pst(sfr, "R_pst", 1024)
                grh = sbt(sfr, "R_grh", (P, CK * STW), BF16)
                grl = sbt(sfr, "R_grl", (P, CK * STW), BF16)
                y1t = sbt(sfr, "R_y1", (P, CK * STW))
                kvt = sbt(sfr, "R_kv", (P, CK * STW))
                xt_ = sbt(sfr, "R_x", (P, CK * STW))
                sigt = sbt(sfr, "R_sig", (P, CK * STW))
                n2t = sbt(sfr, "R_n2", (P, CK * STW))
                sq = sbt(sfr, "R_sq", (P, CK * STW))
                stb = sbt(sfr, "R_stb", (P, 5 * STW))
                qi = sbt(sfr, "R_qi", (P, CK * STW), I32)
                q0 = sbt(sfr, "R_q0", (P, CK * STW))
                q1 = sbt(sfr, "R_q1", (P, CK * STW))
                pmF = [pst(sfr, "pmR0", 512), pst(sfr, "pmR1", 512)]
                for si, (c0, W) in enumerate(subs):
                    op("sync", [(lambda s, b=b, spn=spn, c0=c0, W=W: s.dma_start(
                        schv(b, W), dchv(spn.t, c0, W)))
                        for b, spn in ((y1t, sp["y1"]), (kvt, sp["kv"]),
                                       (grh, spb["grh"]), (grl, spb["grl"]))] +
                        [lambda s, c0=c0, W=W: s.dma_start(
                            schv(xt_, W), dchv(xT, c0 + 1, W))],
                        r=[sp["y1"], sp["kv"], spb["grh"], spb["grl"]],
                        w=[y1t, kvt, grh, grl, xt_], kind="ld")
                    proj3(pmF, frh, frl, grh, grl, W, STW,
                          lambda dch, ps: op("scalar",
                                             lambda s, dch=dch, ps=ps, W=W:
                                             s.activation(
                                                 sigt[:, dch * STW:dch * STW + W],
                                                 ps[:, 0:W], AF.Sigmoid),
                                             r=[ps], w=[sigt]))
                    op("vector", lambda v, W=W: v.tensor_tensor(
                        c3(kvt, STW)[:, :, 0:W], c3(sigt, STW)[:, :, 0:W],
                        c3(kvt, STW)[:, :, 0:W], OP.mult), r=[sigt], w=[kvt])
                    op("vector", lambda v, W=W: v.tensor_tensor(
                        c3(kvt, STW)[:, :, 0:W], c3(kvt, STW)[:, :, 0:W],
                        c3(y1t, STW)[:, :, 0:W], OP.add), r=[y1t], w=[kvt])
                    ln_na(kvt, n2t, W, STW, sq, None, stb, pstat,
                          mm_stats=True)
                    op("vector", lambda v, W=W: v.tensor_tensor(
                        c3(xt_, STW)[:, :, 0:W], c3(xt_, STW)[:, :, 0:W],
                        c3(n2t, STW)[:, :, 0:W], OP.add), r=[n2t], w=[xt_])
                    op("gpsimd", lambda s, c0=c0, W=W: s.dma_start(
                        dchv(sp["x1"].t, c0, W), schv(xt_, W)),
                        r=[xt_], w=[sp["x1"]], kind="st")
                    ln_na(xt_, n2t, W, STW, sq, None, stb, pstat,
                          mm_stats=True)
                    op("scalar", lambda s, W=W: s.activation(
                        c3(q1, STW)[:, :, 0:W], c3(n2t, STW)[:, :, 0:W], AF.Abs),
                        r=[n2t], w=[q1])
                    op("vector", lambda v, W=W: v.tensor_tensor(
                        c3(sq, STW)[:, 0:4, 0:W], c3(q1, STW)[:, 0:8:2, 0:W],
                        c3(q1, STW)[:, 1:8:2, 0:W], OP.max), r=[q1], w=[sq])
                    op("vector", lambda v, W=W: v.tensor_tensor(
                        c3(sq, STW)[:, 0:4:2, 0:W], c3(sq, STW)[:, 0:4:2, 0:W],
                        c3(sq, STW)[:, 1:4:2, 0:W], OP.max), w=[sq])
                    op("vector", lambda v, W=W: v.tensor_tensor(
                        sq[:, 0:W], sq[:, 0:W], sq[:, 2 * STW:2 * STW + W], OP.max),
                        w=[sq])
                    nK = (W + P - 1) // P
                    for k in range(nK):
                        wk_ = min(P, W - k * P)
                        op("tensor", lambda t, k=k, wk_=wk_: t.transpose(
                            pstat[0:wk_, 0:P], sq[:, k * P:k * P + wk_], idn.t[:]),
                            r=[sq, idn], w=[pstat])
                        op("vector", lambda v, k=k, wk_=wk_: v.reduce_max(
                            qcb.t[0:wk_, k:k + 1], pstat[0:wk_, 0:P], axis=AX),
                            r=[pstat], w=[qcb])
                    op("tensor", [(lambda t, k=k, wk_=min(P, W - k * P): t.matmul(
                        pstat[0:1, 512 + k * P:512 + k * P + wk_],
                        qcb[0:wk_, k:k + 1], idn[0:wk_, 0:wk_],
                        start=True, stop=True)) for k in range(nK)],
                        r=[qcb, idn], w=[pstat])
                    op("vector", lambda v, W=W: v.tensor_scalar_max(
                        rrow[0:1, 0:W], pstat[0:1, 512:512 + W], 1e-5),
                        r=[pstat], w=[rrow])
                    op("vector", lambda v, W=W: v.reciprocal(
                        rrow[0:1, 256:256 + W], rrow[0:1, 0:W]), w=[rrow])
                    op("vector", lambda v, c0=c0, W=W: v.tensor_scalar_mul(
                        r1row.t[0:1, c0:c0 + W], rrow[0:1, 256:256 + W], 127.0),
                        r=[rrow], w=[r1row])
                    op("tensor", lambda t, c0=c0, W=W: t.matmul(
                        pstat[:, 0:W], onr[0:1, 0:P], r1row[0:1, c0:c0 + W],
                        start=True, stop=True), r=[onr, r1row], w=[pstat])
                    op("vector", lambda v, W=W: v.tensor_tensor(
                        c3(q0, STW)[:, :, 0:W], c3(n2t, STW)[:, :, 0:W],
                        bc3(pstat[:, 0:W], CK, W), OP.mult),
                        r=[n2t, pstat], w=[q0])
                    op("vector", lambda v, W=W: v.tensor_copy(
                        c3(qi, STW)[:, :, 0:W], c3(q0, STW)[:, :, 0:W]),
                        r=[q0], w=[qi])
                    op("vector", lambda v, W=W: v.tensor_copy(
                        c3(q1, STW)[:, :, 0:W], c3(qi, STW)[:, :, 0:W]),
                        r=[qi], w=[q1])
                    op("vector", lambda v, W=W: v.tensor_tensor(
                        c3(sq, STW)[:, :, 0:W], c3(q0, STW)[:, :, 0:W],
                        c3(q1, STW)[:, :, 0:W], OP.subtract), r=[q0, q1], w=[sq])
                    op("vector", lambda v, W=W: v.scalar_tensor_tensor(
                        c3(q0, STW)[:, :, 0:W], c3(sq, STW)[:, :, 0:W], 0.5,
                        c3(q1, STW)[:, :, 0:W], OP.is_gt, OP.add),
                        r=[sq, q1], w=[q0])
                    op("vector", lambda v, W=W: v.tensor_scalar(
                        c3(q1, STW)[:, :, 0:W], c3(sq, STW)[:, :, 0:W],
                        -0.5, -1.0, OP.is_lt, OP.mult), r=[sq], w=[q1])
                    op("vector", lambda v, c0=c0, W=W: v.tensor_tensor(
                        c3(n2q, RNG)[:, :, c0:c0 + W], c3(q0, STW)[:, :, 0:W],
                        c3(q1, STW)[:, :, 0:W], OP.add), r=[q0, q1], w=[n2q])
            pb.barrier()

            with ExitStack() as sh:
                w1t = sbt(sh, "H_w1t", (P, CK * DF), BF16)
                tloads([(w1t, cc * DF, DF, g_w1q, 0, DF, cc * P, (cc + 1) * P)
                        for cc in range(CK)], [g_w1q], [w1t])
                pmH = [pst(sh, "pmH0", 512), pst(sh, "pmH1", 512)]
                rht = sbt(sh, "H_rh", (P, RNG))
                for c0_, W_ in tsubs(512, 0, RNG):
                    op("tensor", lambda t, c0_=c0_, W_=W_: t.matmul(
                        pmH[0][:, 0:W_], onr[0:1, 0:P], r1row[0:1, c0_:c0_ + W_],
                        start=True, stop=True), r=[onr, r1row], w=[pmH[0]])
                    op("vector", lambda v, c0_=c0_, W_=W_: v.reciprocal(
                        rht.t[:, c0_:c0_ + W_], pmH[0][:, 0:W_]),
                        r=[pmH[0]], w=[rht])
                op("vector", lambda v: v.tensor_scalar_mul(rht.t[:], rht.t[:],
                                                           swc[:, 2:3]),
                   r=[swc], w=[rht])
                hbuf = [sbt(sh, "H_h0", (P, RNG)), sbt(sh, "H_h1", (P, RNG))]
                cbuf = [sbt(sh, "H_c0", (P, NW)), sbt(sh, "H_c1", (P, NW))]
                hsb = [sbt(sh, "H_s0", (P, NW)), sbt(sh, "H_s1", (P, NW))]
                mrun = sbt(sh, "H_mr", (P, NW))
                op("vector", lambda v: v.memset(mrun.t[:], 0.0), w=[mrun])
                for f in range(FCK):
                    hb_, cb_, sb_ = hbuf[f % 2], cbuf[f % 2], hsb[f % 2]
                    for ti, (c0, W) in enumerate(tsubs(512, 0, RNG)):
                        ps = pmH[ti % 2]
                        op("tensor", [(lambda t, f=f, kch=kch, c0=c0, W=W, ps=ps:
                                       t.matmul(ps[:, 0:W],
                                                w1t[:, kch * DF + f * P:
                                                    kch * DF + (f + 1) * P],
                                                n2q[:, kch * RNG + c0:
                                                    kch * RNG + c0 + W],
                                                start=(kch == 0),
                                                stop=(kch == CK - 1)))
                                      for kch in range(CK)],
                           r=[w1t, n2q], w=[ps])
                        op("vector", lambda v, c0=c0, W=W, ps=ps, hb_=hb_:
                           v.tensor_tensor(hb_.t[:, c0:c0 + W], ps[:, 0:W],
                                           rht[:, c0:c0 + W], OP.mult),
                           r=[ps, rht], w=[hb_])
                    op("vector", [
                        lambda v, hb_=hb_: v.tensor_scalar_mul(
                            hb_.t[:, HB - 1:HB], hb_.t[:, HB - 1:HB], met[:, 0:1]),
                        lambda v, hb_=hb_: v.tensor_scalar_mul(
                            hb_.t[:, OWN1:OWN1 + 1], hb_.t[:, OWN1:OWN1 + 1],
                            met[:, 1:2])], r=[met], w=[hb_])
                    op("vector", lambda v, f=f, hb_=hb_, cb_=cb_: v.tensor_scalar(
                        cb_.t[:], hb_.t[:, HB:HB + NW],
                        prm[:, 56 + 3 * f + 1:56 + 3 * f + 2],
                        prm[:, 152 + f:152 + f + 1], OP.mult, OP.add),
                        r=[hb_, prm], w=[cb_])
                    op("vector", lambda v, f=f, hb_=hb_, cb_=cb_:
                       v.scalar_tensor_tensor(
                           cb_.t[:], hb_.t[:, HB - 1:HB - 1 + NW],
                           prm[:, 56 + 3 * f:56 + 3 * f + 1], cb_.t[:],
                           OP.mult, OP.add), r=[hb_, prm], w=[cb_])
                    op("vector", lambda v, f=f, hb_=hb_, cb_=cb_:
                       v.scalar_tensor_tensor(
                           cb_.t[:], hb_.t[:, HB + 1:HB + 1 + NW],
                           prm[:, 56 + 3 * f + 2:56 + 3 * f + 3], cb_.t[:],
                           OP.mult, OP.add), r=[hb_, prm], w=[cb_])
                    op("scalar", lambda s, cb_=cb_, sb_=sb_: s.activation(
                        sb_.t[:], cb_.t[:], AF.Silu), r=[cb_], w=[sb_])
                    op("scalar", lambda s, cb_=cb_, sb_=sb_: s.activation(
                        cb_.t[:], sb_.t[:], AF.Abs), r=[sb_], w=[cb_])
                    op("vector", lambda v, cb_=cb_: v.tensor_tensor(
                        mrun.t[:], mrun.t[:], cb_.t[:], OP.max), r=[cb_], w=[mrun])
                    op("gpsimd", lambda s, f=f, sb_=sb_: s.dma_start(
                        sp_hs.t[f * P:(f + 1) * P, :], sb_.t[:]),
                        r=[sb_], w=[sp_hs], kind="st")
                for k in range(NW // P):
                    pp = pmH[k % 2]
                    op("tensor", lambda t, k=k, pp=pp: t.transpose(
                        pp[0:P, 0:P], mrun[:, k * P:k * P + P], idn.t[:]),
                        r=[mrun, idn], w=[pp])
                    op("vector", lambda v, k=k, pp=pp: v.reduce_max(
                        fqc.t[:, k:k + 1], pp[0:P, 0:P], axis=AX), r=[pp], w=[fqc])
        _ph_frh()
        pb.barrier()

        def _ph_f():
         with ExitStack() as sF:
            w2t = sbt(sF, "F_w2t", (P, FCK * D), BF16)
            tloads([(w2t, f * D, D, g_w2q, 0, D, f * P, (f + 1) * P)
                    for f in range(FCK)], [g_w2q], [w2t])
            psL = pst(sF, "F_psL", 1024)
            psR = pst(sF, "F_psR", 512)
            pkF = pst(sF, "F_pkF", 2048)
            rrow = sbt(sF, "F_rr", (1, 512))
            rbc = [sbt(sF, "F_rb0", (P, 512)), sbt(sF, "F_rb1", (P, 512))]
            hst = sbt(sF, "F_hs", (P, FCK * STF))
            hq2 = [sbt(sF, "F_hq0", (P, FCK * STF), BF16),
                   sbt(sF, "F_hq1", (P, FCK * STF), BF16)]
            x1t = sbt(sF, "F_x1", (P, CK * STF))
            qa = sbt(sF, "F_qa", (P, CK * STF))
            qb = sbt(sF, "F_qb", (P, CK * STF))
            sq = sbt(sF, "F_sq", (P, CK * STF))
            qa2 = sbt(sF, "F_qa2", (P, CK * STF))
            qb2 = sbt(sF, "F_qb2", (P, CK * STF))
            stb = sbt(sF, "F_stb", (P, 5 * STF))
            tiles = tsubs(STF, OWN0, OWN1)

            def emit_loads(ti):
                c0, W = tiles[ti]
                ow = c0 - OWN0
                op("sync", lambda s, ow=ow, W=W: s.dma_start(
                    hst.t[:].rearrange("p (f t) -> p f t", f=FCK)[:, :, 0:W],
                    dchv(sp_hs.t, ow, W, nck=FCK)),
                    r=[sp_hs], w=[hst], kind="ld")

            def emit_x1load(ti):
                c0, W = tiles[ti]
                op("sync", lambda s, c0=c0, W=W: s.dma_start(
                    schv(x1t, W), dchv(sp["x1"].t, c0, W)),
                    r=[sp["x1"]], w=[x1t], kind="ld")

            def emit_rows(ti):
                pi = ti % 2
                W = tiles[ti][1]
                for j in range((W + P - 1) // P):
                    op("tensor", lambda t, j=j, ti=ti: t.matmul(
                        psR[0:1, 0:P], fqc[:, 2 * ti + j:2 * ti + j + 1],
                        idn.t[:], start=True, stop=True), r=[fqc, idn], w=[psR])
                    op("vector", lambda v, j=j: v.tensor_copy(
                        rrow[0:1, j * P:(j + 1) * P], psR[0:1, 0:P]),
                        r=[psR], w=[rrow])
                op("vector", lambda v, W=W: v.tensor_scalar_max(
                    rrow[0:1, 0:W], rrow[0:1, 0:W], 1e-5), w=[rrow])
                op("vector", lambda v, W=W: v.reciprocal(
                    rrow[0:1, 256:256 + W], rrow[0:1, 0:W]), w=[rrow])
                op("vector", lambda v, W=W: v.tensor_scalar_mul(
                    rrow[0:1, 0:W], rrow[0:1, 256:256 + W], 127.0), w=[rrow])
                op("vector", lambda v, W=W: v.reciprocal(
                    rrow[0:1, 256:256 + W], rrow[0:1, 0:W]), w=[rrow])
                op("vector", lambda v, W=W: v.tensor_scalar_mul(
                    rrow[0:1, 256:256 + W], rrow[0:1, 256:256 + W],
                    swc[0:1, 3:4]), r=[swc], w=[rrow])
                op("tensor", lambda t, W=W: t.matmul(
                    psR[:, 0:W], onr[0:1, 0:P], rrow[0:1, 0:W],
                    start=True, stop=True), r=[onr, rrow], w=[psR])
                op("vector", lambda v, pi=pi, W=W: v.tensor_copy(
                    rbc[pi].t[:, 0:W], psR[:, 0:W]), r=[psR], w=[rbc[pi]])
                op("tensor", lambda t, W=W: t.matmul(
                    psR[:, 256:256 + W], onr[0:1, 0:P], rrow[0:1, 256:256 + W],
                    start=True, stop=True), r=[onr, rrow], w=[psR])
                op("vector", lambda v, pi=pi, W=W: v.tensor_copy(
                    rbc[pi].t[:, 256:256 + W], psR[:, 256:256 + W]),
                    r=[psR], w=[rbc[pi]])

            def emit_quant(ti, grps):
                pi = ti % 2
                W = tiles[ti][1]
                hq = hq2[pi]
                sqi3 = sq.t[:].bitcast(I32).rearrange("p (ck t) -> p ck t", t=STF)
                for grp in grps:
                    go = grp * 2
                    op("vector", lambda v, grp=grp, W=W: v.tensor_tensor(
                        c3(qa, STF)[:, :, 0:W],
                        c3(hst, STF, n=8, o=grp * 8)[:, :, 0:W],
                        bc3(rbc[pi][:, 0:W], CK, W), OP.mult),
                        r=[hst, rbc[pi]], w=[qa])
                    op("vector", lambda v, W=W, sqi3=sqi3: v.tensor_copy(
                        sqi3[:, :, 0:W], c3(qa, STF)[:, :, 0:W]), r=[qa], w=[sq])
                    op("vector", lambda v, W=W, sqi3=sqi3: v.tensor_copy(
                        c3(qb, STF)[:, :, 0:W], sqi3[:, :, 0:W]), r=[sq], w=[qb])
                    op("vector", lambda v, W=W: v.tensor_tensor(
                        c3(sq, STF)[:, :, 0:W], c3(qa, STF)[:, :, 0:W],
                        c3(qb, STF)[:, :, 0:W], OP.subtract), r=[qa, qb], w=[sq])
                    op("vector", lambda v, W=W: v.scalar_tensor_tensor(
                        c3(qa, STF)[:, :, 0:W], c3(sq, STF)[:, :, 0:W], 0.5,
                        c3(qb, STF)[:, :, 0:W], OP.is_gt, OP.add),
                        r=[sq, qb], w=[qa])
                    op("vector", lambda v, W=W: v.tensor_scalar(
                        c3(qb, STF)[:, :, 0:W], c3(sq, STF)[:, :, 0:W],
                        -0.5, -1.0, OP.is_lt, OP.mult), r=[sq], w=[qb])
                    op("vector", lambda v, grp=grp, W=W, hq=hq: v.tensor_tensor(
                        c3(hq, STF, n=8, o=grp * 8)[:, :, 0:W],
                        c3(qa, STF)[:, :, 0:W], c3(qb, STF)[:, :, 0:W],
                        OP.add), r=[qa, qb], w=[hq])

            def emit_mmgrp(ti, grp):
                pi = ti % 2
                W = tiles[ti][1]
                hq = hq2[pi]
                fns = []
                for f in range(FCK):
                    for ci in range(4):
                        c = grp * 4 + ci
                        lh = w2t[:, f * D + c * P:f * D + (c + 1) * P]
                        rh = hq[:, f * STF:f * STF + W]
                        fns.append(lambda t, lh=lh, rh=rh, st_=(f == 0),
                                   sp_=(f == FCK - 1), o=ci * 512, W=W:
                                   t.matmul(pkF[:, o:o + W], lh, rh,
                                            start=st_, stop=sp_))
                op("tensor", fns, r=[w2t, hq], w=[pkF])

            def emit_evac(ti, grp):
                pi = ti % 2
                W = tiles[ti][1]
                op("vector", lambda v, grp=grp, pi=pi, W=W: v.tensor_tensor(
                    c3(qa2, STF, n=4, o=grp * 4)[:, :, 0:W],
                    pkF.t[:].rearrange("p (ck t) -> p ck t", t=512)[:, :, 0:W],
                    bc3(rbc[pi][:, 256:256 + W], 4, W), OP.mult),
                   r=[pkF, rbc[pi]], w=[qa2])

            def emit_tail(ti):
                c0, W = tiles[ti]
                ow = c0 - OWN0
                ln_na(qa2, qb2, W, STF, qb2, None, stb, psL, mm_stats=True)
                emit_x1load(ti)
                op("vector", lambda v, W=W: v.tensor_tensor(
                    c3(qb2, STF)[:, :, 0:W], c3(qb2, STF)[:, :, 0:W],
                    c3(x1t, STF)[:, :, 0:W], OP.add), r=[x1t], w=[qb2])
                op("gpsimd", lambda s, ow=ow, W=W: s.dma_start(
                    dchv(outT.t, ow, W), schv(qb2, W)),
                    r=[qb2], w=[outT], kind="st")

            NT = len(tiles)
            emit_loads(0)
            emit_rows(0)
            emit_quant(0, range(4))
            for ti in range(NT):
                nxt = ti + 1 < NT
                emit_mmgrp(ti, 0)
                if nxt:
                    emit_loads(ti + 1)
                    emit_rows(ti + 1)
                    emit_quant(ti + 1, (0, 1))
                emit_evac(ti, 0)
                emit_mmgrp(ti, 1)
                if nxt:
                    emit_quant(ti + 1, (2, 3))
                emit_evac(ti, 1)
                emit_tail(ti)
        _ph_f()

        pb.fence("sync", "st")
        pb.fence("sync", "ld")
        pb.recs["sync"].append(["inst", lambda s: s.nop(), None, None, False, False])

        E = dict(sync=nc.sync, scalar=nc.scalar, vector=nc.vector,
                 tensor=nc.tensor, gpsimd=nc.gpsimd)
        blk = ctx.enter_context(nc.Block())
        pb.run(blk, E, sem_h)
    return nc


def host_inputs(inputs, n_own):
    x = np.asarray(inputs["L0_attn_x"], dtype=np.float32)
    B, T, _ = x.shape
    RNG = HB + n_own + HF
    NX = RNG + 1

    def col(vec):
        return np.ascontiguousarray(np.asarray(vec, np.float32).reshape(CK, P).T)

    pcols = np.zeros((P, 184), np.float32)
    pcols[:, 0:8] = col(inputs["L0_attn_tm_k"])
    pcols[:, 8:16] = col(inputs["L0_attn_tm_v"])
    pcols[:, 16:24] = col(inputs["L0_attn_tm_r"])
    pcols[:, 24:32] = col(inputs["L0_attn_fm_k"])
    pcols[:, 32:40] = col(inputs["L0_attn_fm_r"])
    pcols[:, 40:48] = col(inputs["L0_attn_time_decay"])
    pcols[:, 48:56] = col(inputs["L0_attn_time_first"])
    cw = np.asarray(inputs["L0_ffn_conv_w"], np.float32)[:, 0, :]
    pcols[:, 56:152] = cw.reshape(FCK, P, 3).transpose(1, 0, 2).reshape(P, 96)
    pcols[:, 152:184] = np.asarray(inputs["L0_ffn_conv_b"], np.float32).reshape(FCK, P).T

    import ml_dtypes
    ident = np.eye(P, dtype=np.float32)
    identb = np.eye(P, dtype=ml_dtypes.bfloat16)
    td = np.asarray(inputs["L0_attn_time_decay"], np.float32)
    td8 = np.ascontiguousarray(np.tile(td[None, :], (8, 1)))
    Wm = {nm: np.ascontiguousarray(np.asarray(inputs[key], np.float32))
          for nm, key in (("wk", "L0_attn_Wk"), ("wv", "L0_attn_Wv"),
                          ("wr", "L0_attn_Wr"), ("wo", "L0_attn_Wo"),
                          ("fr", "L0_attn_Fr"), ("fk", "L0_attn_Fk"),
                          ("fv", "L0_attn_Fv"), ("w1", "L0_ffn_w1"),
                          ("w2", "L0_ffn_w2"))}

    in_maps = []
    for c in range(NCORE):
        b, q = divmod(c, 4)
        t0 = q * n_own
        lo = t0 - 9
        xs = np.zeros((NX, D), np.float32)
        s0, s1_ = max(lo, 0), min(lo + NX, T)
        xs[s0 - lo:s1_ - lo] = x[b, s0:s1_]
        meta = np.zeros((P, 8), np.float32)
        meta[:, 0] = 0.0 if q == 0 else 1.0
        meta[:, 1] = 0.0 if q == 3 else 1.0
        meta[:, 2] = 1.0 - meta[0, 0]
        meta[:, 6] = EPS
        meta[:, 7] = float(n_own - 8)
        for j in range(8):
            jb = j // 4
            if jb == b and j < c and (c - j) <= 3:
                meta[j, 3] = float(n_own * (c - 1 - j))
                meta[j, 4] = 1.0
            if jb == b and j == c - 1:
                meta[j, 5] = 1.0
        m = dict(xT=np.ascontiguousarray(xs.T), params_pm=pcols, meta=meta,
                 ident=ident, identb=identb,
                 onescol=np.ones((P, 1), np.float32),
                 onesrow=np.ones((1, P), np.float32),
                 ones2d=np.ones((P, P), np.float32), td8=td8)
        for nm, arr in Wm.items():
            m["w_" + nm] = arr
        in_maps.append(m)
    return in_maps


def kernel(**inputs):
    x = np.asarray(inputs["L0_attn_x"])
    B, T, _ = x.shape
    n_own = B * T // NCORE
    nc = bass.Bass("TRN2", target_bir_lowering=False)
    build(nc, n_own)
    in_maps = host_inputs(inputs, n_own)
    res = run_bass_kernel_spmd(nc, in_maps, core_ids=list(range(NCORE)))
    out = np.empty((B, T, D), np.float32)
    for c in range(NCORE):
        b, q = divmod(c, 4)
        out[b, q * n_own:(q + 1) * n_own, :] = np.asarray(res.results[c]["outT"]).T
    return out


# version 30
# speedup vs baseline: 1.0237x; 1.0099x over previous
import numpy as np
from contextlib import ExitStack

import concourse.bass as bass
import concourse.mybir as mybir
from concourse.bass_utils import run_bass_kernel_spmd

F32 = mybir.dt.float32
BF16 = mybir.dt.bfloat16
I32 = mybir.dt.int32
AF = mybir.ActivationFunctionType
OP = mybir.AluOpType
AX = mybir.AxisListType.X

P = 128
CK = 8
D = 1024
DF = 4096
FCK = 32
EPS = 1e-5
NCORE = 8
HB = 8
HF = 8

ENGS = ("sync", "scalar", "vector", "tensor", "gpsimd")
_ESEM = {"scalar": "s", "vector": "v", "tensor": "t", "gpsimd": "g"}
_SEMS = ("ld", "st", "lw", "l2", "v", "s", "t", "g")


class Buf:
    __slots__ = ("t", "w", "rs")

    def __init__(self, t=None):
        self.t = t
        self.w = None
        self.rs = {}

    def __getitem__(self, sl):
        return self.t[sl]


class PB:
    def __init__(self):
        self.recs = {e: [] for e in ENGS}
        self.cnt = {n: 0 for n in _SEMS}
        self.marks = {e: {} for e in ENGS}
        self.since_drain = {e: 0 for e in ENGS}
        self.lastrec = {e: None for e in ENGS}
        self.bardeps = None
        self.barseen = set()

    def dep(self, name):
        return (name, self.cnt[name])

    def _want(self, eng, d):
        if d is None:
            return
        n, v = d
        if v <= 0:
            return
        if self.marks[eng].get(n, 0) >= v:
            return
        self.marks[eng][n] = v
        self.recs[eng].append(("wait", n, v))

    def op(self, eng, fns, r=(), w=(), deps=(), kind="c"):
        if not isinstance(fns, (list, tuple)):
            fns = [fns]
        dd = list(deps)
        if self.bardeps is not None and eng not in self.barseen:
            self.barseen.add(eng)
            dd.extend(self.bardeps)
        dma = kind != "c"
        sem = kind if dma else _ESEM[eng]
        for b in r:
            dd.append(b.w)
        for b in w:
            if b.rs:
                dd.extend((k, val) for k, val in b.rs.items())
            else:
                dd.append(b.w)
        for d in dd:
            self._want(eng, d)
        last = None
        for fn in fns:
            if dma:
                self.cnt[sem] += 16
                last = ["inst", fn, sem, self.cnt[sem], True, True]
            else:
                last = ["inst", fn, sem, None, False, False]
            self.recs[eng].append(last)
            self.since_drain[eng] += 1
        if not dma:
            self.cnt[sem] += 1
            last[3] = self.cnt[sem]
            last[4] = True
        self.lastrec[eng] = last
        t = (sem, self.cnt[sem])
        for b in r:
            b.rs[sem] = max(b.rs.get(sem, 0), t[1])
        for b in w:
            b.w = t
            b.rs = {}
        return t

    def tagnow(self, eng):
        sem = _ESEM[eng]
        rec = self.lastrec[eng]
        if rec is not None and not rec[4]:
            self.cnt[sem] += 1
            rec[3] = self.cnt[sem]
            rec[4] = True
        return (sem, self.cnt[sem])

    def barrier(self):
        for e in ("scalar", "vector", "tensor", "gpsimd"):
            if self.lastrec[e] is not None:
                self.tagnow(e)
        self.bardeps = [(n, self.cnt[n]) for n in _SEMS if self.cnt[n] > 0]
        self.barseen = set()

    def fence(self, eng, sem):
        v = self.cnt[sem]
        if v <= 0:
            return
        self.recs[eng].append(("wait", sem, v))
        self.marks[eng][sem] = max(self.marks[eng].get(sem, 0), v)

    def run(self, blk, engines, sem_h):
        maxw = {}
        waited = {n: set() for n in ("ld", "st", "lw", "l2")}
        for e in ENGS:
            for rec in self.recs[e]:
                if rec[0] == "wait":
                    maxw[rec[1]] = max(maxw.get(rec[1], 0), rec[2])
                    if rec[1] in waited:
                        waited[rec[1]].add(rec[2])
        for e in ENGS:
            recs = self.recs[e]
            if not any(rec[0] == "inst" and rec[5] for rec in recs):
                continue
            out = []
            fenced = {}
            for rec in recs:
                if rec[0] == "inst" and rec[5]:
                    s_, v_ = rec[2], rec[3]
                    prev = v_ - 16
                    if (prev > fenced.get(s_, 0) and prev in waited[s_]):
                        out.append(("wait", s_, prev))
                        fenced[s_] = prev
                elif rec[0] == "wait" and rec[1] in waited:
                    fenced[rec[1]] = max(fenced.get(rec[1], 0), rec[2])
                out.append(rec)
            self.recs[e] = out
        for ename in ENGS:
            recs = self.recs[ename]
            if not recs:
                continue

            def body(E, recs=recs):
                for rec in recs:
                    if rec[0] == "wait":
                        E.wait_ge(sem_h[rec[1]], rec[2])
                    elif rec[0] == "drain":
                        E.drain()
                    else:
                        _, fn, sem, val, has, dma = rec
                        inst = fn(E)
                        if has and (dma or val <= maxw.get(sem, 0)):
                            inst.then_inc(sem_h[sem], 16 if dma else 1)
            getattr(blk, ename)(body)


def build(nc, n_own):
    RNG = HB + n_own + HF
    WA = (RNG - 16) // 8
    assert 8 * WA + 16 == RNG and WA % 8 == 0 and WA >= 16
    subs = [(s * WA, WA) for s in range(8)] + [(8 * WA, 16)]
    NX = RNG + 1
    OWN0, OWN1 = HB, HB + n_own
    NW = n_own
    STA = WA + 1
    STW = WA
    STS = WA + 1
    STF = 256
    assert NW % STF == 0 or NW < STF

    def tsubs(width, lo, hi):
        out, c = [], lo
        while c < hi:
            w = min(width, hi - c)
            out.append((c, w))
            c += w
        return out

    def ein(name, shape, dt=F32):
        return nc.dram_tensor(name, list(shape), dt, kind="ExternalInput").ap()

    def idr(name, shape, dt):
        return Buf(nc.dram_tensor(name, list(shape), dt, kind="Internal").ap())

    xT = ein("xT", (D, NX))
    prm_in = ein("params_pm", (P, 184))
    meta_in = ein("meta", (P, 8))
    ident_in = ein("ident", (P, P))
    identb_in = ein("identb", (P, P), BF16)
    onescol_in = ein("onescol", (P, 1))
    onesrow_in = ein("onesrow", (1, P))
    ones2d_in = ein("ones2d", (P, P))
    td8_in = ein("td8", (8, D))
    wfull = {w: ein("w_" + w, (DF if w in ("fk", "w1") else D,
                               DF if w in ("fv", "w2") else D))
             for w in ("wk", "wv", "wr", "wo", "fr", "fk", "fv", "w1", "w2")}
    outT = Buf(nc.dram_tensor("outT", [D, n_own], F32, kind="ExternalOutput").ap())

    gsc = {}
    for w in ("wk", "wv", "wr", "wo", "fr", "fk", "fv"):
        r = DF if w == "fk" else D
        c = DF if w == "fv" else D
        for hl in ("h", "l"):
            gsc[w + hl] = idr(f"g_{w}{hl}", (r, c), BF16)
    g_w1q = idr("g_w1q", (DF, D), BF16)
    g_w2q = idr("g_w2q", (D, DF), BF16)
    ci_r1 = idr("ci_r1", (4, D), F32)
    g_r1 = idr("g_r1", (4 * NCORE, D), F32)
    ci_r2 = idr("ci_r2", (2, D), F32)
    g_r2 = idr("g_r2", (2 * NCORE, D), F32)
    sp = {n: idr("sp_" + n, (D, RNG), F32) for n in ("n1", "wk", "z", "y1", "kv", "x1")}
    spb = {n: idr("sp_" + n, (D, RNG), BF16)
           for n in ("xrh", "xrl", "gkh", "gkl", "grh", "grl")}
    sp_kfh = idr("sp_kfh", (DF, RNG), BF16)
    sp_kfl = idr("sp_kfl", (DF, RNG), BF16)
    sp_hs = idr("sp_hs", (DF, NW), F32)

    def dchv(ap_, c0, w, nck=CK):
        return ap_.rearrange("(ck p) t -> p ck t", p=P)[:, :, c0:c0 + w]

    def schv(buf, w, nck=CK):
        return buf.t[:].rearrange("p (ck t) -> p ck t", ck=nck)[:, :, 0:w]

    def c3(buf, st_, n=CK, o=0):
        v = buf.t[:].rearrange("p (ck t) -> p ck t", t=st_)
        return v if (n is None and o == 0) else v[:, o:o + n, :]

    def bc3(row_ap, n, w_):
        return row_ap.unsqueeze(1).broadcast_to([P, n, w_])

    def col3(buf2d, n=CK):
        return buf2d.t[:].unsqueeze(2)

    pb = PB()
    op, dep = pb.op, pb.dep
    RG = list(range(NCORE))

    with ExitStack() as ctx:
        sem_h = {n: ctx.enter_context(nc.semaphore("sem_" + n)) for n in _SEMS}

        def sbt(st, name, shape, dt=F32):
            return Buf(st.enter_context(nc.sbuf_tensor(name, list(shape), dt)))

        def pst(st, name, cols):
            return Buf(st.enter_context(nc.psum_tensor(name, [P, cols], F32)))

        prm = sbt(ctx, "prm", (P, 184))
        met = sbt(ctx, "met", (P, 8))
        idn = sbt(ctx, "idn", (P, P))
        idb = sbt(ctx, "idb", (P, P), BF16)
        onc = sbt(ctx, "onc", (P, 1))
        onr = sbt(ctx, "onr", (1, P))
        on2 = sbt(ctx, "on2", (P, P))
        pm8 = sbt(ctx, "pm8", (P, 8))
        d_pm = sbt(ctx, "d_pm", (P, 8))
        w_pm = sbt(ctx, "w_pm", (P, 8))
        eu_pm = sbt(ctx, "eu_pm", (P, 8))
        gbuf = sbt(ctx, "gbuf", (P, 32))
        iA = sbt(ctx, "iA", (P, 8))
        iB = sbt(ctx, "iB", (P, 8))
        stA = sbt(ctx, "stA", (P, 8))
        stB = sbt(ctx, "stB", (P, 8))
        swc = sbt(ctx, "swc", (P, 4))
        fqc = sbt(ctx, "fqc", (P, 16))

        nc.const_aps.register(F32, EPS, met[:, 6:7])
        nc.const_aps.register(F32, float(n_own - 8), met[:, 7:8])
        op("sync", [(lambda s, d=dst, sr=src: s.dma_start(d.t[:], sr))
                    for dst, src in ((prm, prm_in), (met, meta_in), (idn, ident_in),
                                     (idb, identb_in), (onc, onescol_in),
                                     (onr, onesrow_in), (on2, ones2d_in))],
           w=[prm, met, idn, idb, onc, onr, on2], kind="ld")
        op("vector", [(lambda v, b=b: v.memset(b.t[:], 0.0))
                      for b in (stA, stB, gbuf, iA, iB)],
           w=[stA, stB, gbuf, iA, iB])
        op("scalar", lambda s: s.activation(pm8.t[:], prm[:, 40:48], AF.Exp),
           r=[prm], w=[pm8])
        op("vector", lambda v: v.tensor_scalar_mul(w_pm.t[:], pm8.t[:], -1.0),
           r=[pm8], w=[w_pm])
        op("scalar", lambda s: s.activation(d_pm.t[:], w_pm.t[:], AF.Exp),
           r=[w_pm], w=[d_pm])
        op("scalar", lambda s: s.activation(eu_pm.t[:], prm[:, 48:56], AF.Exp),
           r=[prm], w=[eu_pm])

        def build_dtile(dt_buf):
            op("vector", lambda v: v.memset(dt_buf.t[:], 0.0), w=[dt_buf])
            op("scalar", [(lambda s, ck=ck: s.activation(
                dt_buf[:, ck * WA:(ck + 1) * WA], dt_buf[:, ck * WA:(ck + 1) * WA],
                AF.Identity, bias=d_pm[:, ck:ck + 1], scale=0.0))
                for ck in range(CK)], r=[d_pm], w=[dt_buf])

        def ln_na(xbuf, outbuf, W, ST, sqb, trb, stb, pstat, pe="vector",
                  mm_stats=False):
            x3, s3 = c3(xbuf, ST), c3(sqb, ST)
            o3 = c3(outbuf, ST)
            op("scalar", lambda s: s.activation(
                s3[:, :, 0:W], x3[:, :, 0:W], AF.Square), r=[xbuf], w=[sqb])

            if mm_stats:
                fns = []
                for src, off in ((xbuf, 0), (sqb, 512)):
                    for ck in range(CK):
                        fns.append(lambda t, src=src, off=off, ck=ck: t.matmul(
                            pstat[:, off:off + W], on2.t[:],
                            src[:, ck * ST:ck * ST + W],
                            start=(ck == 0), stop=(ck == CK - 1)))
                op("tensor", fns, r=[on2, xbuf, sqb], w=[pstat])
            else:
                t3 = c3(trb, ST, n=4)

                def tree(srcb, src3, doff):
                    op(pe, lambda v: v.tensor_tensor(
                        t3[:, 0:4, 0:W], src3[:, 0:8:2, 0:W], src3[:, 1:8:2, 0:W],
                        OP.add), r=[srcb], w=[trb])
                    op(pe, lambda v: v.tensor_tensor(
                        t3[:, 0:4:2, 0:W], t3[:, 0:4:2, 0:W], t3[:, 1:4:2, 0:W],
                        OP.add), w=[trb])
                    op(pe, lambda v, doff=doff: v.tensor_tensor(
                        stb[:, doff * ST:doff * ST + W], trb[:, 0:W],
                        trb[:, 2 * ST:2 * ST + W], OP.add), r=[trb], w=[stb])
                tree(xbuf, x3, 0)
                tree(sqb, s3, 1)
                op("tensor", [
                    lambda t: t.matmul(pstat[:, 0:W], on2.t[:], stb[:, 0:W],
                                       start=True, stop=True),
                    lambda t: t.matmul(pstat[:, 512:512 + W], on2.t[:],
                                       stb[:, ST:ST + W], start=True, stop=True)],
                    r=[on2, stb], w=[pstat])
            op("vector", lambda v: v.tensor_scalar_mul(
                stb[:, 2 * ST:2 * ST + W], pstat[:, 0:W], 1.0 / D),
                r=[pstat], w=[stb])
            op("vector", lambda v: v.tensor_tensor(
                stb[:, 3 * ST:3 * ST + W], stb[:, 2 * ST:2 * ST + W],
                stb[:, 2 * ST:2 * ST + W], OP.mult), w=[stb])
            op("vector", lambda v: v.scalar_tensor_tensor(
                stb[:, 4 * ST:4 * ST + W], pstat[:, 512:512 + W], 1.0 / D,
                stb[:, 3 * ST:3 * ST + W], OP.mult, OP.subtract),
                r=[pstat], w=[stb])
            op("scalar", lambda s: s.activation(
                stb[:, 3 * ST:3 * ST + W], stb[:, 4 * ST:4 * ST + W], AF.Sqrt,
                bias=EPS), r=[met], w=[stb])
            op("vector", lambda v: v.reciprocal(
                stb[:, 4 * ST:4 * ST + W], stb[:, 3 * ST:3 * ST + W]), w=[stb])
            op(pe, lambda v: v.tensor_tensor(
                o3[:, :, 0:W], x3[:, :, 0:W],
                bc3(stb[:, 2 * ST:2 * ST + W], CK, W), OP.subtract),
                r=[xbuf, stb], w=[outbuf])
            op("vector", lambda v: v.tensor_tensor(
                o3[:, :, 0:W], o3[:, :, 0:W],
                bc3(stb[:, 4 * ST:4 * ST + W], CK, W), OP.mult),
                r=[stb], w=[outbuf])

        def proj3(psms, whb, wlb, rhb, rlb, W, RST, evac):
            for dch in range(CK):
                ps = psms[dch % 2]
                fns = []
                for kch in range(CK):
                    lh = whb[:, kch * D + dch * P:kch * D + (dch + 1) * P]
                    ll = wlb[:, kch * D + dch * P:kch * D + (dch + 1) * P]
                    rh = rhb[:, kch * RST:kch * RST + W]
                    rl = rlb[:, kch * RST:kch * RST + W]
                    st_ = (kch == 0)
                    sp_ = (kch == CK - 1)
                    fns.append(lambda t, lh=lh, rh=rh, st_=st_, ps=ps, W=W:
                               t.matmul(ps[:, 0:W], lh, rh, start=st_, stop=False))
                    fns.append(lambda t, lh=lh, rl=rl, ps=ps, W=W:
                               t.matmul(ps[:, 0:W], lh, rl, start=False, stop=False))
                    fns.append(lambda t, ll=ll, rh=rh, sp_=sp_, ps=ps, W=W:
                               t.matmul(ps[:, 0:W], ll, rh, start=False, stop=sp_))
                op("tensor", fns, r=[whb, wlb, rhb, rlb], w=[ps])
                evac(dch, ps)

        def split_hi_lo(srcb, dh, dl, W, ST):
            op("scalar", lambda s: s.copy(
                c3(dh, ST)[:, :, 0:W], c3(srcb, ST)[:, :, 0:W]),
                r=[srcb], w=[dh])
            op("vector", lambda v: v.tensor_tensor(
                c3(dl, ST)[:, :, 0:W], c3(srcb, ST)[:, :, 0:W],
                c3(dh, ST)[:, :, 0:W], OP.subtract), r=[srcb, dh], w=[dl])

        def tloads(items, srcs, bufs):
            fns = [(lambda s, b=b, co=co, w=w, src=src, r0=r0, r1=r1, ca=ca, cb=cb:
                    s.dma_start(b[:, co:co + w], src.t[r0:r1, ca:cb],
                                transpose=True))
                   for (b, co, w, src, r0, r1, ca, cb) in items]
            op("sync", fns, r=srcs, w=bufs, kind="lw")

        def _ph_p0():
         with ExitStack() as p0:
            wt = [sbt(p0, "p0_w%d" % i, (P, DF)) for i in range(8)]
            wh = [sbt(p0, "p0_h%d" % i, (P, DF), BF16) for i in range(4)]
            wl = [sbt(p0, "p0_l%d" % i, (P, DF), BF16) for i in range(4)]
            acc = sbt(p0, "p0_a", (P, 34))
            prow = sbt(p0, "p0_row", (1, 16))
            ps0 = pst(p0, "ps0", 512)

            def ldgroup(w, c, it0, n):
                wts = [wt[(it0 + j) % 8] for j in range(n)]
                op("sync", [(lambda s, it=it0 + j, c=c, w=w, b_=wts[j]:
                             s.dma_start(b_[:, 0:c],
                                         wfull[w][it * P:(it + 1) * P, :]))
                            for j in range(n)], w=wts, kind="ld")
                return wts

            for w in ("wk", "wv", "wr", "wo", "fr", "fk", "fv"):
                rr = DF if w == "fk" else D
                c = DF if w == "fv" else D
                for it0 in range(0, rr // P, 4):
                    wts = ldgroup(w, c, it0, 4)
                    for j in range(4):
                        it = it0 + j
                        wtx = wts[j]
                        whb, wlb = wh[it % 4], wl[it % 4]
                        op("scalar", lambda s, c=c, wtx=wtx, whb=whb: s.copy(
                            whb[:, 0:c], wtx[:, 0:c]), r=[wtx], w=[whb])
                        op("vector", lambda v, c=c, wtx=wtx, whb=whb, wlb=wlb:
                           v.tensor_tensor(wlb[:, 0:c], wtx[:, 0:c],
                                           whb[:, 0:c], OP.subtract),
                           r=[wtx, whb], w=[wlb])
                        op("gpsimd",
                           [(lambda g, w=w, it=it, c=c, whb=whb: g.dma_start(
                               gsc[w + "h"].t[it * P:(it + 1) * P, :],
                               whb[:, 0:c])),
                            (lambda g, w=w, it=it, c=c, wlb=wlb: g.dma_start(
                                gsc[w + "l"].t[it * P:(it + 1) * P, :],
                                wlb[:, 0:c]))],
                           r=[whb, wlb], w=[gsc[w + "h"], gsc[w + "l"]],
                           kind="st")

            for it0 in range(0, DF // P, 4):
                wts = ldgroup("w1", D, it0, 4)
                op("vector", [(lambda v, it=it0 + j, wtx=wts[j]: v.reduce_sum(
                    acc[:, it:it + 1], wtx[:, 0:D], axis=AX,
                    apply_absolute_value=True)) for j in range(4)],
                   r=wts, w=[acc])
            op("vector", lambda v: v.reduce_sum(acc[:, 32:33], acc[:, 0:32],
                                                axis=AX), r=[acc], w=[acc])
            for it0 in range(0, D // P, 4):
                wts = ldgroup("w2", DF, it0, 4)
                op("vector", [(lambda v, it=it0 + j, wtx=wts[j]: v.reduce_sum(
                    acc[:, it:it + 1], wtx[:, 0:DF], axis=AX,
                    apply_absolute_value=True)) for j in range(4)],
                   r=wts, w=[acc])
            op("vector", lambda v: v.reduce_sum(acc[:, 33:34], acc[:, 0:8],
                                                axis=AX), r=[acc], w=[acc])
            op("tensor", [
                lambda t: t.matmul(ps0[0:1, 0:1], acc[:, 32:33], onc.t[:],
                                   start=True, stop=True),
                lambda t: t.matmul(ps0[0:1, 1:2], acc[:, 33:34], onc.t[:],
                                   start=True, stop=True)],
                r=[acc, onc], w=[ps0])
            op("vector", lambda v: v.tensor_copy(prow[0:1, 8:10], ps0[0:1, 0:2]),
               r=[ps0], w=[prow])
            op("vector", lambda v: v.tensor_scalar(
                prow[0:1, 10:12], prow[0:1, 8:10], 1.0 / (DF * D), 1e-5,
                OP.mult, OP.max), w=[prow])
            op("vector", lambda v: v.reciprocal(prow[0:1, 12:14], prow[0:1, 10:12]),
               w=[prow])
            op("tensor", [(lambda t, i=i, off=off: t.matmul(
                ps0[:, 8 + i:9 + i], onr.t[:, :], prow[0:1, off:off + 1],
                start=True, stop=True)) for i, off in enumerate((12, 13, 10, 11))],
                r=[onr, prow], w=[ps0])
            op("vector", lambda v: v.tensor_copy(swc.t[:], ps0[:, 8:12]),
               r=[ps0], w=[swc])

            def quant_full(w, gout, col):
                rr = DF if w == "w1" else D
                c = D if w == "w1" else DF
                for it0 in range(0, rr // P, 4):
                    wts = ldgroup(w, c, it0, 4)
                    for j in range(4):
                        it = it0 + j
                        wtx = wts[j]
                        whb, wlb = wh[it % 4], wl[it % 4]
                        op("vector", lambda v, c=c, wtx=wtx, wlb=wlb:
                           v.tensor_scalar(wlb[:, 0:c], wtx[:, 0:c],
                                           swc[:, col:col + 1], 0.5,
                                           OP.mult, OP.is_gt),
                           r=[wtx, swc], w=[wlb])
                        op("vector", lambda v, c=c, wtx=wtx, whb=whb:
                           v.tensor_scalar(whb[:, 0:c], wtx[:, 0:c],
                                           swc[:, col:col + 1], -0.5,
                                           OP.mult, OP.is_lt),
                           r=[wtx, swc], w=[whb])
                        op("vector", lambda v, c=c, whb=whb, wlb=wlb:
                           v.tensor_tensor(wlb[:, 0:c], wlb[:, 0:c],
                                           whb[:, 0:c], OP.subtract),
                           r=[whb], w=[wlb])
                        op("gpsimd", lambda g, it=it, c=c, gout=gout, wlb=wlb:
                           g.dma_start(gout.t[it * P:(it + 1) * P, :],
                                       wlb[:, 0:c]),
                           r=[wlb], w=[gout], kind="st")
            quant_full("w1", g_w1q, 0)
            quant_full("w2", g_w2q, 1)
        _ph_p0()
        pb.barrier()

        def _ph_a():
         with ExitStack() as pa:
            wkh = sbt(pa, "wkh", (P, CK * D), BF16)
            wkl = sbt(pa, "wkl", (P, CK * D), BF16)
            wvh = sbt(pa, "wvh", (P, CK * D), BF16)
            wvl = sbt(pa, "wvl", (P, CK * D), BF16)
            tloads([(buf, cc * D, D, g_, 0, D, cc * P, (cc + 1) * P)
                    for buf, g_ in ((wkh, gsc["wkh"]), (wkl, gsc["wkl"]),
                                    (wvh, gsc["wvh"]), (wvl, gsc["wvl"]))
                    for cc in range(CK)],
                   [gsc["wkh"], gsc["wkl"], gsc["wvh"], gsc["wvl"]],
                   [wkh, wkl, wvh, wvl])
            pstat = pst(pa, "A_pst", 1024)
            dtile = sbt(pa, "A_dt", (P, CK * WA))
            build_dtile(dtile)
            xa = sbt(pa, "A_x", (P, CK * STA))
            sq = sbt(pa, "A_sq", (P, CK * STA))
            n1 = sbt(pa, "A_n1", (P, CK * STA))
            hh = sbt(pa, "A_h", (P, CK * STA))
            stb = sbt(pa, "A_stb", (P, 5 * STA))
            wkt = sbt(pa, "A_wk", (P, CK * STW))
            vt = sbt(pa, "A_v", (P, CK * STW))
            zt = sbt(pa, "A_z", (P, CK * STW))
            scA = sbt(pa, "A_scA", (P, CK * STW))
            scB = sbt(pa, "A_scB", (P, CK * STW))
            mxf = sbt(pa, "A_mx", (P, CK * STW))
            bh = sbt(pa, "A_bh", (P, CK * STW), BF16)
            bl = sbt(pa, "A_bl", (P, CK * STW), BF16)
            bh2 = sbt(pa, "A_bh2", (P, CK * STW), BF16)
            bl2 = sbt(pa, "A_bl2", (P, CK * STW), BF16)
            pmA = [pst(pa, "pmA0", 512), pst(pa, "pmA1", 512)]

            for si, (c0, W) in enumerate(subs):
                W1 = W + 1
                op("sync", lambda s, c0=c0, W1=W1: s.dma_start(
                    schv(xa, W1), dchv(xT, c0, W1)), w=[xa], kind="ld")
                ln_na(xa, n1, W1, STA, sq, None, stb, pstat, mm_stats=True)
                op("gpsimd", lambda s, c0=c0, W=W: s.dma_start(
                    dchv(sp["n1"].t, c0, W),
                    n1.t[:].rearrange("p (ck t) -> p ck t", ck=CK)[:, :, 1:W + 1]),
                    r=[n1], w=[sp["n1"]], kind="st")
                op("vector", lambda v, W1=W1: v.tensor_tensor(
                    stb[:, 3 * STA:3 * STA + W1], stb[:, 4 * STA:4 * STA + W1],
                    stb[:, 4 * STA:4 * STA + W1], OP.mult), w=[stb])
                op("vector", lambda v, W1=W1: v.tensor_scalar(
                    stb[:, 3 * STA:3 * STA + W1], stb[:, 3 * STA:3 * STA + W1],
                    -EPS, 1.0 + EPS, OP.mult, OP.add), w=[stb])
                op("scalar", lambda s, W1=W1: s.activation(
                    stb[:, 3 * STA:3 * STA + W1], stb[:, 3 * STA:3 * STA + W1],
                    AF.Sqrt), w=[stb])
                op("vector", lambda v, W1=W1: v.reciprocal(
                    stb[:, 4 * STA:4 * STA + W1], stb[:, 3 * STA:3 * STA + W1]),
                    w=[stb])
                op("vector", lambda v, W1=W1: v.tensor_tensor(
                    c3(hh, STA)[:, :, 0:W1], c3(n1, STA)[:, :, 0:W1],
                    bc3(stb[:, 4 * STA:4 * STA + W1], CK, W1), OP.mult),
                    r=[n1, stb], w=[hh])
                if si == 0:
                    op("vector", lambda v: v.tensor_scalar_mul(
                        c3(hh, STA)[:, :, 1:1 + HB], c3(hh, STA)[:, :, 1:1 + HB],
                        met[:, 0:1]), r=[met], w=[hh])
                op("vector", lambda v, W1=W1: v.tensor_tensor(
                    c3(sq, STA)[:, :, 1:W1], c3(hh, STA)[:, :, 1:W1],
                    c3(hh, STA)[:, :, 0:W1 - 1], OP.subtract), r=[hh], w=[sq])

                def mix(tmoff, oh, ol):
                    tmb = prm[:, tmoff:tmoff + 8].unsqueeze(2).broadcast_to(
                        [P, CK, W])
                    op("vector", lambda v, tmb=tmb, W=W: v.tensor_tensor(
                        c3(mxf, STW)[:, :, 0:W], c3(sq, STA)[:, :, 1:1 + W],
                        tmb, OP.mult), r=[sq, prm], w=[mxf])
                    op("vector", lambda v, W=W: v.tensor_tensor(
                        c3(mxf, STW)[:, :, 0:W], c3(mxf, STW)[:, :, 0:W],
                        c3(hh, STA)[:, :, 0:W], OP.add), r=[hh], w=[mxf])
                    split_hi_lo(mxf, oh, ol, W, STW)

                mix(0, bh, bl)
                proj3(pmA, wkh, wkl, bh, bl, W, STW,
                      lambda dch, ps: op("scalar", lambda s, dch=dch, ps=ps, W=W:
                                         s.activation(wkt[:, dch * STW:dch * STW + W],
                                                      ps[:, 0:W], AF.Exp),
                                         r=[ps], w=[wkt]))
                mix(8, bh2, bl2)
                proj3(pmA, wvh, wvl, bh2, bl2, W, STW,
                      lambda dch, ps: op("scalar", lambda s, dch=dch, ps=ps, W=W:
                                         s.copy(vt[:, dch * STW:dch * STW + W],
                                                ps[:, 0:W]),
                                         r=[ps], w=[vt]))
                mix(16, bh, bl)
                op("gpsimd", lambda s, c0=c0, W=W: s.dma_start(
                    dchv(spb["xrh"].t, c0, W), schv(bh, W)),
                    r=[bh], w=[spb["xrh"]], kind="st")
                op("gpsimd", lambda s, c0=c0, W=W: s.dma_start(
                    dchv(spb["xrl"].t, c0, W), schv(bl, W)),
                    r=[bl], w=[spb["xrl"]], kind="st")
                op("vector", lambda v, W=W: v.tensor_tensor(
                    c3(zt, STW)[:, :, 0:W], c3(wkt, STW)[:, :, 0:W],
                    c3(vt, STW)[:, :, 0:W], OP.mult), r=[wkt, vt], w=[zt])
                op("gpsimd", lambda s, c0=c0, W=W: s.dma_start(
                    dchv(sp["wk"].t, c0, W), schv(wkt, W)),
                    r=[wkt], w=[sp["wk"]], kind="st")
                op("gpsimd", lambda s, c0=c0, W=W: s.dma_start(
                    dchv(sp["z"].t, c0, W), schv(zt, W)),
                    r=[zt], w=[sp["z"]], kind="st")
                lo, hi = max(c0, OWN0), min(c0 + W, OWN1)
                if lo < hi:
                    o0, w2 = lo - c0, hi - lo
                    op("vector",
                       [(lambda v, ck=ck, o0=o0, w2=w2: v.tensor_tensor_scan(
                           scA[:, ck * STW:ck * STW + w2],
                           dtile[:, ck * WA:ck * WA + w2],
                           zt[:, ck * STW + o0:ck * STW + o0 + w2],
                           stA[:, ck:ck + 1], OP.mult, OP.add))
                        for ck in range(CK)] +
                       [(lambda v, ck=ck, o0=o0, w2=w2: v.tensor_tensor_scan(
                           scB[:, ck * STW:ck * STW + w2],
                           dtile[:, ck * WA:ck * WA + w2],
                           wkt[:, ck * STW + o0:ck * STW + o0 + w2],
                           stB[:, ck:ck + 1], OP.mult, OP.add))
                        for ck in range(CK)],
                       r=[dtile, zt, wkt, stA, stB], w=[scA, scB])
                    op("vector",
                       [(lambda v, w2=w2: v.tensor_copy(
                           col3(stA), c3(scA, STW)[:, :, w2 - 1:w2])),
                        (lambda v, w2=w2: v.tensor_copy(
                            col3(stB), c3(scB, STW)[:, :, w2 - 1:w2]))],
                       r=[scA, scB], w=[stA, stB])
                    caps = []
                    for owncol, gcol in ((OWN1 - 1, 0), (OWN1 - 9, 16)):
                        if lo <= owncol < hi:
                            oc = owncol - lo
                            caps.append(lambda v, oc=oc, gcol=gcol: v.tensor_copy(
                                gbuf.t[:, gcol:gcol + 8].unsqueeze(2),
                                c3(scA, STW)[:, :, oc:oc + 1]))
                            caps.append(lambda v, oc=oc, gcol=gcol: v.tensor_copy(
                                gbuf.t[:, gcol + 8:gcol + 16].unsqueeze(2),
                                c3(scB, STW)[:, :, oc:oc + 1]))
                    if caps:
                        op("vector", caps, r=[scA, scB], w=[gbuf])
        _ph_a()
        pb.barrier()

        def _ph_exch():
         with ExitStack() as pe:
            psx = pst(pe, "psx", 512)
            rows = sbt(pe, "rows", (1, 6 * D))
            t8 = sbt(pe, "t8", (8, D))
            t8b = sbt(pe, "t8b", (8, D))
            e8 = sbt(pe, "e8", (8, D))
            gtA = sbt(pe, "gtA", (8, D))
            gtB = sbt(pe, "gtB", (8, D))
            op("sync", lambda s: s.dma_start(t8.t[:], td8_in), w=[t8], kind="ld")
            op("scalar", lambda s: s.activation(t8b.t[:], t8.t[:], AF.Exp),
               r=[t8], w=[t8b])
            op("vector", lambda v: v.tensor_scalar_mul(t8.t[:], t8b.t[:], -1.0),
               r=[t8b], w=[t8])
            for gcol, roff in ((0, 0), (8, 1), (16, 2), (24, 3)):
                for ck in range(CK):
                    op("tensor", lambda t, gcol=gcol, ck=ck: t.matmul(
                        psx[0:1, 0:P], gbuf[:, gcol + ck:gcol + ck + 1], idn.t[:],
                        start=True, stop=True), r=[gbuf, idn], w=[psx])
                    op("vector", lambda v, roff=roff, ck=ck: v.tensor_copy(
                        rows[0:1, roff * D + ck * P:roff * D + (ck + 1) * P],
                        psx[0:1, 0:P]), r=[psx], w=[rows])
            op("gpsimd", lambda s: s.dma_start(ci_r1.t, rows[0:1, 0:4 * D]),
               r=[rows], w=[ci_r1], kind="st")
            op("gpsimd", lambda g: g.collective_compute(
                "AllGather", OP.bypass, [RG], [ci_r1.t], [g_r1.t]),
                r=[ci_r1], w=[g_r1])
            op("sync", [lambda s: s.dma_start(gtA.t[:], g_r1.t[0::4, :]),
                        lambda s: s.dma_start(gtB.t[:], g_r1.t[1::4, :])],
               r=[g_r1], w=[gtA, gtB], kind="ld")
            op("scalar", lambda s: s.activation(e8.t[:], t8.t[:], AF.Exp,
                                                scale=met[0:8, 3:4]),
               r=[t8, met], w=[e8])
            op("vector", lambda v: v.tensor_scalar_mul(e8.t[:], e8.t[:],
                                                       met[0:8, 4:5]),
               r=[met], w=[e8])
            for tile, off in ((gtA, 4 * D), (gtB, 5 * D)):
                op("vector", lambda v, tile=tile: v.tensor_tensor(
                    tile.t[:], tile.t[:], e8.t[:], OP.mult), r=[e8], w=[tile])
                for half in range(2):
                    op("tensor", lambda t, tile=tile, half=half: t.matmul(
                        psx[0:1, 0:512], onc[0:8, :],
                        tile[:, half * 512:(half + 1) * 512],
                        start=True, stop=True), r=[onc, tile], w=[psx])
                    op("vector", lambda v, off=off, half=half: v.tensor_copy(
                        rows[0:1, off + half * 512:off + (half + 1) * 512],
                        psx[0:1, 0:512]), r=[psx], w=[rows])
            op("scalar", lambda s: s.activation(rows[0:1, 0:D], t8[0:1, :], AF.Exp,
                                                scale=float(n_own - 8)),
               r=[t8, met], w=[rows])
            op("vector", [
                lambda v: v.tensor_tensor(rows[0:1, 4 * D:5 * D],
                                          rows[0:1, 4 * D:5 * D],
                                          rows[0:1, 0:D], OP.mult),
                lambda v: v.tensor_tensor(rows[0:1, 5 * D:6 * D],
                                          rows[0:1, 5 * D:6 * D],
                                          rows[0:1, 0:D], OP.mult)],
                w=[rows])
            op("vector", [
                lambda v: v.tensor_tensor(rows[0:1, 4 * D:5 * D],
                                          rows[0:1, 4 * D:5 * D],
                                          rows[0:1, 2 * D:3 * D], OP.add),
                lambda v: v.tensor_tensor(rows[0:1, 5 * D:6 * D],
                                          rows[0:1, 5 * D:6 * D],
                                          rows[0:1, 3 * D:4 * D], OP.add)],
                w=[rows])
            op("gpsimd", lambda s: s.dma_start(ci_r2.t, rows[0:1, 4 * D:6 * D]),
               r=[rows], w=[ci_r2], kind="st")
            op("gpsimd", lambda g: g.collective_compute(
                "AllGather", OP.bypass, [RG], [ci_r2.t], [g_r2.t]),
                r=[ci_r2], w=[g_r2])
            op("sync", [lambda s: s.dma_start(gtA.t[:], g_r2.t[0::2, :]),
                        lambda s: s.dma_start(gtB.t[:], g_r2.t[1::2, :])],
               r=[g_r2], w=[gtA, gtB], kind="ld")
            for tile, off in ((gtA, 4 * D), (gtB, 5 * D)):
                op("vector", lambda v, tile=tile: v.tensor_scalar_mul(
                    tile.t[:], tile.t[:], met[0:8, 5:6]), r=[met], w=[tile])
                for half in range(2):
                    op("tensor", lambda t, tile=tile, half=half: t.matmul(
                        psx[0:1, 0:512], onc[0:8, :],
                        tile[:, half * 512:(half + 1) * 512],
                        start=True, stop=True), r=[onc, tile], w=[psx])
                    op("vector", lambda v, off=off, half=half: v.tensor_copy(
                        rows[0:1, off + half * 512:off + (half + 1) * 512],
                        psx[0:1, 0:512]), r=[psx], w=[rows])
            for off, colbuf in ((4 * D, iA), (5 * D, iB)):
                for ck in range(CK):
                    op("tensor", lambda t, off=off, ck=ck: t.matmul(
                        psx[:, 0:1], rows[0:1, off + ck * P:off + (ck + 1) * P],
                        onr[0:1, 0:1], start=True, stop=True),
                        r=[rows, onr], w=[psx])
                    op("vector", lambda v, colbuf=colbuf, ck=ck: v.tensor_copy(
                        colbuf.t[:, ck:ck + 1], psx[:, 0:1]), r=[psx], w=[colbuf])
        _ph_exch()
        pb.barrier()

        def _ph_s1():
         with ExitStack() as s1:
            woh = sbt(s1, "woh", (P, CK * D), BF16)
            wol = sbt(s1, "wol", (P, CK * D), BF16)
            wrh = sbt(s1, "wrh", (P, CK * D), BF16)
            wrl = sbt(s1, "wrl", (P, CK * D), BF16)
            tloads([(buf, cc * D, D, g_, 0, D, cc * P, (cc + 1) * P)
                    for buf, g_ in ((wrh, gsc["wrh"]), (wrl, gsc["wrl"]),
                                    (woh, gsc["woh"]), (wol, gsc["wol"]))
                    for cc in range(CK)],
                   [gsc["wrh"], gsc["wrl"], gsc["woh"], gsc["wol"]],
                   [wrh, wrl, woh, wol])
            pstat = pst(s1, "S_pst", 1024)
            dtile = sbt(s1, "S_dt", (P, CK * WA))
            build_dtile(dtile)
            wkt = sbt(s1, "B_wk", (P, CK * STW))
            zt = sbt(s1, "B_z", (P, CK * STW))
            n1t = sbt(s1, "B_n1", (P, CK * STW))
            xrh = sbt(s1, "B_xrh", (P, CK * STW), BF16)
            xrl = sbt(s1, "B_xrl", (P, CK * STW), BF16)
            scA = sbt(s1, "B_scA", (P, CK * STS))
            scB = sbt(s1, "B_scB", (P, CK * STS))
            numt = sbt(s1, "B_num", (P, CK * STW))
            dent = sbt(s1, "B_den", (P, CK * STW))
            srt = sbt(s1, "B_sr", (P, CK * STW))
            y1t = sbt(s1, "B_y1", (P, CK * STW))
            gt = sbt(s1, "B_g", (P, CK * STW))
            sq = sbt(s1, "B_sq", (P, CK * STW))
            stb = sbt(s1, "B_stb", (P, 5 * STW))
            gprev = sbt(s1, "B_gp", (P, CK))
            bh = sbt(s1, "B_bh", (P, CK * STW), BF16)
            bl = sbt(s1, "B_bl", (P, CK * STW), BF16)
            pmB = [pst(s1, "pmB0", 512), pst(s1, "pmB1", 512)]
            op("vector", lambda v: v.memset(gprev.t[:], 0.0), w=[gprev])

            for si, (c0, W) in enumerate(subs):
                op("sync", [(lambda s, b=b, spn=spn, c0=c0, W=W: s.dma_start(
                    schv(b, W), dchv(spn.t, c0, W)))
                    for b, spn in ((wkt, sp["wk"]), (zt, sp["z"]), (n1t, sp["n1"]),
                                   (xrh, spb["xrh"]), (xrl, spb["xrl"]))],
                    r=[sp["wk"], sp["z"], sp["n1"], spb["xrh"], spb["xrl"]],
                    w=[wkt, zt, n1t, xrh, xrl], kind="ld")
                if si == 0:
                    op("vector", [
                        lambda v: v.tensor_scalar_mul(
                            c3(zt, STW)[:, :, 0:HB], c3(zt, STW)[:, :, 0:HB],
                            met[:, 0:1]),
                        lambda v: v.tensor_scalar_mul(
                            c3(wkt, STW)[:, :, 0:HB], c3(wkt, STW)[:, :, 0:HB],
                            met[:, 0:1])], r=[met], w=[zt, wkt])
                    op("vector", [
                        lambda v: v.tensor_copy(c3(scA, STS)[:, :, 0:1], col3(iA)),
                        lambda v: v.tensor_copy(c3(scB, STS)[:, :, 0:1], col3(iB))],
                       r=[iA, iB], w=[scA, scB])
                op("vector",
                   [(lambda v, ck=ck, W=W: v.tensor_tensor_scan(
                       scA[:, ck * STS + 1:ck * STS + 1 + W],
                       dtile[:, ck * WA:ck * WA + W],
                       zt[:, ck * STW:ck * STW + W],
                       scA[:, ck * STS:ck * STS + 1], OP.mult, OP.add))
                    for ck in range(CK)] +
                   [(lambda v, ck=ck, W=W: v.tensor_tensor_scan(
                       scB[:, ck * STS + 1:ck * STS + 1 + W],
                       dtile[:, ck * WA:ck * WA + W],
                       wkt[:, ck * STW:ck * STW + W],
                       scB[:, ck * STS:ck * STS + 1], OP.mult, OP.add))
                    for ck in range(CK)],
                   r=[dtile, zt, wkt], w=[scA, scB])
                eub = eu_pm.t[:].unsqueeze(2).broadcast_to([P, CK, W])
                op("vector", lambda v, W=W, eub=eub: v.tensor_tensor(
                    c3(numt, STW)[:, :, 0:W], c3(zt, STW)[:, :, 0:W], eub,
                    OP.mult), r=[zt, eu_pm], w=[numt])
                op("vector", lambda v, W=W: v.tensor_tensor(
                    c3(numt, STW)[:, :, 0:W], c3(numt, STW)[:, :, 0:W],
                    c3(scA, STS)[:, :, 0:W], OP.add), r=[scA], w=[numt])
                op("vector", lambda v, W=W, eub=eub: v.tensor_tensor(
                    c3(dent, STW)[:, :, 0:W], c3(wkt, STW)[:, :, 0:W], eub,
                    OP.mult), r=[wkt, eu_pm], w=[dent])
                op("vector", lambda v, W=W: v.tensor_tensor(
                    c3(dent, STW)[:, :, 0:W], c3(dent, STW)[:, :, 0:W],
                    c3(scB, STS)[:, :, 0:W], OP.add), r=[scB], w=[dent])
                if si == 0:
                    op("vector", lambda v: v.tensor_scalar_add(
                        c3(dent, STW)[:, :, 0:HB], c3(dent, STW)[:, :, 0:HB],
                        met[:, 2:3]), r=[met], w=[dent])
                op("vector", lambda v, W=W: v.reciprocal(
                    c3(sq, STW)[:, :, 0:W], c3(dent, STW)[:, :, 0:W]),
                    r=[dent], w=[sq])
                op("vector", lambda v, W=W: v.tensor_tensor(
                    c3(numt, STW)[:, :, 0:W], c3(numt, STW)[:, :, 0:W],
                    c3(sq, STW)[:, :, 0:W], OP.mult), r=[sq], w=[numt])
                proj3(pmB, wrh, wrl, xrh, xrl, W, STW,
                      lambda dch, ps: op("scalar", lambda s, dch=dch, ps=ps, W=W:
                                         s.activation(srt[:, dch * STW:dch * STW + W],
                                                      ps[:, 0:W], AF.Sigmoid),
                                         r=[ps], w=[srt]))
                op("vector", lambda v, W=W: v.tensor_tensor(
                    c3(srt, STW)[:, :, 0:W], c3(srt, STW)[:, :, 0:W],
                    c3(numt, STW)[:, :, 0:W], OP.mult), r=[numt], w=[srt])
                split_hi_lo(srt, bh, bl, W, STW)
                proj3(pmB, woh, wol, bh, bl, W, STW,
                      lambda dch, ps: op("vector", lambda v, dch=dch, ps=ps, W=W:
                                         v.tensor_tensor(
                                             y1t[:, dch * STW:dch * STW + W],
                                             ps[:, 0:W],
                                             n1t[:, dch * STW:dch * STW + W], OP.add),
                                         r=[ps, n1t], w=[y1t]))
                op("gpsimd", lambda s, c0=c0, W=W: s.dma_start(
                    dchv(sp["y1"].t, c0, W), schv(y1t, W)),
                    r=[y1t], w=[sp["y1"]], kind="st")
                ln_na(y1t, gt, W, STW, sq, None, stb, pstat, mm_stats=True)
                if si == 0:
                    op("vector", lambda v: v.tensor_scalar_mul(
                        c3(gt, STW)[:, :, HB - 1:HB], c3(gt, STW)[:, :, HB - 1:HB],
                        met[:, 0:1]), r=[met], w=[gt])
                op("vector", [
                    (lambda v, W=W: v.tensor_tensor(
                        c3(sq, STW)[:, :, 1:W], c3(gt, STW)[:, :, 1:W],
                        c3(gt, STW)[:, :, 0:W - 1], OP.subtract)),
                    (lambda v: v.tensor_tensor(
                        c3(sq, STW)[:, :, 0:1], c3(gt, STW)[:, :, 0:1],
                        col3(gprev), OP.subtract))],
                   r=[gt, gprev], w=[sq])

                def mix2(tmoff, sph, spl):
                    tmb = prm[:, tmoff:tmoff + 8].unsqueeze(2).broadcast_to(
                        [P, CK, W])
                    op("vector", lambda v, tmb=tmb, W=W: v.tensor_tensor(
                        c3(numt, STW)[:, :, 0:W], c3(sq, STW)[:, :, 0:W], tmb,
                        OP.mult), r=[sq, prm], w=[numt])
                    op("vector", [
                        (lambda v, W=W: v.tensor_tensor(
                            c3(numt, STW)[:, :, 1:W], c3(numt, STW)[:, :, 1:W],
                            c3(gt, STW)[:, :, 0:W - 1], OP.add)),
                        (lambda v: v.tensor_tensor(
                            c3(numt, STW)[:, :, 0:1], c3(numt, STW)[:, :, 0:1],
                            col3(gprev), OP.add))],
                       r=[gt, gprev], w=[numt])
                    split_hi_lo(numt, bh, bl, W, STW)
                    op("gpsimd", lambda s, c0=c0, W=W, sph=sph: s.dma_start(
                        dchv(sph.t, c0, W), schv(bh, W)),
                        r=[bh], w=[sph], kind="st")
                    op("gpsimd", lambda s, c0=c0, W=W, spl=spl: s.dma_start(
                        dchv(spl.t, c0, W), schv(bl, W)),
                        r=[bl], w=[spl], kind="st")

                mix2(24, spb["gkh"], spb["gkl"])
                mix2(32, spb["grh"], spb["grl"])
                op("vector", lambda v, W=W: v.tensor_copy(
                    col3(gprev), c3(gt, STW)[:, :, W - 1:W]), r=[gt], w=[gprev])
                op("vector", [
                    (lambda v, W=W: v.tensor_copy(
                        c3(scA, STS)[:, :, 0:1], c3(scA, STS)[:, :, W:W + 1])),
                    (lambda v, W=W: v.tensor_copy(
                        c3(scB, STS)[:, :, 0:1], c3(scB, STS)[:, :, W:W + 1]))],
                   w=[scA, scB])
        _ph_s1()
        pb.barrier()

        def _ph_k():
         with ExitStack() as sk:
            gkh_r = sbt(sk, "K_gkh", (P, CK * RNG), BF16)
            gkl_r = sbt(sk, "K_gkl", (P, CK * RNG), BF16)
            op("sync", [lambda s: s.dma_start(schv(gkh_r, RNG),
                                              dchv(spb["gkh"].t, 0, RNG)),
                        lambda s: s.dma_start(schv(gkl_r, RNG),
                                              dchv(spb["gkl"].t, 0, RNG))],
               r=[spb["gkh"], spb["gkl"]], w=[gkh_r, gkl_r], kind="ld")
            HD2 = DF // 2
            fkh = sbt(sk, "K_fkh", (P, CK * HD2), BF16)
            fkl = sbt(sk, "K_fkl", (P, CK * HD2), BF16)
            tmp1 = [sbt(sk, "K_t1a", (P, 512)), sbt(sk, "K_t1b", (P, 512))]
            tmp2 = [sbt(sk, "K_t2a", (P, 512)), sbt(sk, "K_t2b", (P, 512))]
            kbh = [sbt(sk, "K_bha", (P, 512), BF16), sbt(sk, "K_bhb", (P, 512), BF16)]
            kbl = [sbt(sk, "K_bla", (P, 512), BF16), sbt(sk, "K_blb", (P, 512), BF16)]
            pmK = [pst(sk, "pmK0", 512), pst(sk, "pmK1", 512)]
    
            NQ = 4
            FLQ = (HD2 // P) // NQ
            fkhF = [Buf(fkh.t) for _ in range(NQ)]
            fklF = [Buf(fkl.t) for _ in range(NQ)]
            pi = 0
            for half in range(2):
                for q_ in range(NQ):
                    tloads([(buf, cc * HD2 + q_ * FLQ * P, FLQ * P, g_,
                             half * HD2 + q_ * FLQ * P,
                             half * HD2 + (q_ + 1) * FLQ * P,
                             cc * P, (cc + 1) * P)
                            for buf, g_ in ((fkh, gsc["fkh"]), (fkl, gsc["fkl"]))
                            for cc in range(CK)],
                           [gsc["fkh"], gsc["fkl"]], [fkhF[q_], fklF[q_]])
                for fl in range(HD2 // P):
                    f = half * (HD2 // P) + fl
                    for ti, (c0, W) in enumerate(tsubs(512, 0, RNG)):
                        bi = pi % 2
                        pi += 1
                        ps = pmK[bi]
                        fns = []
                        for kch in range(CK):
                            lh = fkh[:, kch * HD2 + fl * P:kch * HD2 + (fl + 1) * P]
                            ll = fkl[:, kch * HD2 + fl * P:kch * HD2 + (fl + 1) * P]
                            rh = gkh_r[:, kch * RNG + c0:kch * RNG + c0 + W]
                            rl = gkl_r[:, kch * RNG + c0:kch * RNG + c0 + W]
                            st_, sp_ = kch == 0, kch == CK - 1
                            fns.append(lambda t, lh=lh, rh=rh, st_=st_, ps=ps, W=W:
                                       t.matmul(ps[:, 0:W], lh, rh, start=st_,
                                                stop=False))
                            fns.append(lambda t, lh=lh, rl=rl, ps=ps, W=W:
                                       t.matmul(ps[:, 0:W], lh, rl, start=False,
                                                stop=False))
                            fns.append(lambda t, ll=ll, rh=rh, sp_=sp_, ps=ps, W=W:
                                       t.matmul(ps[:, 0:W], ll, rh, start=False,
                                                stop=sp_))
                        op("tensor", fns,
                           r=[fkhF[fl // FLQ], fklF[fl // FLQ], gkh_r, gkl_r],
                           w=[ps])
                        t1, t2 = tmp1[bi], tmp2[bi]
                        bh_, bl_ = kbh[bi], kbl[bi]
                        op("scalar", lambda s, ps=ps, t1=t1, W=W: s.activation(
                            t1[:, 0:W], ps[:, 0:W], AF.Relu), r=[ps], w=[t1])
                        op("scalar", lambda s, t1=t1, t2=t2, W=W: s.activation(
                            t2[:, 0:W], t1[:, 0:W], AF.Square), r=[t1], w=[t2])
                        op("scalar", lambda s, t2=t2, bh_=bh_, W=W: s.copy(
                            bh_[:, 0:W], t2[:, 0:W]), r=[t2], w=[bh_])
                        op("vector", lambda v, t2=t2, bh_=bh_, bl_=bl_, W=W:
                           v.tensor_tensor(bl_[:, 0:W], t2[:, 0:W], bh_[:, 0:W],
                                           OP.subtract), r=[t2, bh_], w=[bl_])
                        op("gpsimd", lambda s, f=f, c0=c0, bh_=bh_, W=W: s.dma_start(
                            sp_kfh.t[f * P:(f + 1) * P, c0:c0 + W], bh_[:, 0:W]),
                            r=[bh_], w=[sp_kfh], kind="st")
                        op("gpsimd", lambda s, f=f, c0=c0, bl_=bl_, W=W: s.dma_start(
                            sp_kfl.t[f * P:(f + 1) * P, c0:c0 + W], bl_[:, 0:W]),
                            r=[bl_], w=[sp_kfl], kind="st")
        _ph_k()
        pb.barrier()

        def _ph_v():
         with ExitStack() as sv:
            fvh = sbt(sv, "V_fvh", (P, FCK * D), BF16)
            fvl = sbt(sv, "V_fvl", (P, FCK * D), BF16)
            tloads([(buf, f * D, D, g_, 0, D, f * P, (f + 1) * P)
                    for buf, g_ in ((fvh, gsc["fvh"]), (fvl, gsc["fvl"]))
                    for f in range(FCK)],
                   [gsc["fvh"], gsc["fvl"]], [fvh, fvl])
            kfh_t = sbt(sv, "V_kfh", (P, FCK * 512), BF16)
            kfl_t = sbt(sv, "V_kfl", (P, FCK * 512), BF16)
            kfhR = [kfh_t, Buf(kfh_t.t)]
            kflR = [kfl_t, Buf(kfl_t.t)]
            ev = [sbt(sv, "V_ev0", (P, 512)), sbt(sv, "V_ev1", (P, 512))]
            pk = [pst(sv, "pkV0", 2048), pst(sv, "pkV1", 2048)]
            for si, (c0, W) in enumerate(tsubs(256, 0, RNG)):
                sl = si % 2
                o0 = sl * 256
                op("sync", [lambda s, c0=c0, W=W, o0=o0: s.dma_start(
                    kfh_t.t[:].rearrange("p (f t) -> p f t", f=FCK)[:, :, o0:o0 + W],
                    dchv(sp_kfh.t, c0, W, nck=FCK)),
                    lambda s, c0=c0, W=W, o0=o0: s.dma_start(
                    kfl_t.t[:].rearrange("p (f t) -> p f t", f=FCK)[:, :, o0:o0 + W],
                    dchv(sp_kfl.t, c0, W, nck=FCK))],
                   r=[sp_kfh, sp_kfl], w=[kfhR[sl], kflR[sl]], kind="ld")
                for grp in range(2):
                    pkg = pk[grp]
                    fns = []
                    for f in range(FCK):
                        for ci in range(4):
                            c = grp * 4 + ci
                            lh = fvh[:, f * D + c * P:f * D + (c + 1) * P]
                            ll = fvl[:, f * D + c * P:f * D + (c + 1) * P]
                            rh = kfh_t[:, f * 512 + o0:f * 512 + o0 + W]
                            rl = kfl_t[:, f * 512 + o0:f * 512 + o0 + W]
                            st_, sp_ = f == 0, f == FCK - 1
                            o = ci * 512
                            fns.append(lambda t, lh=lh, rh=rh, st_=st_, o=o,
                                       pkg=pkg, W=W:
                                       t.matmul(pkg[:, o:o + W], lh, rh,
                                                start=st_, stop=False))
                            fns.append(lambda t, lh=lh, rl=rl, o=o, pkg=pkg, W=W:
                                       t.matmul(pkg[:, o:o + W], lh, rl,
                                                start=False, stop=False))
                            fns.append(lambda t, ll=ll, rh=rh, sp_=sp_, o=o,
                                       pkg=pkg, W=W:
                                       t.matmul(pkg[:, o:o + W], ll, rh,
                                                start=False, stop=sp_))
                    op("tensor", fns, r=[fvh, fvl, kfhR[sl], kflR[sl]], w=[pkg])
                    for ci in range(4):
                        c = grp * 4 + ci
                        eb = ev[c % 2]
                        op("vector", lambda v, eb=eb, ci=ci, W=W, pkg=pkg:
                           v.tensor_copy(eb.t[:, 0:W], pkg[:, ci * 512:ci * 512 + W]),
                           r=[pkg], w=[eb])
                        op("gpsimd", lambda s, eb=eb, c=c, c0=c0, W=W: s.dma_start(
                            sp["kv"].t[c * P:(c + 1) * P, c0:c0 + W], eb.t[:, 0:W]),
                            r=[eb], w=[sp["kv"]], kind="st")
        _ph_v()
        pb.barrier()

        def _ph_frh():
         with ExitStack() as sf:
            n2q = sbt(sf, "n2q", (P, CK * RNG), BF16)
            r1row = sbt(sf, "r1row", (1, RNG))
            rrow = sbt(sf, "rrow", (1, 512))
            qcb = sbt(sf, "qcb", (P, 16))
            with ExitStack() as sfr:
                frh = sbt(sfr, "frh", (P, CK * D), BF16)
                frl = sbt(sfr, "frl", (P, CK * D), BF16)
                tloads([(buf, cc * D, D, g_, 0, D, cc * P, (cc + 1) * P)
                        for buf, g_ in ((frh, gsc["frh"]), (frl, gsc["frl"]))
                        for cc in range(CK)],
                       [gsc["frh"], gsc["frl"]], [frh, frl])
                pstat = pst(sfr, "R_pst", 1024)
                grh = sbt(sfr, "R_grh", (P, CK * STW), BF16)
                grl = sbt(sfr, "R_grl", (P, CK * STW), BF16)
                y1t = sbt(sfr, "R_y1", (P, CK * STW))
                kvt = sbt(sfr, "R_kv", (P, CK * STW))
                xt_ = sbt(sfr, "R_x", (P, CK * STW))
                sigt = sbt(sfr, "R_sig", (P, CK * STW))
                n2t = sbt(sfr, "R_n2", (P, CK * STW))
                sq = sbt(sfr, "R_sq", (P, CK * STW))
                stb = sbt(sfr, "R_stb", (P, 5 * STW))
                qi = sbt(sfr, "R_qi", (P, CK * STW), I32)
                q0 = sbt(sfr, "R_q0", (P, CK * STW))
                q1 = sbt(sfr, "R_q1", (P, CK * STW))
                pmF = [pst(sfr, "pmR0", 512), pst(sfr, "pmR1", 512)]
                def frloads_a(si):
                    c0, W = subs[si]
                    op("sync", [(lambda s, b=b, spn=spn, c0=c0, W=W: s.dma_start(
                        schv(b, W), dchv(spn.t, c0, W)))
                        for b, spn in ((grh, spb["grh"]), (grl, spb["grl"]))],
                        r=[spb["grh"], spb["grl"]], w=[grh, grl], kind="ld")

                def frloads_b(si):
                    c0, W = subs[si]
                    op("sync", [(lambda s, b=b, spn=spn, c0=c0, W=W: s.dma_start(
                        schv(b, W), dchv(spn.t, c0, W)))
                        for b, spn in ((y1t, sp["y1"]), (kvt, sp["kv"]))] +
                        [lambda s, c0=c0, W=W: s.dma_start(
                            schv(xt_, W), dchv(xT, c0 + 1, W))],
                        r=[sp["y1"], sp["kv"]],
                        w=[y1t, kvt, xt_], kind="ld")

                def frproj(si):
                    c0, W = subs[si]
                    proj3(pmF, frh, frl, grh, grl, W, STW,
                          lambda dch, ps: op("scalar",
                                             lambda s, dch=dch, ps=ps, W=W:
                                             s.activation(
                                                 sigt[:, dch * STW:dch * STW + W],
                                                 ps[:, 0:W], AF.Sigmoid),
                                             r=[ps], w=[sigt]))

                def fry2(si):
                    c0, W = subs[si]
                    op("vector", lambda v, W=W: v.tensor_tensor(
                        c3(kvt, STW)[:, :, 0:W], c3(sigt, STW)[:, :, 0:W],
                        c3(kvt, STW)[:, :, 0:W], OP.mult), r=[sigt], w=[kvt])
                    op("vector", lambda v, W=W: v.tensor_tensor(
                        c3(kvt, STW)[:, :, 0:W], c3(kvt, STW)[:, :, 0:W],
                        c3(y1t, STW)[:, :, 0:W], OP.add), r=[y1t], w=[kvt])

                def frrest(si):
                    c0, W = subs[si]
                    ln_na(kvt, n2t, W, STW, sq, None, stb, pstat,
                          mm_stats=True)
                    op("vector", lambda v, W=W: v.tensor_tensor(
                        c3(xt_, STW)[:, :, 0:W], c3(xt_, STW)[:, :, 0:W],
                        c3(n2t, STW)[:, :, 0:W], OP.add), r=[n2t], w=[xt_])
                    op("gpsimd", lambda s, c0=c0, W=W: s.dma_start(
                        dchv(sp["x1"].t, c0, W), schv(xt_, W)),
                        r=[xt_], w=[sp["x1"]], kind="st")
                    ln_na(xt_, n2t, W, STW, sq, None, stb, pstat,
                          mm_stats=True)
                    op("scalar", lambda s, W=W: s.activation(
                        c3(q1, STW)[:, :, 0:W], c3(n2t, STW)[:, :, 0:W], AF.Abs),
                        r=[n2t], w=[q1])
                    op("vector", lambda v, W=W: v.tensor_tensor(
                        c3(sq, STW)[:, 0:4, 0:W], c3(q1, STW)[:, 0:8:2, 0:W],
                        c3(q1, STW)[:, 1:8:2, 0:W], OP.max), r=[q1], w=[sq])
                    op("vector", lambda v, W=W: v.tensor_tensor(
                        c3(sq, STW)[:, 0:4:2, 0:W], c3(sq, STW)[:, 0:4:2, 0:W],
                        c3(sq, STW)[:, 1:4:2, 0:W], OP.max), w=[sq])
                    op("vector", lambda v, W=W: v.tensor_tensor(
                        sq[:, 0:W], sq[:, 0:W], sq[:, 2 * STW:2 * STW + W], OP.max),
                        w=[sq])
                    nK = (W + P - 1) // P
                    for k in range(nK):
                        wk_ = min(P, W - k * P)
                        op("tensor", lambda t, k=k, wk_=wk_: t.transpose(
                            pstat[0:wk_, 0:P], sq[:, k * P:k * P + wk_], idn.t[:]),
                            r=[sq, idn], w=[pstat])
                        op("vector", lambda v, k=k, wk_=wk_: v.reduce_max(
                            qcb.t[0:wk_, k:k + 1], pstat[0:wk_, 0:P], axis=AX),
                            r=[pstat], w=[qcb])
                    op("tensor", [(lambda t, k=k, wk_=min(P, W - k * P): t.matmul(
                        pstat[0:1, 512 + k * P:512 + k * P + wk_],
                        qcb[0:wk_, k:k + 1], idn[0:wk_, 0:wk_],
                        start=True, stop=True)) for k in range(nK)],
                        r=[qcb, idn], w=[pstat])
                    op("vector", lambda v, W=W: v.tensor_scalar_max(
                        rrow[0:1, 0:W], pstat[0:1, 512:512 + W], 1e-5),
                        r=[pstat], w=[rrow])
                    op("vector", lambda v, W=W: v.reciprocal(
                        rrow[0:1, 256:256 + W], rrow[0:1, 0:W]), w=[rrow])
                    op("vector", lambda v, c0=c0, W=W: v.tensor_scalar_mul(
                        r1row.t[0:1, c0:c0 + W], rrow[0:1, 256:256 + W], 127.0),
                        r=[rrow], w=[r1row])
                    op("tensor", lambda t, c0=c0, W=W: t.matmul(
                        pstat[:, 0:W], onr[0:1, 0:P], r1row[0:1, c0:c0 + W],
                        start=True, stop=True), r=[onr, r1row], w=[pstat])
                    op("vector", lambda v, W=W: v.tensor_tensor(
                        c3(q0, STW)[:, :, 0:W], c3(n2t, STW)[:, :, 0:W],
                        bc3(pstat[:, 0:W], CK, W), OP.mult),
                        r=[n2t, pstat], w=[q0])
                    op("vector", lambda v, W=W: v.tensor_copy(
                        c3(qi, STW)[:, :, 0:W], c3(q0, STW)[:, :, 0:W]),
                        r=[q0], w=[qi])
                    op("vector", lambda v, W=W: v.tensor_copy(
                        c3(q1, STW)[:, :, 0:W], c3(qi, STW)[:, :, 0:W]),
                        r=[qi], w=[q1])
                    op("vector", lambda v, W=W: v.tensor_tensor(
                        c3(sq, STW)[:, :, 0:W], c3(q0, STW)[:, :, 0:W],
                        c3(q1, STW)[:, :, 0:W], OP.subtract), r=[q0, q1], w=[sq])
                    op("vector", lambda v, W=W: v.scalar_tensor_tensor(
                        c3(q0, STW)[:, :, 0:W], c3(sq, STW)[:, :, 0:W], 0.5,
                        c3(q1, STW)[:, :, 0:W], OP.is_gt, OP.add),
                        r=[sq, q1], w=[q0])
                    op("vector", lambda v, W=W: v.tensor_scalar(
                        c3(q1, STW)[:, :, 0:W], c3(sq, STW)[:, :, 0:W],
                        -0.5, -1.0, OP.is_lt, OP.mult), r=[sq], w=[q1])
                    op("vector", lambda v, c0=c0, W=W: v.tensor_tensor(
                        c3(n2q, RNG)[:, :, c0:c0 + W], c3(q0, STW)[:, :, 0:W],
                        c3(q1, STW)[:, :, 0:W], OP.add), r=[q0, q1], w=[n2q])

                NS = len(subs)
                frloads_a(0)
                frloads_b(0)
                frproj(0)
                for si in range(NS):
                    fry2(si)
                    if si + 1 < NS:
                        frloads_a(si + 1)
                        frproj(si + 1)
                    frrest(si)
                    if si + 1 < NS:
                        frloads_b(si + 1)
            pb.barrier()

            with ExitStack() as sh:
                w1t = sbt(sh, "H_w1t", (P, CK * DF), BF16)
                tloads([(w1t, cc * DF, DF, g_w1q, 0, DF, cc * P, (cc + 1) * P)
                        for cc in range(CK)], [g_w1q], [w1t])
                pmH = [pst(sh, "pmH0", 512), pst(sh, "pmH1", 512)]
                rht = sbt(sh, "H_rh", (P, RNG))
                for c0_, W_ in tsubs(512, 0, RNG):
                    op("tensor", lambda t, c0_=c0_, W_=W_: t.matmul(
                        pmH[0][:, 0:W_], onr[0:1, 0:P], r1row[0:1, c0_:c0_ + W_],
                        start=True, stop=True), r=[onr, r1row], w=[pmH[0]])
                    op("vector", lambda v, c0_=c0_, W_=W_: v.reciprocal(
                        rht.t[:, c0_:c0_ + W_], pmH[0][:, 0:W_]),
                        r=[pmH[0]], w=[rht])
                op("vector", lambda v: v.tensor_scalar_mul(rht.t[:], rht.t[:],
                                                           swc[:, 2:3]),
                   r=[swc], w=[rht])
                hbuf = [sbt(sh, "H_h0", (P, RNG)), sbt(sh, "H_h1", (P, RNG))]
                cbuf = [sbt(sh, "H_c0", (P, NW)), sbt(sh, "H_c1", (P, NW))]
                hsb = [sbt(sh, "H_s0", (P, NW)), sbt(sh, "H_s1", (P, NW))]
                mrun = sbt(sh, "H_mr", (P, NW))
                op("vector", lambda v: v.memset(mrun.t[:], 0.0), w=[mrun])
                for f in range(FCK):
                    hb_, cb_, sb_ = hbuf[f % 2], cbuf[f % 2], hsb[f % 2]
                    for ti, (c0, W) in enumerate(tsubs(512, 0, RNG)):
                        ps = pmH[ti % 2]
                        op("tensor", [(lambda t, f=f, kch=kch, c0=c0, W=W, ps=ps:
                                       t.matmul(ps[:, 0:W],
                                                w1t[:, kch * DF + f * P:
                                                    kch * DF + (f + 1) * P],
                                                n2q[:, kch * RNG + c0:
                                                    kch * RNG + c0 + W],
                                                start=(kch == 0),
                                                stop=(kch == CK - 1)))
                                      for kch in range(CK)],
                           r=[w1t, n2q], w=[ps])
                        op("vector", lambda v, c0=c0, W=W, ps=ps, hb_=hb_:
                           v.tensor_tensor(hb_.t[:, c0:c0 + W], ps[:, 0:W],
                                           rht[:, c0:c0 + W], OP.mult),
                           r=[ps, rht], w=[hb_])
                    op("vector", [
                        lambda v, hb_=hb_: v.tensor_scalar_mul(
                            hb_.t[:, HB - 1:HB], hb_.t[:, HB - 1:HB], met[:, 0:1]),
                        lambda v, hb_=hb_: v.tensor_scalar_mul(
                            hb_.t[:, OWN1:OWN1 + 1], hb_.t[:, OWN1:OWN1 + 1],
                            met[:, 1:2])], r=[met], w=[hb_])
                    op("vector", lambda v, f=f, hb_=hb_, cb_=cb_: v.tensor_scalar(
                        cb_.t[:], hb_.t[:, HB:HB + NW],
                        prm[:, 56 + 3 * f + 1:56 + 3 * f + 2],
                        prm[:, 152 + f:152 + f + 1], OP.mult, OP.add),
                        r=[hb_, prm], w=[cb_])
                    op("vector", lambda v, f=f, hb_=hb_, cb_=cb_:
                       v.scalar_tensor_tensor(
                           cb_.t[:], hb_.t[:, HB - 1:HB - 1 + NW],
                           prm[:, 56 + 3 * f:56 + 3 * f + 1], cb_.t[:],
                           OP.mult, OP.add), r=[hb_, prm], w=[cb_])
                    op("vector", lambda v, f=f, hb_=hb_, cb_=cb_:
                       v.scalar_tensor_tensor(
                           cb_.t[:], hb_.t[:, HB + 1:HB + 1 + NW],
                           prm[:, 56 + 3 * f + 2:56 + 3 * f + 3], cb_.t[:],
                           OP.mult, OP.add), r=[hb_, prm], w=[cb_])
                    op("scalar", lambda s, cb_=cb_, sb_=sb_: s.activation(
                        sb_.t[:], cb_.t[:], AF.Silu), r=[cb_], w=[sb_])
                    op("scalar", lambda s, cb_=cb_, sb_=sb_: s.activation(
                        cb_.t[:], sb_.t[:], AF.Abs), r=[sb_], w=[cb_])
                    op("vector", lambda v, cb_=cb_: v.tensor_tensor(
                        mrun.t[:], mrun.t[:], cb_.t[:], OP.max), r=[cb_], w=[mrun])
                    op("gpsimd", lambda s, f=f, sb_=sb_: s.dma_start(
                        sp_hs.t[f * P:(f + 1) * P, :], sb_.t[:]),
                        r=[sb_], w=[sp_hs], kind="st")
                for k in range(NW // P):
                    pp = pmH[k % 2]
                    op("tensor", lambda t, k=k, pp=pp: t.transpose(
                        pp[0:P, 0:P], mrun[:, k * P:k * P + P], idn.t[:]),
                        r=[mrun, idn], w=[pp])
                    op("vector", lambda v, k=k, pp=pp: v.reduce_max(
                        fqc.t[:, k:k + 1], pp[0:P, 0:P], axis=AX), r=[pp], w=[fqc])
        _ph_frh()
        pb.barrier()

        def _ph_f():
         with ExitStack() as sF:
            w2t = sbt(sF, "F_w2t", (P, FCK * D), BF16)
            tloads([(w2t, f * D, D, g_w2q, 0, D, f * P, (f + 1) * P)
                    for f in range(FCK)], [g_w2q], [w2t])
            psL = pst(sF, "F_psL", 1024)
            psR = pst(sF, "F_psR", 512)
            pkF = pst(sF, "F_pkF", 2048)
            rrow = sbt(sF, "F_rr", (1, 512))
            rbc = [sbt(sF, "F_rb0", (P, 512)), sbt(sF, "F_rb1", (P, 512))]
            hst = sbt(sF, "F_hs", (P, FCK * STF))
            hq = sbt(sF, "F_hq", (P, FCK * STF), BF16)
            x1t = sbt(sF, "F_x1", (P, CK * STF))
            qa = sbt(sF, "F_qa", (P, CK * STF))
            qb = sbt(sF, "F_qb", (P, CK * STF))
            sq = sbt(sF, "F_sq", (P, CK * STF))
            qa2 = sbt(sF, "F_qa2", (P, CK * STF))
            qb2 = sbt(sF, "F_qb2", (P, CK * STF))
            stb = sbt(sF, "F_stb", (P, 5 * STF))
            tiles = tsubs(STF, OWN0, OWN1)

            def emit_loads(ti):
                c0, W = tiles[ti]
                ow = c0 - OWN0
                op("sync", lambda s, ow=ow, W=W: s.dma_start(
                    hst.t[:].rearrange("p (f t) -> p f t", f=FCK)[:, :, 0:W],
                    dchv(sp_hs.t, ow, W, nck=FCK)),
                    r=[sp_hs], w=[hst], kind="ld")

            def emit_x1load(ti):
                c0, W = tiles[ti]
                op("sync", lambda s, c0=c0, W=W: s.dma_start(
                    schv(x1t, W), dchv(sp["x1"].t, c0, W)),
                    r=[sp["x1"]], w=[x1t], kind="ld")

            def emit_rows(ti):
                pi = ti % 2
                W = tiles[ti][1]
                for j in range((W + P - 1) // P):
                    op("tensor", lambda t, j=j, ti=ti: t.matmul(
                        psR[0:1, 0:P], fqc[:, 2 * ti + j:2 * ti + j + 1],
                        idn.t[:], start=True, stop=True), r=[fqc, idn], w=[psR])
                    op("vector", lambda v, j=j: v.tensor_copy(
                        rrow[0:1, j * P:(j + 1) * P], psR[0:1, 0:P]),
                        r=[psR], w=[rrow])
                op("vector", lambda v, W=W: v.tensor_scalar_max(
                    rrow[0:1, 0:W], rrow[0:1, 0:W], 1e-5), w=[rrow])
                op("vector", lambda v, W=W: v.reciprocal(
                    rrow[0:1, 256:256 + W], rrow[0:1, 0:W]), w=[rrow])
                op("vector", lambda v, W=W: v.tensor_scalar_mul(
                    rrow[0:1, 0:W], rrow[0:1, 256:256 + W], 127.0), w=[rrow])
                op("vector", lambda v, W=W: v.reciprocal(
                    rrow[0:1, 256:256 + W], rrow[0:1, 0:W]), w=[rrow])
                op("vector", lambda v, W=W: v.tensor_scalar_mul(
                    rrow[0:1, 256:256 + W], rrow[0:1, 256:256 + W],
                    swc[0:1, 3:4]), r=[swc], w=[rrow])
                op("tensor", lambda t, W=W: t.matmul(
                    psR[:, 0:W], onr[0:1, 0:P], rrow[0:1, 0:W],
                    start=True, stop=True), r=[onr, rrow], w=[psR])
                op("vector", lambda v, pi=pi, W=W: v.tensor_copy(
                    rbc[pi].t[:, 0:W], psR[:, 0:W]), r=[psR], w=[rbc[pi]])
                op("tensor", lambda t, W=W: t.matmul(
                    psR[:, 256:256 + W], onr[0:1, 0:P], rrow[0:1, 256:256 + W],
                    start=True, stop=True), r=[onr, rrow], w=[psR])
                op("vector", lambda v, pi=pi, W=W: v.tensor_copy(
                    rbc[pi].t[:, 256:256 + W], psR[:, 256:256 + W]),
                    r=[psR], w=[rbc[pi]])

            def emit_quant(ti):
                pi = ti % 2
                W = tiles[ti][1]
                sqi3 = sq.t[:].bitcast(I32).rearrange("p (ck t) -> p ck t", t=STF)
                for grp in range(4):
                    go = grp * 2
                    op("vector", lambda v, grp=grp, W=W: v.tensor_tensor(
                        c3(qa, STF)[:, :, 0:W],
                        c3(hst, STF, n=8, o=grp * 8)[:, :, 0:W],
                        bc3(rbc[pi][:, 0:W], CK, W), OP.mult),
                        r=[hst, rbc[pi]], w=[qa])
                    op("vector", lambda v, W=W, sqi3=sqi3: v.tensor_copy(
                        sqi3[:, :, 0:W], c3(qa, STF)[:, :, 0:W]), r=[qa], w=[sq])
                    op("vector", lambda v, W=W, sqi3=sqi3: v.tensor_copy(
                        c3(qb, STF)[:, :, 0:W], sqi3[:, :, 0:W]), r=[sq], w=[qb])
                    op("vector", lambda v, W=W: v.tensor_tensor(
                        c3(sq, STF)[:, :, 0:W], c3(qa, STF)[:, :, 0:W],
                        c3(qb, STF)[:, :, 0:W], OP.subtract), r=[qa, qb], w=[sq])
                    op("vector", lambda v, W=W: v.scalar_tensor_tensor(
                        c3(qa, STF)[:, :, 0:W], c3(sq, STF)[:, :, 0:W], 0.5,
                        c3(qb, STF)[:, :, 0:W], OP.is_gt, OP.add),
                        r=[sq, qb], w=[qa])
                    op("vector", lambda v, W=W: v.tensor_scalar(
                        c3(qb, STF)[:, :, 0:W], c3(sq, STF)[:, :, 0:W],
                        -0.5, -1.0, OP.is_lt, OP.mult), r=[sq], w=[qb])
                    op("vector", lambda v, grp=grp, W=W: v.tensor_tensor(
                        c3(hq, STF, n=8, o=grp * 8)[:, :, 0:W],
                        c3(qa, STF)[:, :, 0:W], c3(qb, STF)[:, :, 0:W],
                        OP.add), r=[qa, qb], w=[hq])

            def emit_mm(ti):
                pi = ti % 2
                W = tiles[ti][1]
                for grp in range(2):
                    fns = []
                    for f in range(FCK):
                        for ci in range(4):
                            c = grp * 4 + ci
                            lh = w2t[:, f * D + c * P:f * D + (c + 1) * P]
                            rh = hq[:, f * STF:f * STF + W]
                            fns.append(lambda t, lh=lh, rh=rh, st_=(f == 0),
                                       sp_=(f == FCK - 1), o=ci * 512, W=W:
                                       t.matmul(pkF[:, o:o + W], lh, rh,
                                                start=st_, stop=sp_))
                    op("tensor", fns, r=[w2t, hq], w=[pkF])
                    op("vector", lambda v, grp=grp, pi=pi, W=W: v.tensor_tensor(
                        c3(qa2, STF, n=4, o=grp * 4)[:, :, 0:W],
                        pkF.t[:].rearrange("p (ck t) -> p ck t", t=512)[:, :, 0:W],
                        bc3(rbc[pi][:, 256:256 + W], 4, W), OP.mult),
                       r=[pkF, rbc[pi]], w=[qa2])

            def emit_tail(ti):
                c0, W = tiles[ti]
                ow = c0 - OWN0
                ln_na(qa2, qb2, W, STF, qb2, None, stb, psL, mm_stats=True)
                emit_x1load(ti)
                op("vector", lambda v, W=W: v.tensor_tensor(
                    c3(qb2, STF)[:, :, 0:W], c3(qb2, STF)[:, :, 0:W],
                    c3(x1t, STF)[:, :, 0:W], OP.add), r=[x1t], w=[qb2])
                op("gpsimd", lambda s, ow=ow, W=W: s.dma_start(
                    dchv(outT.t, ow, W), schv(qb2, W)),
                    r=[qb2], w=[outT], kind="st")

            NT = len(tiles)
            emit_loads(0)
            emit_rows(0)
            emit_quant(0)
            for ti in range(NT):
                emit_mm(ti)
                if ti + 1 < NT:
                    emit_loads(ti + 1)
                    emit_rows(ti + 1)
                    emit_quant(ti + 1)
                emit_tail(ti)
        _ph_f()

        pb.fence("sync", "st")
        pb.fence("sync", "ld")
        pb.recs["sync"].append(["inst", lambda s: s.nop(), None, None, False, False])

        E = dict(sync=nc.sync, scalar=nc.scalar, vector=nc.vector,
                 tensor=nc.tensor, gpsimd=nc.gpsimd)
        blk = ctx.enter_context(nc.Block())
        pb.run(blk, E, sem_h)
    return nc


def host_inputs(inputs, n_own):
    x = np.asarray(inputs["L0_attn_x"], dtype=np.float32)
    B, T, _ = x.shape
    RNG = HB + n_own + HF
    NX = RNG + 1

    def col(vec):
        return np.ascontiguousarray(np.asarray(vec, np.float32).reshape(CK, P).T)

    pcols = np.zeros((P, 184), np.float32)
    pcols[:, 0:8] = col(inputs["L0_attn_tm_k"])
    pcols[:, 8:16] = col(inputs["L0_attn_tm_v"])
    pcols[:, 16:24] = col(inputs["L0_attn_tm_r"])
    pcols[:, 24:32] = col(inputs["L0_attn_fm_k"])
    pcols[:, 32:40] = col(inputs["L0_attn_fm_r"])
    pcols[:, 40:48] = col(inputs["L0_attn_time_decay"])
    pcols[:, 48:56] = col(inputs["L0_attn_time_first"])
    cw = np.asarray(inputs["L0_ffn_conv_w"], np.float32)[:, 0, :]
    pcols[:, 56:152] = cw.reshape(FCK, P, 3).transpose(1, 0, 2).reshape(P, 96)
    pcols[:, 152:184] = np.asarray(inputs["L0_ffn_conv_b"], np.float32).reshape(FCK, P).T

    import ml_dtypes
    ident = np.eye(P, dtype=np.float32)
    identb = np.eye(P, dtype=ml_dtypes.bfloat16)
    td = np.asarray(inputs["L0_attn_time_decay"], np.float32)
    td8 = np.ascontiguousarray(np.tile(td[None, :], (8, 1)))
    Wm = {nm: np.ascontiguousarray(np.asarray(inputs[key], np.float32))
          for nm, key in (("wk", "L0_attn_Wk"), ("wv", "L0_attn_Wv"),
                          ("wr", "L0_attn_Wr"), ("wo", "L0_attn_Wo"),
                          ("fr", "L0_attn_Fr"), ("fk", "L0_attn_Fk"),
                          ("fv", "L0_attn_Fv"), ("w1", "L0_ffn_w1"),
                          ("w2", "L0_ffn_w2"))}

    in_maps = []
    for c in range(NCORE):
        b, q = divmod(c, 4)
        t0 = q * n_own
        lo = t0 - 9
        xs = np.zeros((NX, D), np.float32)
        s0, s1_ = max(lo, 0), min(lo + NX, T)
        xs[s0 - lo:s1_ - lo] = x[b, s0:s1_]
        meta = np.zeros((P, 8), np.float32)
        meta[:, 0] = 0.0 if q == 0 else 1.0
        meta[:, 1] = 0.0 if q == 3 else 1.0
        meta[:, 2] = 1.0 - meta[0, 0]
        meta[:, 6] = EPS
        meta[:, 7] = float(n_own - 8)
        for j in range(8):
            jb = j // 4
            if jb == b and j < c and (c - j) <= 3:
                meta[j, 3] = float(n_own * (c - 1 - j))
                meta[j, 4] = 1.0
            if jb == b and j == c - 1:
                meta[j, 5] = 1.0
        m = dict(xT=np.ascontiguousarray(xs.T), params_pm=pcols, meta=meta,
                 ident=ident, identb=identb,
                 onescol=np.ones((P, 1), np.float32),
                 onesrow=np.ones((1, P), np.float32),
                 ones2d=np.ones((P, P), np.float32), td8=td8)
        for nm, arr in Wm.items():
            m["w_" + nm] = arr
        in_maps.append(m)
    return in_maps


def kernel(**inputs):
    x = np.asarray(inputs["L0_attn_x"])
    B, T, _ = x.shape
    n_own = B * T // NCORE
    nc = bass.Bass("TRN2", target_bir_lowering=False)
    build(nc, n_own)
    in_maps = host_inputs(inputs, n_own)
    res = run_bass_kernel_spmd(nc, in_maps, core_ids=list(range(NCORE)))
    out = np.empty((B, T, D), np.float32)
    for c in range(NCORE):
        b, q = divmod(c, 4)
        out[b, q * n_own:(q + 1) * n_own, :] = np.asarray(res.results[c]["outT"]).T
    return out


# version 31
# speedup vs baseline: 1.0458x; 1.0216x over previous
import numpy as np
from contextlib import ExitStack

import concourse.bass as bass
import concourse.mybir as mybir
from concourse.bass_utils import run_bass_kernel_spmd

F32 = mybir.dt.float32
BF16 = mybir.dt.bfloat16
I32 = mybir.dt.int32
AF = mybir.ActivationFunctionType
OP = mybir.AluOpType
AX = mybir.AxisListType.X

P = 128
CK = 8
D = 1024
DF = 4096
FCK = 32
EPS = 1e-5
NCORE = 8
HB = 8
HF = 8

ENGS = ("sync", "scalar", "vector", "tensor", "gpsimd")
_ESEM = {"scalar": "s", "vector": "v", "tensor": "t", "gpsimd": "g"}
_SEMS = ("ld", "st", "lw", "l2", "v", "s", "t", "g")


class Buf:
    __slots__ = ("t", "w", "rs")

    def __init__(self, t=None):
        self.t = t
        self.w = None
        self.rs = {}

    def __getitem__(self, sl):
        return self.t[sl]


class PB:
    def __init__(self):
        self.recs = {e: [] for e in ENGS}
        self.cnt = {n: 0 for n in _SEMS}
        self.marks = {e: {} for e in ENGS}
        self.since_drain = {e: 0 for e in ENGS}
        self.lastrec = {e: None for e in ENGS}
        self.bardeps = None
        self.barseen = set()

    def dep(self, name):
        return (name, self.cnt[name])

    def _want(self, eng, d):
        if d is None:
            return
        n, v = d
        if v <= 0:
            return
        if self.marks[eng].get(n, 0) >= v:
            return
        self.marks[eng][n] = v
        self.recs[eng].append(("wait", n, v))

    def op(self, eng, fns, r=(), w=(), deps=(), kind="c"):
        if not isinstance(fns, (list, tuple)):
            fns = [fns]
        dd = list(deps)
        if self.bardeps is not None and eng not in self.barseen:
            self.barseen.add(eng)
            dd.extend(self.bardeps)
        dma = kind != "c"
        sem = kind if dma else _ESEM[eng]
        for b in r:
            dd.append(b.w)
        for b in w:
            if b.rs:
                dd.extend((k, val) for k, val in b.rs.items())
            else:
                dd.append(b.w)
        for d in dd:
            self._want(eng, d)
        last = None
        for fn in fns:
            if dma:
                self.cnt[sem] += 16
                last = ["inst", fn, sem, self.cnt[sem], True, True]
            else:
                last = ["inst", fn, sem, None, False, False]
            self.recs[eng].append(last)
            self.since_drain[eng] += 1
        if not dma:
            self.cnt[sem] += 1
            last[3] = self.cnt[sem]
            last[4] = True
        self.lastrec[eng] = last
        t = (sem, self.cnt[sem])
        for b in r:
            b.rs[sem] = max(b.rs.get(sem, 0), t[1])
        for b in w:
            b.w = t
            b.rs = {}
        return t

    def tagnow(self, eng):
        sem = _ESEM[eng]
        rec = self.lastrec[eng]
        if rec is not None and not rec[4]:
            self.cnt[sem] += 1
            rec[3] = self.cnt[sem]
            rec[4] = True
        return (sem, self.cnt[sem])

    def barrier(self):
        for e in ("scalar", "vector", "tensor", "gpsimd"):
            if self.lastrec[e] is not None:
                self.tagnow(e)
        self.bardeps = [(n, self.cnt[n]) for n in _SEMS if self.cnt[n] > 0]
        self.barseen = set()

    def fence(self, eng, sem):
        v = self.cnt[sem]
        if v <= 0:
            return
        self.recs[eng].append(("wait", sem, v))
        self.marks[eng][sem] = max(self.marks[eng].get(sem, 0), v)

    def run(self, blk, engines, sem_h):
        maxw = {}
        waited = {n: set() for n in ("ld", "st", "lw", "l2")}
        for e in ENGS:
            for rec in self.recs[e]:
                if rec[0] == "wait":
                    maxw[rec[1]] = max(maxw.get(rec[1], 0), rec[2])
                    if rec[1] in waited:
                        waited[rec[1]].add(rec[2])
        for e in ENGS:
            recs = self.recs[e]
            if not any(rec[0] == "inst" and rec[5] for rec in recs):
                continue
            out = []
            fenced = {}
            for rec in recs:
                if rec[0] == "inst" and rec[5]:
                    s_, v_ = rec[2], rec[3]
                    prev = v_ - 16
                    if (prev > fenced.get(s_, 0) and prev in waited[s_]):
                        out.append(("wait", s_, prev))
                        fenced[s_] = prev
                elif rec[0] == "wait" and rec[1] in waited:
                    fenced[rec[1]] = max(fenced.get(rec[1], 0), rec[2])
                out.append(rec)
            self.recs[e] = out
        for ename in ENGS:
            recs = self.recs[ename]
            if not recs:
                continue

            def body(E, recs=recs):
                for rec in recs:
                    if rec[0] == "wait":
                        E.wait_ge(sem_h[rec[1]], rec[2])
                    elif rec[0] == "drain":
                        E.drain()
                    else:
                        _, fn, sem, val, has, dma = rec
                        inst = fn(E)
                        if has and (dma or val <= maxw.get(sem, 0)):
                            inst.then_inc(sem_h[sem], 16 if dma else 1)
            getattr(blk, ename)(body)


def build(nc, n_own):
    RNG = HB + n_own + HF
    WA = (RNG - 16) // 8
    assert 8 * WA + 16 == RNG and WA % 8 == 0 and WA >= 16
    subs = [(s * WA, WA) for s in range(8)] + [(8 * WA, 16)]
    NX = RNG + 1
    OWN0, OWN1 = HB, HB + n_own
    NW = n_own
    STA = WA + 1
    STW = WA
    STS = WA + 1
    STF = 256
    assert NW % STF == 0 or NW < STF

    def tsubs(width, lo, hi):
        out, c = [], lo
        while c < hi:
            w = min(width, hi - c)
            out.append((c, w))
            c += w
        return out

    def ein(name, shape, dt=F32):
        return nc.dram_tensor(name, list(shape), dt, kind="ExternalInput").ap()

    def idr(name, shape, dt):
        return Buf(nc.dram_tensor(name, list(shape), dt, kind="Internal").ap())

    xT = ein("xT", (D, NX))
    prm_in = ein("params_pm", (P, 184))
    meta_in = ein("meta", (P, 8))
    ident_in = ein("ident", (P, P))
    identb_in = ein("identb", (P, P), BF16)
    onescol_in = ein("onescol", (P, 1))
    onesrow_in = ein("onesrow", (1, P))
    ones2d_in = ein("ones2d", (P, P))
    td8_in = ein("td8", (8, D))
    wfull = {w: ein("w_" + w, (DF if w in ("fk", "w1") else D,
                               DF if w in ("fv", "w2") else D))
             for w in ("wk", "wv", "wr", "wo", "fr", "fk", "fv", "w1", "w2")}
    outT = Buf(nc.dram_tensor("outT", [D, n_own], F32, kind="ExternalOutput").ap())

    gsc = {}
    for w in ("wk", "wv", "wr", "wo", "fr", "fk", "fv"):
        r = DF if w == "fk" else D
        c = DF if w == "fv" else D
        for hl in ("h", "l"):
            gsc[w + hl] = idr(f"g_{w}{hl}", (r, c), BF16)
    g_w1q = idr("g_w1q", (DF, D), BF16)
    g_w2q = idr("g_w2q", (D, DF), BF16)
    ci_r1 = idr("ci_r1", (4, D), F32)
    g_r1 = idr("g_r1", (4 * NCORE, D), F32)
    ci_r2 = idr("ci_r2", (2, D), F32)
    g_r2 = idr("g_r2", (2 * NCORE, D), F32)
    sp = {n: idr("sp_" + n, (D, RNG), F32) for n in ("n1", "wk", "z", "y1", "kv", "x1")}
    spb = {n: idr("sp_" + n, (D, RNG), BF16)
           for n in ("xrh", "xrl", "gkh", "gkl", "grh", "grl")}
    sp_kfh = idr("sp_kfh", (DF, RNG), BF16)
    sp_kfl = idr("sp_kfl", (DF, RNG), BF16)
    sp_hs = idr("sp_hs", (DF, NW), F32)

    def dchv(ap_, c0, w, nck=CK):
        return ap_.rearrange("(ck p) t -> p ck t", p=P)[:, :, c0:c0 + w]

    def schv(buf, w, nck=CK):
        return buf.t[:].rearrange("p (ck t) -> p ck t", ck=nck)[:, :, 0:w]

    def c3(buf, st_, n=CK, o=0):
        v = buf.t[:].rearrange("p (ck t) -> p ck t", t=st_)
        return v if (n is None and o == 0) else v[:, o:o + n, :]

    def bc3(row_ap, n, w_):
        return row_ap.unsqueeze(1).broadcast_to([P, n, w_])

    def col3(buf2d, n=CK):
        return buf2d.t[:].unsqueeze(2)

    pb = PB()
    op, dep = pb.op, pb.dep
    RG = list(range(NCORE))

    with ExitStack() as ctx:
        sem_h = {n: ctx.enter_context(nc.semaphore("sem_" + n)) for n in _SEMS}

        def sbt(st, name, shape, dt=F32):
            return Buf(st.enter_context(nc.sbuf_tensor(name, list(shape), dt)))

        def pst(st, name, cols):
            return Buf(st.enter_context(nc.psum_tensor(name, [P, cols], F32)))

        prm = sbt(ctx, "prm", (P, 184))
        met = sbt(ctx, "met", (P, 8))
        idn = sbt(ctx, "idn", (P, P))
        idb = sbt(ctx, "idb", (P, P), BF16)
        onc = sbt(ctx, "onc", (P, 1))
        onr = sbt(ctx, "onr", (1, P))
        on2 = sbt(ctx, "on2", (P, P))
        pm8 = sbt(ctx, "pm8", (P, 8))
        d_pm = sbt(ctx, "d_pm", (P, 8))
        w_pm = sbt(ctx, "w_pm", (P, 8))
        eu_pm = sbt(ctx, "eu_pm", (P, 8))
        gbuf = sbt(ctx, "gbuf", (P, 32))
        iA = sbt(ctx, "iA", (P, 8))
        iB = sbt(ctx, "iB", (P, 8))
        stA = sbt(ctx, "stA", (P, 8))
        stB = sbt(ctx, "stB", (P, 8))
        swc = sbt(ctx, "swc", (P, 4))
        fqc = sbt(ctx, "fqc", (P, 16))

        nc.const_aps.register(F32, EPS, met[:, 6:7])
        nc.const_aps.register(F32, float(n_own - 8), met[:, 7:8])
        op("sync", [(lambda s, d=dst, sr=src: s.dma_start(d.t[:], sr))
                    for dst, src in ((prm, prm_in), (met, meta_in), (idn, ident_in),
                                     (idb, identb_in), (onc, onescol_in),
                                     (onr, onesrow_in), (on2, ones2d_in))],
           w=[prm, met, idn, idb, onc, onr, on2], kind="ld")
        op("vector", [(lambda v, b=b: v.memset(b.t[:], 0.0))
                      for b in (stA, stB, gbuf, iA, iB)],
           w=[stA, stB, gbuf, iA, iB])
        op("scalar", lambda s: s.activation(pm8.t[:], prm[:, 40:48], AF.Exp),
           r=[prm], w=[pm8])
        op("vector", lambda v: v.tensor_scalar_mul(w_pm.t[:], pm8.t[:], -1.0),
           r=[pm8], w=[w_pm])
        op("scalar", lambda s: s.activation(d_pm.t[:], w_pm.t[:], AF.Exp),
           r=[w_pm], w=[d_pm])
        op("scalar", lambda s: s.activation(eu_pm.t[:], prm[:, 48:56], AF.Exp),
           r=[prm], w=[eu_pm])

        def build_dtile(dt_buf):
            op("vector", lambda v: v.memset(dt_buf.t[:], 0.0), w=[dt_buf])
            op("scalar", [(lambda s, ck=ck: s.activation(
                dt_buf[:, ck * WA:(ck + 1) * WA], dt_buf[:, ck * WA:(ck + 1) * WA],
                AF.Identity, bias=d_pm[:, ck:ck + 1], scale=0.0))
                for ck in range(CK)], r=[d_pm], w=[dt_buf])

        def ln_na(xbuf, outbuf, W, ST, sqb, trb, stb, pstat, pe="vector",
                  mm_stats=False):
            x3, s3 = c3(xbuf, ST), c3(sqb, ST)
            o3 = c3(outbuf, ST)
            op("scalar", lambda s: s.activation(
                s3[:, :, 0:W], x3[:, :, 0:W], AF.Square), r=[xbuf], w=[sqb])

            if mm_stats:
                fns = []
                for src, off in ((xbuf, 0), (sqb, 512)):
                    for ck in range(CK):
                        fns.append(lambda t, src=src, off=off, ck=ck: t.matmul(
                            pstat[:, off:off + W], on2.t[:],
                            src[:, ck * ST:ck * ST + W],
                            start=(ck == 0), stop=(ck == CK - 1)))
                op("tensor", fns, r=[on2, xbuf, sqb], w=[pstat])
            else:
                t3 = c3(trb, ST, n=4)

                def tree(srcb, src3, doff):
                    op(pe, lambda v: v.tensor_tensor(
                        t3[:, 0:4, 0:W], src3[:, 0:8:2, 0:W], src3[:, 1:8:2, 0:W],
                        OP.add), r=[srcb], w=[trb])
                    op(pe, lambda v: v.tensor_tensor(
                        t3[:, 0:4:2, 0:W], t3[:, 0:4:2, 0:W], t3[:, 1:4:2, 0:W],
                        OP.add), w=[trb])
                    op(pe, lambda v, doff=doff: v.tensor_tensor(
                        stb[:, doff * ST:doff * ST + W], trb[:, 0:W],
                        trb[:, 2 * ST:2 * ST + W], OP.add), r=[trb], w=[stb])
                tree(xbuf, x3, 0)
                tree(sqb, s3, 1)
                op("tensor", [
                    lambda t: t.matmul(pstat[:, 0:W], on2.t[:], stb[:, 0:W],
                                       start=True, stop=True),
                    lambda t: t.matmul(pstat[:, 512:512 + W], on2.t[:],
                                       stb[:, ST:ST + W], start=True, stop=True)],
                    r=[on2, stb], w=[pstat])
            op("vector", lambda v: v.tensor_scalar_mul(
                stb[:, 2 * ST:2 * ST + W], pstat[:, 0:W], 1.0 / D),
                r=[pstat], w=[stb])
            op("vector", lambda v: v.tensor_tensor(
                stb[:, 3 * ST:3 * ST + W], stb[:, 2 * ST:2 * ST + W],
                stb[:, 2 * ST:2 * ST + W], OP.mult), w=[stb])
            op("vector", lambda v: v.scalar_tensor_tensor(
                stb[:, 4 * ST:4 * ST + W], pstat[:, 512:512 + W], 1.0 / D,
                stb[:, 3 * ST:3 * ST + W], OP.mult, OP.subtract),
                r=[pstat], w=[stb])
            op("scalar", lambda s: s.activation(
                stb[:, 3 * ST:3 * ST + W], stb[:, 4 * ST:4 * ST + W], AF.Sqrt,
                bias=EPS), r=[met], w=[stb])
            op("vector", lambda v: v.reciprocal(
                stb[:, 4 * ST:4 * ST + W], stb[:, 3 * ST:3 * ST + W]), w=[stb])
            op(pe, lambda v: v.tensor_tensor(
                o3[:, :, 0:W], x3[:, :, 0:W],
                bc3(stb[:, 2 * ST:2 * ST + W], CK, W), OP.subtract),
                r=[xbuf, stb], w=[outbuf])
            op("vector", lambda v: v.tensor_tensor(
                o3[:, :, 0:W], o3[:, :, 0:W],
                bc3(stb[:, 4 * ST:4 * ST + W], CK, W), OP.mult),
                r=[stb], w=[outbuf])

        def proj3(psms, whb, wlb, rhb, rlb, W, RST, evac):
            for dch in range(CK):
                ps = psms[dch % 2]
                fns = []
                for kch in range(CK):
                    lh = whb[:, kch * D + dch * P:kch * D + (dch + 1) * P]
                    ll = wlb[:, kch * D + dch * P:kch * D + (dch + 1) * P]
                    rh = rhb[:, kch * RST:kch * RST + W]
                    rl = rlb[:, kch * RST:kch * RST + W]
                    st_ = (kch == 0)
                    sp_ = (kch == CK - 1)
                    fns.append(lambda t, lh=lh, rh=rh, st_=st_, ps=ps, W=W:
                               t.matmul(ps[:, 0:W], lh, rh, start=st_, stop=False))
                    fns.append(lambda t, lh=lh, rl=rl, ps=ps, W=W:
                               t.matmul(ps[:, 0:W], lh, rl, start=False, stop=False))
                    fns.append(lambda t, ll=ll, rh=rh, sp_=sp_, ps=ps, W=W:
                               t.matmul(ps[:, 0:W], ll, rh, start=False, stop=sp_))
                op("tensor", fns, r=[whb, wlb, rhb, rlb], w=[ps])
                evac(dch, ps)

        def split_hi_lo(srcb, dh, dl, W, ST):
            op("scalar", lambda s: s.copy(
                c3(dh, ST)[:, :, 0:W], c3(srcb, ST)[:, :, 0:W]),
                r=[srcb], w=[dh])
            op("vector", lambda v: v.tensor_tensor(
                c3(dl, ST)[:, :, 0:W], c3(srcb, ST)[:, :, 0:W],
                c3(dh, ST)[:, :, 0:W], OP.subtract), r=[srcb, dh], w=[dl])

        def tloads(items, srcs, bufs):
            fns = [(lambda s, b=b, co=co, w=w, src=src, r0=r0, r1=r1, ca=ca, cb=cb:
                    s.dma_start(b[:, co:co + w], src.t[r0:r1, ca:cb],
                                transpose=True))
                   for (b, co, w, src, r0, r1, ca, cb) in items]
            op("sync", fns, r=srcs, w=bufs, kind="lw")

        def _ph_p0():
         with ExitStack() as p0:
            wt = [sbt(p0, "p0_w%d" % i, (P, DF)) for i in range(8)]
            wh = [sbt(p0, "p0_h%d" % i, (P, DF), BF16) for i in range(4)]
            wl = [sbt(p0, "p0_l%d" % i, (P, DF), BF16) for i in range(4)]
            def ldgroup(w, c, it0, n):
                wts = [wt[(it0 + j) % 8] for j in range(n)]
                op("sync", [(lambda s, it=it0 + j, c=c, w=w, b_=wts[j]:
                             s.dma_start(b_[:, 0:c],
                                         wfull[w][it * P:(it + 1) * P, :]))
                            for j in range(n)], w=wts, kind="ld")
                return wts

            for w in ("wk", "wv", "wr", "wo", "fr", "fk", "fv"):
                rr = DF if w == "fk" else D
                c = DF if w == "fv" else D
                for it0 in range(0, rr // P, 4):
                    wts = ldgroup(w, c, it0, 4)
                    for j in range(4):
                        it = it0 + j
                        wtx = wts[j]
                        whb, wlb = wh[it % 4], wl[it % 4]
                        op("scalar", lambda s, c=c, wtx=wtx, whb=whb: s.copy(
                            whb[:, 0:c], wtx[:, 0:c]), r=[wtx], w=[whb])
                        op("vector", lambda v, c=c, wtx=wtx, whb=whb, wlb=wlb:
                           v.tensor_tensor(wlb[:, 0:c], wtx[:, 0:c],
                                           whb[:, 0:c], OP.subtract),
                           r=[wtx, whb], w=[wlb])
                        op("gpsimd",
                           [(lambda g, w=w, it=it, c=c, whb=whb: g.dma_start(
                               gsc[w + "h"].t[it * P:(it + 1) * P, :],
                               whb[:, 0:c])),
                            (lambda g, w=w, it=it, c=c, wlb=wlb: g.dma_start(
                                gsc[w + "l"].t[it * P:(it + 1) * P, :],
                                wlb[:, 0:c]))],
                           r=[whb, wlb], w=[gsc[w + "h"], gsc[w + "l"]],
                           kind="st")

        _ph_p0()
        pb.barrier()

        def _ph_a():
         with ExitStack() as pa:
            wkh = sbt(pa, "wkh", (P, CK * D), BF16)
            wkl = sbt(pa, "wkl", (P, CK * D), BF16)
            wvh = sbt(pa, "wvh", (P, CK * D), BF16)
            wvl = sbt(pa, "wvl", (P, CK * D), BF16)
            tloads([(buf, cc * D, D, g_, 0, D, cc * P, (cc + 1) * P)
                    for buf, g_ in ((wkh, gsc["wkh"]), (wkl, gsc["wkl"]),
                                    (wvh, gsc["wvh"]), (wvl, gsc["wvl"]))
                    for cc in range(CK)],
                   [gsc["wkh"], gsc["wkl"], gsc["wvh"], gsc["wvl"]],
                   [wkh, wkl, wvh, wvl])
            pstat = pst(pa, "A_pst", 1024)
            dtile = sbt(pa, "A_dt", (P, CK * WA))
            build_dtile(dtile)
            xa = sbt(pa, "A_x", (P, CK * STA))
            sq = sbt(pa, "A_sq", (P, CK * STA))
            n1 = sbt(pa, "A_n1", (P, CK * STA))
            hh = sbt(pa, "A_h", (P, CK * STA))
            stb = sbt(pa, "A_stb", (P, 5 * STA))
            wkt = sbt(pa, "A_wk", (P, CK * STW))
            vt = sbt(pa, "A_v", (P, CK * STW))
            zt = sbt(pa, "A_z", (P, CK * STW))
            scA = sbt(pa, "A_scA", (P, CK * STW))
            scB = sbt(pa, "A_scB", (P, CK * STW))
            mxf = sbt(pa, "A_mx", (P, CK * STW))
            bh = sbt(pa, "A_bh", (P, CK * STW), BF16)
            bl = sbt(pa, "A_bl", (P, CK * STW), BF16)
            bh2 = sbt(pa, "A_bh2", (P, CK * STW), BF16)
            bl2 = sbt(pa, "A_bl2", (P, CK * STW), BF16)
            pmA = [pst(pa, "pmA0", 512), pst(pa, "pmA1", 512)]

            for si, (c0, W) in enumerate(subs):
                W1 = W + 1
                op("sync", lambda s, c0=c0, W1=W1: s.dma_start(
                    schv(xa, W1), dchv(xT, c0, W1)), w=[xa], kind="ld")
                ln_na(xa, n1, W1, STA, sq, None, stb, pstat, mm_stats=True)
                op("gpsimd", lambda s, c0=c0, W=W: s.dma_start(
                    dchv(sp["n1"].t, c0, W),
                    n1.t[:].rearrange("p (ck t) -> p ck t", ck=CK)[:, :, 1:W + 1]),
                    r=[n1], w=[sp["n1"]], kind="st")
                op("vector", lambda v, W1=W1: v.tensor_tensor(
                    stb[:, 3 * STA:3 * STA + W1], stb[:, 4 * STA:4 * STA + W1],
                    stb[:, 4 * STA:4 * STA + W1], OP.mult), w=[stb])
                op("vector", lambda v, W1=W1: v.tensor_scalar(
                    stb[:, 3 * STA:3 * STA + W1], stb[:, 3 * STA:3 * STA + W1],
                    -EPS, 1.0 + EPS, OP.mult, OP.add), w=[stb])
                op("scalar", lambda s, W1=W1: s.activation(
                    stb[:, 3 * STA:3 * STA + W1], stb[:, 3 * STA:3 * STA + W1],
                    AF.Sqrt), w=[stb])
                op("vector", lambda v, W1=W1: v.reciprocal(
                    stb[:, 4 * STA:4 * STA + W1], stb[:, 3 * STA:3 * STA + W1]),
                    w=[stb])
                op("vector", lambda v, W1=W1: v.tensor_tensor(
                    c3(hh, STA)[:, :, 0:W1], c3(n1, STA)[:, :, 0:W1],
                    bc3(stb[:, 4 * STA:4 * STA + W1], CK, W1), OP.mult),
                    r=[n1, stb], w=[hh])
                if si == 0:
                    op("vector", lambda v: v.tensor_scalar_mul(
                        c3(hh, STA)[:, :, 1:1 + HB], c3(hh, STA)[:, :, 1:1 + HB],
                        met[:, 0:1]), r=[met], w=[hh])
                op("vector", lambda v, W1=W1: v.tensor_tensor(
                    c3(sq, STA)[:, :, 1:W1], c3(hh, STA)[:, :, 1:W1],
                    c3(hh, STA)[:, :, 0:W1 - 1], OP.subtract), r=[hh], w=[sq])

                def mix(tmoff, oh, ol):
                    tmb = prm[:, tmoff:tmoff + 8].unsqueeze(2).broadcast_to(
                        [P, CK, W])
                    op("vector", lambda v, tmb=tmb, W=W: v.tensor_tensor(
                        c3(mxf, STW)[:, :, 0:W], c3(sq, STA)[:, :, 1:1 + W],
                        tmb, OP.mult), r=[sq, prm], w=[mxf])
                    op("vector", lambda v, W=W: v.tensor_tensor(
                        c3(mxf, STW)[:, :, 0:W], c3(mxf, STW)[:, :, 0:W],
                        c3(hh, STA)[:, :, 0:W], OP.add), r=[hh], w=[mxf])
                    split_hi_lo(mxf, oh, ol, W, STW)

                mix(0, bh, bl)
                proj3(pmA, wkh, wkl, bh, bl, W, STW,
                      lambda dch, ps: op("scalar", lambda s, dch=dch, ps=ps, W=W:
                                         s.activation(wkt[:, dch * STW:dch * STW + W],
                                                      ps[:, 0:W], AF.Exp),
                                         r=[ps], w=[wkt]))
                mix(8, bh2, bl2)
                proj3(pmA, wvh, wvl, bh2, bl2, W, STW,
                      lambda dch, ps: op("scalar", lambda s, dch=dch, ps=ps, W=W:
                                         s.copy(vt[:, dch * STW:dch * STW + W],
                                                ps[:, 0:W]),
                                         r=[ps], w=[vt]))
                mix(16, bh, bl)
                op("gpsimd", lambda s, c0=c0, W=W: s.dma_start(
                    dchv(spb["xrh"].t, c0, W), schv(bh, W)),
                    r=[bh], w=[spb["xrh"]], kind="st")
                op("gpsimd", lambda s, c0=c0, W=W: s.dma_start(
                    dchv(spb["xrl"].t, c0, W), schv(bl, W)),
                    r=[bl], w=[spb["xrl"]], kind="st")
                op("vector", lambda v, W=W: v.tensor_tensor(
                    c3(zt, STW)[:, :, 0:W], c3(wkt, STW)[:, :, 0:W],
                    c3(vt, STW)[:, :, 0:W], OP.mult), r=[wkt, vt], w=[zt])
                op("gpsimd", lambda s, c0=c0, W=W: s.dma_start(
                    dchv(sp["wk"].t, c0, W), schv(wkt, W)),
                    r=[wkt], w=[sp["wk"]], kind="st")
                op("gpsimd", lambda s, c0=c0, W=W: s.dma_start(
                    dchv(sp["z"].t, c0, W), schv(zt, W)),
                    r=[zt], w=[sp["z"]], kind="st")
                lo, hi = max(c0, OWN0), min(c0 + W, OWN1)
                if lo < hi:
                    o0, w2 = lo - c0, hi - lo
                    op("vector",
                       [(lambda v, ck=ck, o0=o0, w2=w2: v.tensor_tensor_scan(
                           scA[:, ck * STW:ck * STW + w2],
                           dtile[:, ck * WA:ck * WA + w2],
                           zt[:, ck * STW + o0:ck * STW + o0 + w2],
                           stA[:, ck:ck + 1], OP.mult, OP.add))
                        for ck in range(CK)] +
                       [(lambda v, ck=ck, o0=o0, w2=w2: v.tensor_tensor_scan(
                           scB[:, ck * STW:ck * STW + w2],
                           dtile[:, ck * WA:ck * WA + w2],
                           wkt[:, ck * STW + o0:ck * STW + o0 + w2],
                           stB[:, ck:ck + 1], OP.mult, OP.add))
                        for ck in range(CK)],
                       r=[dtile, zt, wkt, stA, stB], w=[scA, scB])
                    op("vector",
                       [(lambda v, w2=w2: v.tensor_copy(
                           col3(stA), c3(scA, STW)[:, :, w2 - 1:w2])),
                        (lambda v, w2=w2: v.tensor_copy(
                            col3(stB), c3(scB, STW)[:, :, w2 - 1:w2]))],
                       r=[scA, scB], w=[stA, stB])
                    caps = []
                    for owncol, gcol in ((OWN1 - 1, 0), (OWN1 - 9, 16)):
                        if lo <= owncol < hi:
                            oc = owncol - lo
                            caps.append(lambda v, oc=oc, gcol=gcol: v.tensor_copy(
                                gbuf.t[:, gcol:gcol + 8].unsqueeze(2),
                                c3(scA, STW)[:, :, oc:oc + 1]))
                            caps.append(lambda v, oc=oc, gcol=gcol: v.tensor_copy(
                                gbuf.t[:, gcol + 8:gcol + 16].unsqueeze(2),
                                c3(scB, STW)[:, :, oc:oc + 1]))
                    if caps:
                        op("vector", caps, r=[scA, scB], w=[gbuf])
        _ph_a()
        pb.barrier()

        def _ph_exch():
         with ExitStack() as pe:
            psx = pst(pe, "psx", 512)
            rows = sbt(pe, "rows", (1, 6 * D))
            t8 = sbt(pe, "t8", (8, D))
            t8b = sbt(pe, "t8b", (8, D))
            e8 = sbt(pe, "e8", (8, D))
            gtA = sbt(pe, "gtA", (8, D))
            gtB = sbt(pe, "gtB", (8, D))
            op("sync", lambda s: s.dma_start(t8.t[:], td8_in), w=[t8], kind="ld")
            op("scalar", lambda s: s.activation(t8b.t[:], t8.t[:], AF.Exp),
               r=[t8], w=[t8b])
            op("vector", lambda v: v.tensor_scalar_mul(t8.t[:], t8b.t[:], -1.0),
               r=[t8b], w=[t8])
            for gcol, roff in ((0, 0), (8, 1), (16, 2), (24, 3)):
                for ck in range(CK):
                    op("tensor", lambda t, gcol=gcol, ck=ck: t.matmul(
                        psx[0:1, 0:P], gbuf[:, gcol + ck:gcol + ck + 1], idn.t[:],
                        start=True, stop=True), r=[gbuf, idn], w=[psx])
                    op("vector", lambda v, roff=roff, ck=ck: v.tensor_copy(
                        rows[0:1, roff * D + ck * P:roff * D + (ck + 1) * P],
                        psx[0:1, 0:P]), r=[psx], w=[rows])
            op("gpsimd", lambda s: s.dma_start(ci_r1.t, rows[0:1, 0:4 * D]),
               r=[rows], w=[ci_r1], kind="st")
            op("gpsimd", lambda g: g.collective_compute(
                "AllGather", OP.bypass, [RG], [ci_r1.t], [g_r1.t]),
                r=[ci_r1], w=[g_r1])
            op("sync", [lambda s: s.dma_start(gtA.t[:], g_r1.t[0::4, :]),
                        lambda s: s.dma_start(gtB.t[:], g_r1.t[1::4, :])],
               r=[g_r1], w=[gtA, gtB], kind="ld")
            op("scalar", lambda s: s.activation(e8.t[:], t8.t[:], AF.Exp,
                                                scale=met[0:8, 3:4]),
               r=[t8, met], w=[e8])
            op("vector", lambda v: v.tensor_scalar_mul(e8.t[:], e8.t[:],
                                                       met[0:8, 4:5]),
               r=[met], w=[e8])
            for tile, off in ((gtA, 4 * D), (gtB, 5 * D)):
                op("vector", lambda v, tile=tile: v.tensor_tensor(
                    tile.t[:], tile.t[:], e8.t[:], OP.mult), r=[e8], w=[tile])
                for half in range(2):
                    op("tensor", lambda t, tile=tile, half=half: t.matmul(
                        psx[0:1, 0:512], onc[0:8, :],
                        tile[:, half * 512:(half + 1) * 512],
                        start=True, stop=True), r=[onc, tile], w=[psx])
                    op("vector", lambda v, off=off, half=half: v.tensor_copy(
                        rows[0:1, off + half * 512:off + (half + 1) * 512],
                        psx[0:1, 0:512]), r=[psx], w=[rows])
            op("scalar", lambda s: s.activation(rows[0:1, 0:D], t8[0:1, :], AF.Exp,
                                                scale=float(n_own - 8)),
               r=[t8, met], w=[rows])
            op("vector", [
                lambda v: v.tensor_tensor(rows[0:1, 4 * D:5 * D],
                                          rows[0:1, 4 * D:5 * D],
                                          rows[0:1, 0:D], OP.mult),
                lambda v: v.tensor_tensor(rows[0:1, 5 * D:6 * D],
                                          rows[0:1, 5 * D:6 * D],
                                          rows[0:1, 0:D], OP.mult)],
                w=[rows])
            op("vector", [
                lambda v: v.tensor_tensor(rows[0:1, 4 * D:5 * D],
                                          rows[0:1, 4 * D:5 * D],
                                          rows[0:1, 2 * D:3 * D], OP.add),
                lambda v: v.tensor_tensor(rows[0:1, 5 * D:6 * D],
                                          rows[0:1, 5 * D:6 * D],
                                          rows[0:1, 3 * D:4 * D], OP.add)],
                w=[rows])
            op("gpsimd", lambda s: s.dma_start(ci_r2.t, rows[0:1, 4 * D:6 * D]),
               r=[rows], w=[ci_r2], kind="st")
            op("gpsimd", lambda g: g.collective_compute(
                "AllGather", OP.bypass, [RG], [ci_r2.t], [g_r2.t]),
                r=[ci_r2], w=[g_r2])
            op("sync", [lambda s: s.dma_start(gtA.t[:], g_r2.t[0::2, :]),
                        lambda s: s.dma_start(gtB.t[:], g_r2.t[1::2, :])],
               r=[g_r2], w=[gtA, gtB], kind="ld")
            for tile, off in ((gtA, 4 * D), (gtB, 5 * D)):
                op("vector", lambda v, tile=tile: v.tensor_scalar_mul(
                    tile.t[:], tile.t[:], met[0:8, 5:6]), r=[met], w=[tile])
                for half in range(2):
                    op("tensor", lambda t, tile=tile, half=half: t.matmul(
                        psx[0:1, 0:512], onc[0:8, :],
                        tile[:, half * 512:(half + 1) * 512],
                        start=True, stop=True), r=[onc, tile], w=[psx])
                    op("vector", lambda v, off=off, half=half: v.tensor_copy(
                        rows[0:1, off + half * 512:off + (half + 1) * 512],
                        psx[0:1, 0:512]), r=[psx], w=[rows])
            for off, colbuf in ((4 * D, iA), (5 * D, iB)):
                for ck in range(CK):
                    op("tensor", lambda t, off=off, ck=ck: t.matmul(
                        psx[:, 0:1], rows[0:1, off + ck * P:off + (ck + 1) * P],
                        onr[0:1, 0:1], start=True, stop=True),
                        r=[rows, onr], w=[psx])
                    op("vector", lambda v, colbuf=colbuf, ck=ck: v.tensor_copy(
                        colbuf.t[:, ck:ck + 1], psx[:, 0:1]), r=[psx], w=[colbuf])
        _ph_exch()
        pb.barrier()

        def _ph_s1():
         with ExitStack() as s1:
            woh = sbt(s1, "woh", (P, CK * D), BF16)
            wol = sbt(s1, "wol", (P, CK * D), BF16)
            wrh = sbt(s1, "wrh", (P, CK * D), BF16)
            wrl = sbt(s1, "wrl", (P, CK * D), BF16)
            tloads([(buf, cc * D, D, g_, 0, D, cc * P, (cc + 1) * P)
                    for buf, g_ in ((wrh, gsc["wrh"]), (wrl, gsc["wrl"]),
                                    (woh, gsc["woh"]), (wol, gsc["wol"]))
                    for cc in range(CK)],
                   [gsc["wrh"], gsc["wrl"], gsc["woh"], gsc["wol"]],
                   [wrh, wrl, woh, wol])
            pstat = pst(s1, "S_pst", 1024)
            dtile = sbt(s1, "S_dt", (P, CK * WA))
            build_dtile(dtile)
            wkt = sbt(s1, "B_wk", (P, CK * STW))
            zt = sbt(s1, "B_z", (P, CK * STW))
            n1t = sbt(s1, "B_n1", (P, CK * STW))
            xrh = sbt(s1, "B_xrh", (P, CK * STW), BF16)
            xrl = sbt(s1, "B_xrl", (P, CK * STW), BF16)
            scA = sbt(s1, "B_scA", (P, CK * STS))
            scB = sbt(s1, "B_scB", (P, CK * STS))
            numt = sbt(s1, "B_num", (P, CK * STW))
            dent = sbt(s1, "B_den", (P, CK * STW))
            srt = sbt(s1, "B_sr", (P, CK * STW))
            y1t = sbt(s1, "B_y1", (P, CK * STW))
            gt = sbt(s1, "B_g", (P, CK * STW))
            sq = sbt(s1, "B_sq", (P, CK * STW))
            stb = sbt(s1, "B_stb", (P, 5 * STW))
            gprev = sbt(s1, "B_gp", (P, CK))
            bh = sbt(s1, "B_bh", (P, CK * STW), BF16)
            bl = sbt(s1, "B_bl", (P, CK * STW), BF16)
            pmB = [pst(s1, "pmB0", 512), pst(s1, "pmB1", 512)]
            op("vector", lambda v: v.memset(gprev.t[:], 0.0), w=[gprev])

            for si, (c0, W) in enumerate(subs):
                op("sync", [(lambda s, b=b, spn=spn, c0=c0, W=W: s.dma_start(
                    schv(b, W), dchv(spn.t, c0, W)))
                    for b, spn in ((wkt, sp["wk"]), (zt, sp["z"]), (n1t, sp["n1"]),
                                   (xrh, spb["xrh"]), (xrl, spb["xrl"]))],
                    r=[sp["wk"], sp["z"], sp["n1"], spb["xrh"], spb["xrl"]],
                    w=[wkt, zt, n1t, xrh, xrl], kind="ld")
                if si == 0:
                    op("vector", [
                        lambda v: v.tensor_scalar_mul(
                            c3(zt, STW)[:, :, 0:HB], c3(zt, STW)[:, :, 0:HB],
                            met[:, 0:1]),
                        lambda v: v.tensor_scalar_mul(
                            c3(wkt, STW)[:, :, 0:HB], c3(wkt, STW)[:, :, 0:HB],
                            met[:, 0:1])], r=[met], w=[zt, wkt])
                    op("vector", [
                        lambda v: v.tensor_copy(c3(scA, STS)[:, :, 0:1], col3(iA)),
                        lambda v: v.tensor_copy(c3(scB, STS)[:, :, 0:1], col3(iB))],
                       r=[iA, iB], w=[scA, scB])
                op("vector",
                   [(lambda v, ck=ck, W=W: v.tensor_tensor_scan(
                       scA[:, ck * STS + 1:ck * STS + 1 + W],
                       dtile[:, ck * WA:ck * WA + W],
                       zt[:, ck * STW:ck * STW + W],
                       scA[:, ck * STS:ck * STS + 1], OP.mult, OP.add))
                    for ck in range(CK)] +
                   [(lambda v, ck=ck, W=W: v.tensor_tensor_scan(
                       scB[:, ck * STS + 1:ck * STS + 1 + W],
                       dtile[:, ck * WA:ck * WA + W],
                       wkt[:, ck * STW:ck * STW + W],
                       scB[:, ck * STS:ck * STS + 1], OP.mult, OP.add))
                    for ck in range(CK)],
                   r=[dtile, zt, wkt], w=[scA, scB])
                eub = eu_pm.t[:].unsqueeze(2).broadcast_to([P, CK, W])
                op("vector", lambda v, W=W, eub=eub: v.tensor_tensor(
                    c3(numt, STW)[:, :, 0:W], c3(zt, STW)[:, :, 0:W], eub,
                    OP.mult), r=[zt, eu_pm], w=[numt])
                op("vector", lambda v, W=W: v.tensor_tensor(
                    c3(numt, STW)[:, :, 0:W], c3(numt, STW)[:, :, 0:W],
                    c3(scA, STS)[:, :, 0:W], OP.add), r=[scA], w=[numt])
                op("vector", lambda v, W=W, eub=eub: v.tensor_tensor(
                    c3(dent, STW)[:, :, 0:W], c3(wkt, STW)[:, :, 0:W], eub,
                    OP.mult), r=[wkt, eu_pm], w=[dent])
                op("vector", lambda v, W=W: v.tensor_tensor(
                    c3(dent, STW)[:, :, 0:W], c3(dent, STW)[:, :, 0:W],
                    c3(scB, STS)[:, :, 0:W], OP.add), r=[scB], w=[dent])
                if si == 0:
                    op("vector", lambda v: v.tensor_scalar_add(
                        c3(dent, STW)[:, :, 0:HB], c3(dent, STW)[:, :, 0:HB],
                        met[:, 2:3]), r=[met], w=[dent])
                op("vector", lambda v, W=W: v.reciprocal(
                    c3(sq, STW)[:, :, 0:W], c3(dent, STW)[:, :, 0:W]),
                    r=[dent], w=[sq])
                op("vector", lambda v, W=W: v.tensor_tensor(
                    c3(numt, STW)[:, :, 0:W], c3(numt, STW)[:, :, 0:W],
                    c3(sq, STW)[:, :, 0:W], OP.mult), r=[sq], w=[numt])
                proj3(pmB, wrh, wrl, xrh, xrl, W, STW,
                      lambda dch, ps: op("scalar", lambda s, dch=dch, ps=ps, W=W:
                                         s.activation(srt[:, dch * STW:dch * STW + W],
                                                      ps[:, 0:W], AF.Sigmoid),
                                         r=[ps], w=[srt]))
                op("vector", lambda v, W=W: v.tensor_tensor(
                    c3(srt, STW)[:, :, 0:W], c3(srt, STW)[:, :, 0:W],
                    c3(numt, STW)[:, :, 0:W], OP.mult), r=[numt], w=[srt])
                split_hi_lo(srt, bh, bl, W, STW)
                proj3(pmB, woh, wol, bh, bl, W, STW,
                      lambda dch, ps: op("vector", lambda v, dch=dch, ps=ps, W=W:
                                         v.tensor_tensor(
                                             y1t[:, dch * STW:dch * STW + W],
                                             ps[:, 0:W],
                                             n1t[:, dch * STW:dch * STW + W], OP.add),
                                         r=[ps, n1t], w=[y1t]))
                op("gpsimd", lambda s, c0=c0, W=W: s.dma_start(
                    dchv(sp["y1"].t, c0, W), schv(y1t, W)),
                    r=[y1t], w=[sp["y1"]], kind="st")
                ln_na(y1t, gt, W, STW, sq, None, stb, pstat, mm_stats=True)
                if si == 0:
                    op("vector", lambda v: v.tensor_scalar_mul(
                        c3(gt, STW)[:, :, HB - 1:HB], c3(gt, STW)[:, :, HB - 1:HB],
                        met[:, 0:1]), r=[met], w=[gt])
                op("vector", [
                    (lambda v, W=W: v.tensor_tensor(
                        c3(sq, STW)[:, :, 1:W], c3(gt, STW)[:, :, 1:W],
                        c3(gt, STW)[:, :, 0:W - 1], OP.subtract)),
                    (lambda v: v.tensor_tensor(
                        c3(sq, STW)[:, :, 0:1], c3(gt, STW)[:, :, 0:1],
                        col3(gprev), OP.subtract))],
                   r=[gt, gprev], w=[sq])

                def mix2(tmoff, sph, spl):
                    tmb = prm[:, tmoff:tmoff + 8].unsqueeze(2).broadcast_to(
                        [P, CK, W])
                    op("vector", lambda v, tmb=tmb, W=W: v.tensor_tensor(
                        c3(numt, STW)[:, :, 0:W], c3(sq, STW)[:, :, 0:W], tmb,
                        OP.mult), r=[sq, prm], w=[numt])
                    op("vector", [
                        (lambda v, W=W: v.tensor_tensor(
                            c3(numt, STW)[:, :, 1:W], c3(numt, STW)[:, :, 1:W],
                            c3(gt, STW)[:, :, 0:W - 1], OP.add)),
                        (lambda v: v.tensor_tensor(
                            c3(numt, STW)[:, :, 0:1], c3(numt, STW)[:, :, 0:1],
                            col3(gprev), OP.add))],
                       r=[gt, gprev], w=[numt])
                    split_hi_lo(numt, bh, bl, W, STW)
                    op("gpsimd", lambda s, c0=c0, W=W, sph=sph: s.dma_start(
                        dchv(sph.t, c0, W), schv(bh, W)),
                        r=[bh], w=[sph], kind="st")
                    op("gpsimd", lambda s, c0=c0, W=W, spl=spl: s.dma_start(
                        dchv(spl.t, c0, W), schv(bl, W)),
                        r=[bl], w=[spl], kind="st")

                mix2(24, spb["gkh"], spb["gkl"])
                mix2(32, spb["grh"], spb["grl"])
                op("vector", lambda v, W=W: v.tensor_copy(
                    col3(gprev), c3(gt, STW)[:, :, W - 1:W]), r=[gt], w=[gprev])
                op("vector", [
                    (lambda v, W=W: v.tensor_copy(
                        c3(scA, STS)[:, :, 0:1], c3(scA, STS)[:, :, W:W + 1])),
                    (lambda v, W=W: v.tensor_copy(
                        c3(scB, STS)[:, :, 0:1], c3(scB, STS)[:, :, W:W + 1]))],
                   w=[scA, scB])
        _ph_s1()
        pb.barrier()

        def _ph_k():
         with ExitStack() as sk:
            gkh_r = sbt(sk, "K_gkh", (P, CK * RNG), BF16)
            gkl_r = sbt(sk, "K_gkl", (P, CK * RNG), BF16)
            op("sync", [lambda s: s.dma_start(schv(gkh_r, RNG),
                                              dchv(spb["gkh"].t, 0, RNG)),
                        lambda s: s.dma_start(schv(gkl_r, RNG),
                                              dchv(spb["gkl"].t, 0, RNG))],
               r=[spb["gkh"], spb["gkl"]], w=[gkh_r, gkl_r], kind="ld")
            HD2 = DF // 2
            fkh = sbt(sk, "K_fkh", (P, CK * HD2), BF16)
            fkl = sbt(sk, "K_fkl", (P, CK * HD2), BF16)
            tmp1 = [sbt(sk, "K_t1a", (P, 512)), sbt(sk, "K_t1b", (P, 512))]
            tmp2 = [sbt(sk, "K_t2a", (P, 512)), sbt(sk, "K_t2b", (P, 512))]
            kbh = [sbt(sk, "K_bha", (P, 512), BF16), sbt(sk, "K_bhb", (P, 512), BF16)]
            kbl = [sbt(sk, "K_bla", (P, 512), BF16), sbt(sk, "K_blb", (P, 512), BF16)]
            pmK = [pst(sk, "pmK0", 512), pst(sk, "pmK1", 512)]
            ps0 = pst(sk, "K_ps0", 512)
            wtq = [sbt(sk, "K_q%d" % i, (P, 2048)) for i in range(4)]
            whq = [sbt(sk, "K_qh%d" % i, (P, 2048), BF16) for i in range(2)]
            wlq = [sbt(sk, "K_ql%d" % i, (P, 2048), BF16) for i in range(2)]
            accq = sbt(sk, "K_acc", (P, 64))
            prow = sbt(sk, "K_prow", (1, 16))
            w2t_ = [(it, h) for it in range(D // P) for h in range(2)]

            def qload(tiles4):
                bufs = [wtq[j % 4] for j in range(len(tiles4))]
                fns = []
                for j, (w, it, h, c) in enumerate(tiles4):
                    fns.append(lambda s, w=w, it=it, h=h, c=c, b_=bufs[j]:
                               s.dma_start(b_[:, 0:c],
                                           wfull[w][it * P:(it + 1) * P,
                                                    h * 2048:h * 2048 + c]))
                op("sync", fns, w=bufs, kind="ld")
                return bufs

            def mk_sum(tiles4, cols):
                def emit():
                    bufs = qload(tiles4)
                    op("vector", [(lambda v, col=cols[j], b_=bufs[j],
                                   c=tiles4[j][3]: v.reduce_sum(
                                       accq[:, col:col + 1], b_[:, 0:c], axis=AX,
                                       apply_absolute_value=True))
                                  for j in range(len(tiles4))],
                       r=bufs, w=[accq])
                return emit

            def mk_swc():
                def emit():
                    op("vector", lambda v: v.tensor_tensor(
                        accq[:, 50:58], accq[:, 34:42], accq[:, 42:50], OP.add),
                       w=[accq])
                    op("vector", [
                        lambda v: v.reduce_sum(accq[:, 32:33], accq[:, 0:32],
                                               axis=AX),
                        lambda v: v.reduce_sum(accq[:, 33:34], accq[:, 50:58],
                                               axis=AX)], w=[accq])
                    op("tensor", [
                        lambda t: t.matmul(ps0[0:1, 0:1], accq[:, 32:33],
                                           onc.t[:], start=True, stop=True),
                        lambda t: t.matmul(ps0[0:1, 1:2], accq[:, 33:34],
                                           onc.t[:], start=True, stop=True)],
                        r=[accq, onc], w=[ps0])
                    op("vector", lambda v: v.tensor_copy(prow[0:1, 8:10],
                                                         ps0[0:1, 0:2]),
                       r=[ps0], w=[prow])
                    op("vector", lambda v: v.tensor_scalar(
                        prow[0:1, 10:12], prow[0:1, 8:10], 1.0 / (DF * D), 1e-5,
                        OP.mult, OP.max), w=[prow])
                    op("vector", lambda v: v.reciprocal(prow[0:1, 12:14],
                                                        prow[0:1, 10:12]),
                       w=[prow])
                    op("tensor", [(lambda t, i=i, off=off: t.matmul(
                        ps0[:, 8 + i:9 + i], onr.t[:, :], prow[0:1, off:off + 1],
                        start=True, stop=True))
                        for i, off in enumerate((12, 13, 10, 11))],
                        r=[onr, prow], w=[ps0])
                    op("vector", lambda v: v.tensor_copy(swc.t[:], ps0[:, 8:12]),
                       r=[ps0], w=[swc])
                return emit

            def mk_quant(tiles4, gout, col):
                def emit():
                    bufs = qload(tiles4)
                    for j, (w, it, h, c) in enumerate(tiles4):
                        wtx = bufs[j]
                        whb, wlb = whq[j % 2], wlq[j % 2]
                        op("vector", lambda v, c=c, wtx=wtx, wlb=wlb:
                           v.tensor_scalar(wlb[:, 0:c], wtx[:, 0:c],
                                           swc[:, col:col + 1], 0.5,
                                           OP.mult, OP.is_gt),
                           r=[wtx, swc], w=[wlb])
                        op("vector", lambda v, c=c, wtx=wtx, whb=whb:
                           v.tensor_scalar(whb[:, 0:c], wtx[:, 0:c],
                                           swc[:, col:col + 1], -0.5,
                                           OP.mult, OP.is_lt),
                           r=[wtx, swc], w=[whb])
                        op("vector", lambda v, c=c, whb=whb, wlb=wlb:
                           v.tensor_tensor(wlb[:, 0:c], wlb[:, 0:c],
                                           whb[:, 0:c], OP.subtract),
                           r=[whb], w=[wlb])
                        op("gpsimd", lambda g, it=it, h=h, c=c, wlb=wlb:
                           g.dma_start(gout.t[it * P:(it + 1) * P,
                                              h * 2048:h * 2048 + c],
                                       wlb[:, 0:c]),
                           r=[wlb], w=[gout], kind="st")
                return emit

            chunks = []
            for it0 in range(0, DF // P, 4):
                chunks.append(mk_sum([("w1", it0 + j, 0, D) for j in range(4)],
                                     [it0 + j for j in range(4)]))
            for g4 in range(4):
                tl = w2t_[g4 * 4:(g4 + 1) * 4]
                chunks.append(mk_sum([("w2", it, h, 2048) for it, h in tl],
                                     [34 + g4 * 4 + j for j in range(4)]))
            chunks.append(mk_swc())
            for it0 in range(0, DF // P, 4):
                chunks.append(mk_quant([("w1", it0 + j, 0, D) for j in range(4)],
                                       g_w1q, 0))
            for g4 in range(4):
                tl = w2t_[g4 * 4:(g4 + 1) * 4]
                chunks.append(mk_quant([("w2", it, h, 2048) for it, h in tl],
                                       g_w2q, 1))

    
            NQ = 4
            FLQ = (HD2 // P) // NQ
            fkhF = [Buf(fkh.t) for _ in range(NQ)]
            fklF = [Buf(fkl.t) for _ in range(NQ)]
            pi = 0
            for half in range(2):
                for q_ in range(NQ):
                    tloads([(buf, cc * HD2 + q_ * FLQ * P, FLQ * P, g_,
                             half * HD2 + q_ * FLQ * P,
                             half * HD2 + (q_ + 1) * FLQ * P,
                             cc * P, (cc + 1) * P)
                            for buf, g_ in ((fkh, gsc["fkh"]), (fkl, gsc["fkl"]))
                            for cc in range(CK)],
                           [gsc["fkh"], gsc["fkl"]], [fkhF[q_], fklF[q_]])
                for fl in range(HD2 // P):
                    f = half * (HD2 // P) + fl
                    for ti, (c0, W) in enumerate(tsubs(512, 0, RNG)):
                        bi = pi % 2
                        pi += 1
                        ps = pmK[bi]
                        fns = []
                        for kch in range(CK):
                            lh = fkh[:, kch * HD2 + fl * P:kch * HD2 + (fl + 1) * P]
                            ll = fkl[:, kch * HD2 + fl * P:kch * HD2 + (fl + 1) * P]
                            rh = gkh_r[:, kch * RNG + c0:kch * RNG + c0 + W]
                            rl = gkl_r[:, kch * RNG + c0:kch * RNG + c0 + W]
                            st_, sp_ = kch == 0, kch == CK - 1
                            fns.append(lambda t, lh=lh, rh=rh, st_=st_, ps=ps, W=W:
                                       t.matmul(ps[:, 0:W], lh, rh, start=st_,
                                                stop=False))
                            fns.append(lambda t, lh=lh, rl=rl, ps=ps, W=W:
                                       t.matmul(ps[:, 0:W], lh, rl, start=False,
                                                stop=False))
                            fns.append(lambda t, ll=ll, rh=rh, sp_=sp_, ps=ps, W=W:
                                       t.matmul(ps[:, 0:W], ll, rh, start=False,
                                                stop=sp_))
                        op("tensor", fns,
                           r=[fkhF[fl // FLQ], fklF[fl // FLQ], gkh_r, gkl_r],
                           w=[ps])
                        t1, t2 = tmp1[bi], tmp2[bi]
                        bh_, bl_ = kbh[bi], kbl[bi]
                        op("scalar", lambda s, ps=ps, t1=t1, W=W: s.activation(
                            t1[:, 0:W], ps[:, 0:W], AF.Relu), r=[ps], w=[t1])
                        op("scalar", lambda s, t1=t1, t2=t2, W=W: s.activation(
                            t2[:, 0:W], t1[:, 0:W], AF.Square), r=[t1], w=[t2])
                        op("scalar", lambda s, t2=t2, bh_=bh_, W=W: s.copy(
                            bh_[:, 0:W], t2[:, 0:W]), r=[t2], w=[bh_])
                        op("vector", lambda v, t2=t2, bh_=bh_, bl_=bl_, W=W:
                           v.tensor_tensor(bl_[:, 0:W], t2[:, 0:W], bh_[:, 0:W],
                                           OP.subtract), r=[t2, bh_], w=[bl_])
                        op("gpsimd", lambda s, f=f, c0=c0, bh_=bh_, W=W: s.dma_start(
                            sp_kfh.t[f * P:(f + 1) * P, c0:c0 + W], bh_[:, 0:W]),
                            r=[bh_], w=[sp_kfh], kind="st")
                        op("gpsimd", lambda s, f=f, c0=c0, bl_=bl_, W=W: s.dma_start(
                            sp_kfl.t[f * P:(f + 1) * P, c0:c0 + W], bl_[:, 0:W]),
                            r=[bl_], w=[sp_kfl], kind="st")
                    if chunks:
                        chunks.pop(0)()
            while chunks:
                chunks.pop(0)()
        _ph_k()
        pb.barrier()

        def _ph_v():
         with ExitStack() as sv:
            fvh = sbt(sv, "V_fvh", (P, FCK * D), BF16)
            fvl = sbt(sv, "V_fvl", (P, FCK * D), BF16)
            tloads([(buf, f * D, D, g_, 0, D, f * P, (f + 1) * P)
                    for buf, g_ in ((fvh, gsc["fvh"]), (fvl, gsc["fvl"]))
                    for f in range(FCK)],
                   [gsc["fvh"], gsc["fvl"]], [fvh, fvl])
            kfh_t = sbt(sv, "V_kfh", (P, FCK * 512), BF16)
            kfl_t = sbt(sv, "V_kfl", (P, FCK * 512), BF16)
            kfhR = [kfh_t, Buf(kfh_t.t)]
            kflR = [kfl_t, Buf(kfl_t.t)]
            ev = [sbt(sv, "V_ev0", (P, 512)), sbt(sv, "V_ev1", (P, 512))]
            pk = [pst(sv, "pkV0", 2048), pst(sv, "pkV1", 2048)]
            for si, (c0, W) in enumerate(tsubs(256, 0, RNG)):
                sl = si % 2
                o0 = sl * 256
                op("sync", [lambda s, c0=c0, W=W, o0=o0: s.dma_start(
                    kfh_t.t[:].rearrange("p (f t) -> p f t", f=FCK)[:, :, o0:o0 + W],
                    dchv(sp_kfh.t, c0, W, nck=FCK)),
                    lambda s, c0=c0, W=W, o0=o0: s.dma_start(
                    kfl_t.t[:].rearrange("p (f t) -> p f t", f=FCK)[:, :, o0:o0 + W],
                    dchv(sp_kfl.t, c0, W, nck=FCK))],
                   r=[sp_kfh, sp_kfl], w=[kfhR[sl], kflR[sl]], kind="ld")
                for grp in range(2):
                    pkg = pk[grp]
                    fns = []
                    for f in range(FCK):
                        for ci in range(4):
                            c = grp * 4 + ci
                            lh = fvh[:, f * D + c * P:f * D + (c + 1) * P]
                            ll = fvl[:, f * D + c * P:f * D + (c + 1) * P]
                            rh = kfh_t[:, f * 512 + o0:f * 512 + o0 + W]
                            rl = kfl_t[:, f * 512 + o0:f * 512 + o0 + W]
                            st_, sp_ = f == 0, f == FCK - 1
                            o = ci * 512
                            fns.append(lambda t, lh=lh, rh=rh, st_=st_, o=o,
                                       pkg=pkg, W=W:
                                       t.matmul(pkg[:, o:o + W], lh, rh,
                                                start=st_, stop=False))
                            fns.append(lambda t, lh=lh, rl=rl, o=o, pkg=pkg, W=W:
                                       t.matmul(pkg[:, o:o + W], lh, rl,
                                                start=False, stop=False))
                            fns.append(lambda t, ll=ll, rh=rh, sp_=sp_, o=o,
                                       pkg=pkg, W=W:
                                       t.matmul(pkg[:, o:o + W], ll, rh,
                                                start=False, stop=sp_))
                    op("tensor", fns, r=[fvh, fvl, kfhR[sl], kflR[sl]], w=[pkg])
                    for ci in range(4):
                        c = grp * 4 + ci
                        eb = ev[c % 2]
                        op("vector", lambda v, eb=eb, ci=ci, W=W, pkg=pkg:
                           v.tensor_copy(eb.t[:, 0:W], pkg[:, ci * 512:ci * 512 + W]),
                           r=[pkg], w=[eb])
                        op("gpsimd", lambda s, eb=eb, c=c, c0=c0, W=W: s.dma_start(
                            sp["kv"].t[c * P:(c + 1) * P, c0:c0 + W], eb.t[:, 0:W]),
                            r=[eb], w=[sp["kv"]], kind="st")
        _ph_v()
        pb.barrier()

        def _ph_frh():
         with ExitStack() as sf:
            n2q = sbt(sf, "n2q", (P, CK * RNG), BF16)
            r1row = sbt(sf, "r1row", (1, RNG))
            rrow = sbt(sf, "rrow", (1, 512))
            qcb = sbt(sf, "qcb", (P, 16))
            with ExitStack() as sfr:
                frh = sbt(sfr, "frh", (P, CK * D), BF16)
                frl = sbt(sfr, "frl", (P, CK * D), BF16)
                tloads([(buf, cc * D, D, g_, 0, D, cc * P, (cc + 1) * P)
                        for buf, g_ in ((frh, gsc["frh"]), (frl, gsc["frl"]))
                        for cc in range(CK)],
                       [gsc["frh"], gsc["frl"]], [frh, frl])
                pstat = pst(sfr, "R_pst", 1024)
                grh = sbt(sfr, "R_grh", (P, CK * STW), BF16)
                grl = sbt(sfr, "R_grl", (P, CK * STW), BF16)
                y1t = sbt(sfr, "R_y1", (P, CK * STW))
                kvt = sbt(sfr, "R_kv", (P, CK * STW))
                xt_ = sbt(sfr, "R_x", (P, CK * STW))
                sigt = sbt(sfr, "R_sig", (P, CK * STW))
                n2t = sbt(sfr, "R_n2", (P, CK * STW))
                sq = sbt(sfr, "R_sq", (P, CK * STW))
                stb = sbt(sfr, "R_stb", (P, 5 * STW))
                qi = sbt(sfr, "R_qi", (P, CK * STW), I32)
                q0 = sbt(sfr, "R_q0", (P, CK * STW))
                q1 = sbt(sfr, "R_q1", (P, CK * STW))
                pmF = [pst(sfr, "pmR0", 512), pst(sfr, "pmR1", 512)]
                for si, (c0, W) in enumerate(subs):
                    op("sync", [(lambda s, b=b, spn=spn, c0=c0, W=W: s.dma_start(
                        schv(b, W), dchv(spn.t, c0, W)))
                        for b, spn in ((y1t, sp["y1"]), (kvt, sp["kv"]),
                                       (grh, spb["grh"]), (grl, spb["grl"]))] +
                        [lambda s, c0=c0, W=W: s.dma_start(
                            schv(xt_, W), dchv(xT, c0 + 1, W))],
                        r=[sp["y1"], sp["kv"], spb["grh"], spb["grl"]],
                        w=[y1t, kvt, grh, grl, xt_], kind="ld")
                    proj3(pmF, frh, frl, grh, grl, W, STW,
                          lambda dch, ps: op("scalar",
                                             lambda s, dch=dch, ps=ps, W=W:
                                             s.activation(
                                                 sigt[:, dch * STW:dch * STW + W],
                                                 ps[:, 0:W], AF.Sigmoid),
                                             r=[ps], w=[sigt]))
                    op("vector", lambda v, W=W: v.tensor_tensor(
                        c3(kvt, STW)[:, :, 0:W], c3(sigt, STW)[:, :, 0:W],
                        c3(kvt, STW)[:, :, 0:W], OP.mult), r=[sigt], w=[kvt])
                    op("vector", lambda v, W=W: v.tensor_tensor(
                        c3(kvt, STW)[:, :, 0:W], c3(kvt, STW)[:, :, 0:W],
                        c3(y1t, STW)[:, :, 0:W], OP.add), r=[y1t], w=[kvt])
                    ln_na(kvt, n2t, W, STW, sq, None, stb, pstat,
                          mm_stats=True)
                    op("vector", lambda v, W=W: v.tensor_tensor(
                        c3(xt_, STW)[:, :, 0:W], c3(xt_, STW)[:, :, 0:W],
                        c3(n2t, STW)[:, :, 0:W], OP.add), r=[n2t], w=[xt_])
                    op("gpsimd", lambda s, c0=c0, W=W: s.dma_start(
                        dchv(sp["x1"].t, c0, W), schv(xt_, W)),
                        r=[xt_], w=[sp["x1"]], kind="st")
                    ln_na(xt_, n2t, W, STW, sq, None, stb, pstat,
                          mm_stats=True)
                    op("scalar", lambda s, W=W: s.activation(
                        c3(q1, STW)[:, :, 0:W], c3(n2t, STW)[:, :, 0:W], AF.Abs),
                        r=[n2t], w=[q1])
                    op("vector", lambda v, W=W: v.tensor_tensor(
                        c3(sq, STW)[:, 0:4, 0:W], c3(q1, STW)[:, 0:8:2, 0:W],
                        c3(q1, STW)[:, 1:8:2, 0:W], OP.max), r=[q1], w=[sq])
                    op("vector", lambda v, W=W: v.tensor_tensor(
                        c3(sq, STW)[:, 0:4:2, 0:W], c3(sq, STW)[:, 0:4:2, 0:W],
                        c3(sq, STW)[:, 1:4:2, 0:W], OP.max), w=[sq])
                    op("vector", lambda v, W=W: v.tensor_tensor(
                        sq[:, 0:W], sq[:, 0:W], sq[:, 2 * STW:2 * STW + W], OP.max),
                        w=[sq])
                    nK = (W + P - 1) // P
                    for k in range(nK):
                        wk_ = min(P, W - k * P)
                        op("tensor", lambda t, k=k, wk_=wk_: t.transpose(
                            pstat[0:wk_, 0:P], sq[:, k * P:k * P + wk_], idn.t[:]),
                            r=[sq, idn], w=[pstat])
                        op("vector", lambda v, k=k, wk_=wk_: v.reduce_max(
                            qcb.t[0:wk_, k:k + 1], pstat[0:wk_, 0:P], axis=AX),
                            r=[pstat], w=[qcb])
                    op("tensor", [(lambda t, k=k, wk_=min(P, W - k * P): t.matmul(
                        pstat[0:1, 512 + k * P:512 + k * P + wk_],
                        qcb[0:wk_, k:k + 1], idn[0:wk_, 0:wk_],
                        start=True, stop=True)) for k in range(nK)],
                        r=[qcb, idn], w=[pstat])
                    op("vector", lambda v, W=W: v.tensor_scalar_max(
                        rrow[0:1, 0:W], pstat[0:1, 512:512 + W], 1e-5),
                        r=[pstat], w=[rrow])
                    op("vector", lambda v, W=W: v.reciprocal(
                        rrow[0:1, 256:256 + W], rrow[0:1, 0:W]), w=[rrow])
                    op("vector", lambda v, c0=c0, W=W: v.tensor_scalar_mul(
                        r1row.t[0:1, c0:c0 + W], rrow[0:1, 256:256 + W], 127.0),
                        r=[rrow], w=[r1row])
                    op("tensor", lambda t, c0=c0, W=W: t.matmul(
                        pstat[:, 0:W], onr[0:1, 0:P], r1row[0:1, c0:c0 + W],
                        start=True, stop=True), r=[onr, r1row], w=[pstat])
                    op("vector", lambda v, W=W: v.tensor_tensor(
                        c3(q0, STW)[:, :, 0:W], c3(n2t, STW)[:, :, 0:W],
                        bc3(pstat[:, 0:W], CK, W), OP.mult),
                        r=[n2t, pstat], w=[q0])
                    op("vector", lambda v, W=W: v.tensor_copy(
                        c3(qi, STW)[:, :, 0:W], c3(q0, STW)[:, :, 0:W]),
                        r=[q0], w=[qi])
                    op("vector", lambda v, W=W: v.tensor_copy(
                        c3(q1, STW)[:, :, 0:W], c3(qi, STW)[:, :, 0:W]),
                        r=[qi], w=[q1])
                    op("vector", lambda v, W=W: v.tensor_tensor(
                        c3(sq, STW)[:, :, 0:W], c3(q0, STW)[:, :, 0:W],
                        c3(q1, STW)[:, :, 0:W], OP.subtract), r=[q0, q1], w=[sq])
                    op("vector", lambda v, W=W: v.scalar_tensor_tensor(
                        c3(q0, STW)[:, :, 0:W], c3(sq, STW)[:, :, 0:W], 0.5,
                        c3(q1, STW)[:, :, 0:W], OP.is_gt, OP.add),
                        r=[sq, q1], w=[q0])
                    op("vector", lambda v, W=W: v.tensor_scalar(
                        c3(q1, STW)[:, :, 0:W], c3(sq, STW)[:, :, 0:W],
                        -0.5, -1.0, OP.is_lt, OP.mult), r=[sq], w=[q1])
                    op("vector", lambda v, c0=c0, W=W: v.tensor_tensor(
                        c3(n2q, RNG)[:, :, c0:c0 + W], c3(q0, STW)[:, :, 0:W],
                        c3(q1, STW)[:, :, 0:W], OP.add), r=[q0, q1], w=[n2q])
            pb.barrier()

            with ExitStack() as sh:
                w1t = sbt(sh, "H_w1t", (P, CK * DF), BF16)
                tloads([(w1t, cc * DF, DF, g_w1q, 0, DF, cc * P, (cc + 1) * P)
                        for cc in range(CK)], [g_w1q], [w1t])
                pmH = [pst(sh, "pmH0", 512), pst(sh, "pmH1", 512)]
                rht = sbt(sh, "H_rh", (P, RNG))
                for c0_, W_ in tsubs(512, 0, RNG):
                    op("tensor", lambda t, c0_=c0_, W_=W_: t.matmul(
                        pmH[0][:, 0:W_], onr[0:1, 0:P], r1row[0:1, c0_:c0_ + W_],
                        start=True, stop=True), r=[onr, r1row], w=[pmH[0]])
                    op("vector", lambda v, c0_=c0_, W_=W_: v.reciprocal(
                        rht.t[:, c0_:c0_ + W_], pmH[0][:, 0:W_]),
                        r=[pmH[0]], w=[rht])
                op("vector", lambda v: v.tensor_scalar_mul(rht.t[:], rht.t[:],
                                                           swc[:, 2:3]),
                   r=[swc], w=[rht])
                hbuf = [sbt(sh, "H_h0", (P, RNG)), sbt(sh, "H_h1", (P, RNG))]
                cbuf = [sbt(sh, "H_c0", (P, NW)), sbt(sh, "H_c1", (P, NW))]
                hsb = [sbt(sh, "H_s0", (P, NW)), sbt(sh, "H_s1", (P, NW))]
                mrun = sbt(sh, "H_mr", (P, NW))
                op("vector", lambda v: v.memset(mrun.t[:], 0.0), w=[mrun])
                for f in range(FCK):
                    hb_, cb_, sb_ = hbuf[f % 2], cbuf[f % 2], hsb[f % 2]
                    for ti, (c0, W) in enumerate(tsubs(512, 0, RNG)):
                        ps = pmH[ti % 2]
                        op("tensor", [(lambda t, f=f, kch=kch, c0=c0, W=W, ps=ps:
                                       t.matmul(ps[:, 0:W],
                                                w1t[:, kch * DF + f * P:
                                                    kch * DF + (f + 1) * P],
                                                n2q[:, kch * RNG + c0:
                                                    kch * RNG + c0 + W],
                                                start=(kch == 0),
                                                stop=(kch == CK - 1)))
                                      for kch in range(CK)],
                           r=[w1t, n2q], w=[ps])
                        op("vector", lambda v, c0=c0, W=W, ps=ps, hb_=hb_:
                           v.tensor_tensor(hb_.t[:, c0:c0 + W], ps[:, 0:W],
                                           rht[:, c0:c0 + W], OP.mult),
                           r=[ps, rht], w=[hb_])
                    op("vector", [
                        lambda v, hb_=hb_: v.tensor_scalar_mul(
                            hb_.t[:, HB - 1:HB], hb_.t[:, HB - 1:HB], met[:, 0:1]),
                        lambda v, hb_=hb_: v.tensor_scalar_mul(
                            hb_.t[:, OWN1:OWN1 + 1], hb_.t[:, OWN1:OWN1 + 1],
                            met[:, 1:2])], r=[met], w=[hb_])
                    op("vector", lambda v, f=f, hb_=hb_, cb_=cb_: v.tensor_scalar(
                        cb_.t[:], hb_.t[:, HB:HB + NW],
                        prm[:, 56 + 3 * f + 1:56 + 3 * f + 2],
                        prm[:, 152 + f:152 + f + 1], OP.mult, OP.add),
                        r=[hb_, prm], w=[cb_])
                    op("vector", lambda v, f=f, hb_=hb_, cb_=cb_:
                       v.scalar_tensor_tensor(
                           cb_.t[:], hb_.t[:, HB - 1:HB - 1 + NW],
                           prm[:, 56 + 3 * f:56 + 3 * f + 1], cb_.t[:],
                           OP.mult, OP.add), r=[hb_, prm], w=[cb_])
                    op("vector", lambda v, f=f, hb_=hb_, cb_=cb_:
                       v.scalar_tensor_tensor(
                           cb_.t[:], hb_.t[:, HB + 1:HB + 1 + NW],
                           prm[:, 56 + 3 * f + 2:56 + 3 * f + 3], cb_.t[:],
                           OP.mult, OP.add), r=[hb_, prm], w=[cb_])
                    op("scalar", lambda s, cb_=cb_, sb_=sb_: s.activation(
                        sb_.t[:], cb_.t[:], AF.Silu), r=[cb_], w=[sb_])
                    op("scalar", lambda s, cb_=cb_, sb_=sb_: s.activation(
                        cb_.t[:], sb_.t[:], AF.Abs), r=[sb_], w=[cb_])
                    op("vector", lambda v, cb_=cb_: v.tensor_tensor(
                        mrun.t[:], mrun.t[:], cb_.t[:], OP.max), r=[cb_], w=[mrun])
                    op("gpsimd", lambda s, f=f, sb_=sb_: s.dma_start(
                        sp_hs.t[f * P:(f + 1) * P, :], sb_.t[:]),
                        r=[sb_], w=[sp_hs], kind="st")
                for k in range(NW // P):
                    pp = pmH[k % 2]
                    op("tensor", lambda t, k=k, pp=pp: t.transpose(
                        pp[0:P, 0:P], mrun[:, k * P:k * P + P], idn.t[:]),
                        r=[mrun, idn], w=[pp])
                    op("vector", lambda v, k=k, pp=pp: v.reduce_max(
                        fqc.t[:, k:k + 1], pp[0:P, 0:P], axis=AX), r=[pp], w=[fqc])
        _ph_frh()
        pb.barrier()

        def _ph_f():
         with ExitStack() as sF:
            w2t = sbt(sF, "F_w2t", (P, FCK * D), BF16)
            tloads([(w2t, f * D, D, g_w2q, 0, D, f * P, (f + 1) * P)
                    for f in range(FCK)], [g_w2q], [w2t])
            psL = pst(sF, "F_psL", 1024)
            psR = pst(sF, "F_psR", 512)
            pkF = pst(sF, "F_pkF", 2048)
            rrow = sbt(sF, "F_rr", (1, 512))
            rbc = [sbt(sF, "F_rb0", (P, 512)), sbt(sF, "F_rb1", (P, 512))]
            hst = sbt(sF, "F_hs", (P, FCK * STF))
            hq = sbt(sF, "F_hq", (P, FCK * STF), BF16)
            x1t = sbt(sF, "F_x1", (P, CK * STF))
            qa = sbt(sF, "F_qa", (P, CK * STF))
            qb = sbt(sF, "F_qb", (P, CK * STF))
            sq = sbt(sF, "F_sq", (P, CK * STF))
            qa2 = sbt(sF, "F_qa2", (P, CK * STF))
            qb2 = sbt(sF, "F_qb2", (P, CK * STF))
            stb = sbt(sF, "F_stb", (P, 5 * STF))
            tiles = tsubs(STF, OWN0, OWN1)

            def emit_loads(ti):
                c0, W = tiles[ti]
                ow = c0 - OWN0
                op("sync", lambda s, ow=ow, W=W: s.dma_start(
                    hst.t[:].rearrange("p (f t) -> p f t", f=FCK)[:, :, 0:W],
                    dchv(sp_hs.t, ow, W, nck=FCK)),
                    r=[sp_hs], w=[hst], kind="ld")

            def emit_x1load(ti):
                c0, W = tiles[ti]
                op("sync", lambda s, c0=c0, W=W: s.dma_start(
                    schv(x1t, W), dchv(sp["x1"].t, c0, W)),
                    r=[sp["x1"]], w=[x1t], kind="ld")

            def emit_rows(ti):
                pi = ti % 2
                W = tiles[ti][1]
                for j in range((W + P - 1) // P):
                    op("tensor", lambda t, j=j, ti=ti: t.matmul(
                        psR[0:1, 0:P], fqc[:, 2 * ti + j:2 * ti + j + 1],
                        idn.t[:], start=True, stop=True), r=[fqc, idn], w=[psR])
                    op("vector", lambda v, j=j: v.tensor_copy(
                        rrow[0:1, j * P:(j + 1) * P], psR[0:1, 0:P]),
                        r=[psR], w=[rrow])
                op("vector", lambda v, W=W: v.tensor_scalar_max(
                    rrow[0:1, 0:W], rrow[0:1, 0:W], 1e-5), w=[rrow])
                op("vector", lambda v, W=W: v.reciprocal(
                    rrow[0:1, 256:256 + W], rrow[0:1, 0:W]), w=[rrow])
                op("vector", lambda v, W=W: v.tensor_scalar_mul(
                    rrow[0:1, 0:W], rrow[0:1, 256:256 + W], 127.0), w=[rrow])
                op("vector", lambda v, W=W: v.reciprocal(
                    rrow[0:1, 256:256 + W], rrow[0:1, 0:W]), w=[rrow])
                op("vector", lambda v, W=W: v.tensor_scalar_mul(
                    rrow[0:1, 256:256 + W], rrow[0:1, 256:256 + W],
                    swc[0:1, 3:4]), r=[swc], w=[rrow])
                op("tensor", lambda t, W=W: t.matmul(
                    psR[:, 0:W], onr[0:1, 0:P], rrow[0:1, 0:W],
                    start=True, stop=True), r=[onr, rrow], w=[psR])
                op("vector", lambda v, pi=pi, W=W: v.tensor_copy(
                    rbc[pi].t[:, 0:W], psR[:, 0:W]), r=[psR], w=[rbc[pi]])
                op("tensor", lambda t, W=W: t.matmul(
                    psR[:, 256:256 + W], onr[0:1, 0:P], rrow[0:1, 256:256 + W],
                    start=True, stop=True), r=[onr, rrow], w=[psR])
                op("vector", lambda v, pi=pi, W=W: v.tensor_copy(
                    rbc[pi].t[:, 256:256 + W], psR[:, 256:256 + W]),
                    r=[psR], w=[rbc[pi]])

            def emit_quant(ti):
                pi = ti % 2
                W = tiles[ti][1]
                sqi3 = sq.t[:].bitcast(I32).rearrange("p (ck t) -> p ck t", t=STF)
                for grp in range(4):
                    go = grp * 2
                    op("vector", lambda v, grp=grp, W=W: v.tensor_tensor(
                        c3(qa, STF)[:, :, 0:W],
                        c3(hst, STF, n=8, o=grp * 8)[:, :, 0:W],
                        bc3(rbc[pi][:, 0:W], CK, W), OP.mult),
                        r=[hst, rbc[pi]], w=[qa])
                    op("vector", lambda v, W=W, sqi3=sqi3: v.tensor_copy(
                        sqi3[:, :, 0:W], c3(qa, STF)[:, :, 0:W]), r=[qa], w=[sq])
                    op("vector", lambda v, W=W, sqi3=sqi3: v.tensor_copy(
                        c3(qb, STF)[:, :, 0:W], sqi3[:, :, 0:W]), r=[sq], w=[qb])
                    op("vector", lambda v, W=W: v.tensor_tensor(
                        c3(sq, STF)[:, :, 0:W], c3(qa, STF)[:, :, 0:W],
                        c3(qb, STF)[:, :, 0:W], OP.subtract), r=[qa, qb], w=[sq])
                    op("vector", lambda v, W=W: v.scalar_tensor_tensor(
                        c3(qa, STF)[:, :, 0:W], c3(sq, STF)[:, :, 0:W], 0.5,
                        c3(qb, STF)[:, :, 0:W], OP.is_gt, OP.add),
                        r=[sq, qb], w=[qa])
                    op("vector", lambda v, W=W: v.tensor_scalar(
                        c3(qb, STF)[:, :, 0:W], c3(sq, STF)[:, :, 0:W],
                        -0.5, -1.0, OP.is_lt, OP.mult), r=[sq], w=[qb])
                    op("vector", lambda v, grp=grp, W=W: v.tensor_tensor(
                        c3(hq, STF, n=8, o=grp * 8)[:, :, 0:W],
                        c3(qa, STF)[:, :, 0:W], c3(qb, STF)[:, :, 0:W],
                        OP.add), r=[qa, qb], w=[hq])

            def emit_mm(ti):
                pi = ti % 2
                W = tiles[ti][1]
                for grp in range(2):
                    fns = []
                    for f in range(FCK):
                        for ci in range(4):
                            c = grp * 4 + ci
                            lh = w2t[:, f * D + c * P:f * D + (c + 1) * P]
                            rh = hq[:, f * STF:f * STF + W]
                            fns.append(lambda t, lh=lh, rh=rh, st_=(f == 0),
                                       sp_=(f == FCK - 1), o=ci * 512, W=W:
                                       t.matmul(pkF[:, o:o + W], lh, rh,
                                                start=st_, stop=sp_))
                    op("tensor", fns, r=[w2t, hq], w=[pkF])
                    op("vector", lambda v, grp=grp, pi=pi, W=W: v.tensor_tensor(
                        c3(qa2, STF, n=4, o=grp * 4)[:, :, 0:W],
                        pkF.t[:].rearrange("p (ck t) -> p ck t", t=512)[:, :, 0:W],
                        bc3(rbc[pi][:, 256:256 + W], 4, W), OP.mult),
                       r=[pkF, rbc[pi]], w=[qa2])

            def emit_tail(ti):
                c0, W = tiles[ti]
                ow = c0 - OWN0
                ln_na(qa2, qb2, W, STF, qb2, None, stb, psL, mm_stats=True)
                emit_x1load(ti)
                op("vector", lambda v, W=W: v.tensor_tensor(
                    c3(qb2, STF)[:, :, 0:W], c3(qb2, STF)[:, :, 0:W],
                    c3(x1t, STF)[:, :, 0:W], OP.add), r=[x1t], w=[qb2])
                op("gpsimd", lambda s, ow=ow, W=W: s.dma_start(
                    dchv(outT.t, ow, W), schv(qb2, W)),
                    r=[qb2], w=[outT], kind="st")

            NT = len(tiles)
            emit_loads(0)
            emit_rows(0)
            emit_quant(0)
            for ti in range(NT):
                emit_mm(ti)
                if ti + 1 < NT:
                    emit_loads(ti + 1)
                    emit_rows(ti + 1)
                    emit_quant(ti + 1)
                emit_tail(ti)
        _ph_f()

        pb.fence("sync", "st")
        pb.fence("sync", "ld")
        pb.recs["sync"].append(["inst", lambda s: s.nop(), None, None, False, False])

        E = dict(sync=nc.sync, scalar=nc.scalar, vector=nc.vector,
                 tensor=nc.tensor, gpsimd=nc.gpsimd)
        blk = ctx.enter_context(nc.Block())
        pb.run(blk, E, sem_h)
    return nc


def host_inputs(inputs, n_own):
    x = np.asarray(inputs["L0_attn_x"], dtype=np.float32)
    B, T, _ = x.shape
    RNG = HB + n_own + HF
    NX = RNG + 1

    def col(vec):
        return np.ascontiguousarray(np.asarray(vec, np.float32).reshape(CK, P).T)

    pcols = np.zeros((P, 184), np.float32)
    pcols[:, 0:8] = col(inputs["L0_attn_tm_k"])
    pcols[:, 8:16] = col(inputs["L0_attn_tm_v"])
    pcols[:, 16:24] = col(inputs["L0_attn_tm_r"])
    pcols[:, 24:32] = col(inputs["L0_attn_fm_k"])
    pcols[:, 32:40] = col(inputs["L0_attn_fm_r"])
    pcols[:, 40:48] = col(inputs["L0_attn_time_decay"])
    pcols[:, 48:56] = col(inputs["L0_attn_time_first"])
    cw = np.asarray(inputs["L0_ffn_conv_w"], np.float32)[:, 0, :]
    pcols[:, 56:152] = cw.reshape(FCK, P, 3).transpose(1, 0, 2).reshape(P, 96)
    pcols[:, 152:184] = np.asarray(inputs["L0_ffn_conv_b"], np.float32).reshape(FCK, P).T

    import ml_dtypes
    ident = np.eye(P, dtype=np.float32)
    identb = np.eye(P, dtype=ml_dtypes.bfloat16)
    td = np.asarray(inputs["L0_attn_time_decay"], np.float32)
    td8 = np.ascontiguousarray(np.tile(td[None, :], (8, 1)))
    Wm = {nm: np.ascontiguousarray(np.asarray(inputs[key], np.float32))
          for nm, key in (("wk", "L0_attn_Wk"), ("wv", "L0_attn_Wv"),
                          ("wr", "L0_attn_Wr"), ("wo", "L0_attn_Wo"),
                          ("fr", "L0_attn_Fr"), ("fk", "L0_attn_Fk"),
                          ("fv", "L0_attn_Fv"), ("w1", "L0_ffn_w1"),
                          ("w2", "L0_ffn_w2"))}

    in_maps = []
    for c in range(NCORE):
        b, q = divmod(c, 4)
        t0 = q * n_own
        lo = t0 - 9
        xs = np.zeros((NX, D), np.float32)
        s0, s1_ = max(lo, 0), min(lo + NX, T)
        xs[s0 - lo:s1_ - lo] = x[b, s0:s1_]
        meta = np.zeros((P, 8), np.float32)
        meta[:, 0] = 0.0 if q == 0 else 1.0
        meta[:, 1] = 0.0 if q == 3 else 1.0
        meta[:, 2] = 1.0 - meta[0, 0]
        meta[:, 6] = EPS
        meta[:, 7] = float(n_own - 8)
        for j in range(8):
            jb = j // 4
            if jb == b and j < c and (c - j) <= 3:
                meta[j, 3] = float(n_own * (c - 1 - j))
                meta[j, 4] = 1.0
            if jb == b and j == c - 1:
                meta[j, 5] = 1.0
        m = dict(xT=np.ascontiguousarray(xs.T), params_pm=pcols, meta=meta,
                 ident=ident, identb=identb,
                 onescol=np.ones((P, 1), np.float32),
                 onesrow=np.ones((1, P), np.float32),
                 ones2d=np.ones((P, P), np.float32), td8=td8)
        for nm, arr in Wm.items():
            m["w_" + nm] = arr
        in_maps.append(m)
    return in_maps


def kernel(**inputs):
    x = np.asarray(inputs["L0_attn_x"])
    B, T, _ = x.shape
    n_own = B * T // NCORE
    nc = bass.Bass("TRN2", target_bir_lowering=False)
    build(nc, n_own)
    in_maps = host_inputs(inputs, n_own)
    res = run_bass_kernel_spmd(nc, in_maps, core_ids=list(range(NCORE)))
    out = np.empty((B, T, D), np.float32)
    for c in range(NCORE):
        b, q = divmod(c, 4)
        out[b, q * n_own:(q + 1) * n_own, :] = np.asarray(res.results[c]["outT"]).T
    return out
